# version 10
# baseline (speedup 1.0000x reference)
import math
import numpy as np
import ml_dtypes
import concourse.bass as bass
import concourse.mybir as mybir
from concourse.bass_utils import run_bass_kernel_spmd

F32 = mybir.dt.float32
BF16 = mybir.dt.bfloat16
AF = mybir.ActivationFunctionType
ALU = mybir.AluOpType
AX = mybir.AxisListType
bfnp = ml_dtypes.bfloat16

N_CORES = 8
B_PER_CORE = 2
T = 2048
D = 1024
TILE = 128
NT = T // TILE

H0, DK, DV = 8, 96, 192
KD, VD = H0 * DK, H0 * DV
IFF = 4096
H1, DH1 = 16, 64

ENGINES = ("pe", "dve", "act", "pool", "sp")
HOIST_ON = {"aQK", "aV", "aBA", "a2G", "a2Wo", "a3GU", "a3Wd",
            "bV", "bQK", "bWo", "bGl", "bS", "bAV", "bMoE",
            "cGl", "cQK", "cV", "dMoE"}
ENG_ATTR = {"pe": "tensor", "dve": "vector", "act": "scalar",
            "pool": "gpsimd", "sp": "sync"}


class Prog:

    def __init__(self, nc):
        self.nc = nc
        self.ops = {e: [] for e in ENGINES}
        self.val = {e: 0 for e in ENGINES}
        self.waited = {e: {d: 0 for d in ENGINES} for e in ENGINES}
        self.pending = {e: {} for e in ENGINES}
        self.bufs = {}
        self.n_ops = 0
        self.hoist_run = {e: None for e in ENGINES}

    def barrier(self):
        snap = dict(self.val)
        for e in ENGINES:
            for d in ENGINES:
                if d == e and d not in ("sp", "pool"):
                    continue
                if snap[d] > self.waited[e][d]:
                    self.pending[e][d] = max(self.pending[e].get(d, 0), snap[d])

    def _buf(self, key):
        return self.bufs.setdefault(key, {"w": None, "r": {}})

    def op(self, eng, fn, reads=(), writes=(), dma=False, hoist=None):
        assert eng in ENGINES, eng
        waits = {}
        if self.pending[eng]:
            for d, v in self.pending[eng].items():
                if v > self.waited[eng][d]:
                    waits[d] = max(waits.get(d, 0), v)
            self.pending[eng] = {}

        def need_wait(d, v, d_dma):
            if d == eng and eng == "pe" and not d_dma and not dma:
                return False
            return self.waited[eng][d] < v

        for key in tuple(reads) + tuple(writes):
            b = self._buf(key)
            if b["w"] is not None:
                d, v, d_dma = b["w"]
                if need_wait(d, v, d_dma):
                    waits[d] = max(waits.get(d, 0), v)
        for key in writes:
            b = self._buf(key)
            for d, (v, d_dma) in b["r"].items():
                if need_wait(d, v, d_dma):
                    waits[d] = max(waits.get(d, 0), v)
        for d in list(waits):
            if d in ("sp", "pool") and waits[d] < self.val[d]:
                waits[d] = self.val[d]
        for d, v in waits.items():
            self.waited[eng][d] = v
        inc = 16 if dma else 1
        self.val[eng] += inc
        v_after = self.val[eng]
        if hoist is not None and hoist[0] not in HOIST_ON:
            hoist = None
        run = self.hoist_run[eng]
        if hoist is not None and run is not None and run[0] == hoist:
            first_idx, snap = run[1], run[2]
            first_waits = self.ops[eng][first_idx][0]
            kept = {}
            for d, v in waits.items():
                if d != eng and v <= snap.get(d, 0):
                    first_waits[d] = max(first_waits.get(d, 0), v)
                else:
                    kept[d] = v
            waits = kept
        elif hoist is not None:
            self.hoist_run[eng] = (hoist, len(self.ops[eng]), dict(self.val))
        else:
            self.hoist_run[eng] = None
        self.ops[eng].append((dict(waits), fn, inc))
        for key in reads:
            self._buf(key)["r"][eng] = (v_after, dma)
        for key in writes:
            b = self._buf(key)
            b["w"] = (eng, v_after, dma)
            b["r"] = {}
        self.n_ops += 1
        return v_after

    def alias(self, newkey, oldkeys):
        b = self._buf(newkey)
        for ok in oldkeys:
            if ok in self.bufs:
                ob = self.bufs[ok]
                if ob["w"] is not None:
                    if b["w"] is None or ob["w"][1] > b["w"][1]:
                        pass
                for d, rv in ob["r"].items():
                    if d not in b["r"] or rv[0] > b["r"][d][0]:
                        b["r"][d] = rv
                if ob["w"] is not None:
                    d, v, dd = ob["w"]
                    if d not in b["r"] or v > b["r"][d][0]:
                        b["r"][d] = (v, dd)

    def emit(self, blk, sems):
        for e in ENGINES:
            ops = self.ops[e]
            if not ops:
                continue

            def make(eng_ops, eng_name):
                def body(engine):
                    for waits, fn, inc in eng_ops:
                        for d, v in waits.items():
                            engine.wait_ge(sems[d], v)
                        ins = fn(engine)
                        ins.then_inc(sems[eng_name], inc)
                return body

            getattr(blk, ENG_ATTR[e])(make(ops, e))


def host_consts():
    c = {}
    c["c_ident_bf"] = np.eye(TILE, dtype=bfnp)
    c["c_ident_f"] = np.eye(TILE, dtype=np.float32)
    jj, ii = np.meshgrid(np.arange(TILE), np.arange(TILE), indexing="ij")
    c["c_mask_ui"] = np.where(jj <= ii, 0.0, -30000.0).astype(np.float32)
    c["c_mask_us"] = np.where(jj < ii, 0.0, -30000.0).astype(np.float32)
    c["c_mask_li"] = np.where(jj >= ii, 0.0, -30000.0).astype(np.float32)
    c["c_mask_ls"] = np.where(jj > ii, 0.0, -30000.0).astype(np.float32)
    c["c_nmask_ls"] = np.where(jj > ii, 0.0, 30000.0).astype(np.float32)
    c["c_triu_f"] = np.triu(np.ones((TILE, TILE), np.float32))
    bas = np.zeros((8, 8 * 128), np.float32)
    for h in range(8):
        bas[h, h * 128:(h + 1) * 128] = 1.0
    c["c_basis8"] = bas
    kk = np.arange(TILE)[:, None]
    qq = np.arange(512)[None, :]
    for r in range(4):
        c[f"c_causal{r}"] = np.where(TILE * r + kk <= qq, 0.0,
                                     -30000.0).astype(np.float32)
    return c


CONSTS = host_consts()


class Ctx:

    def __init__(self, nc, p, es):
        self.nc = nc
        self.p = p
        self.es = es
        self.uid = 0
        self._const_handles = {}
        self._bank_keys = [("psbank", i) for i in range(8)]
        self.psum = []
        self.sems = {}

    def name(self, base):
        self.uid += 1
        return f"{base}_{self.uid}"

    def sb(self, base, shape, dtype):
        es = self._stack[-1] if getattr(self, "_stack", None) else self.es
        if not hasattr(self, "ledger"):
            self.ledger = []
        nbytes = int(np.prod(shape[1:])) * (2 if dtype == BF16 else 4)
        self.ledger.append((base, tuple(shape), nbytes,
                            len(getattr(self, "_stack", []) or [])))
        try:
            return es.enter_context(
                self.nc.sbuf_tensor(self.name(base), list(shape), dtype))
        except Exception:
            live = {}
            for nm, shp, nb, lvl in self.ledger:
                live[nm] = (shp, nb, lvl)
            print("LEDGER (all ever allocated):")
            for nm, shp, nb, lvl in self.ledger:
                print(f"  {nm} {shp} {nb}B lvl{lvl}")
            raise

    def push(self):
        from contextlib import ExitStack
        if not hasattr(self, "_stack"):
            self._stack = []
        self._stack.append(ExitStack())

    def pop(self):
        if hasattr(self, "ledger"):
            self.ledger.append(("__POP__", (), 0, len(self._stack)))
        self._stack.pop().close()

    def init_psum(self):
        self.psum = [self.es.enter_context(
            self.nc.psum_tensor(self.name("ps"), [128, 512], F32))
            for _ in range(6)]
        self.psb = [self.es.enter_context(
            self.nc.psum_tensor(self.name("psb"), [128, 1024], BF16))
            for _ in range(2)]

    def init_sems(self):
        for e in ENGINES:
            self.sems[e] = self.es.enter_context(
                self.nc.semaphore(self.name("sem_" + e)))

    def const_in(self, key):
        if key not in self._const_handles:
            v = CONSTS[key]
            dt = BF16 if v.dtype == bfnp else F32
            self._const_handles[key] = self.nc.dram_tensor(
                key, list(v.shape), dt, kind="ExternalInput").ap()
        return self._const_handles[key]

    def const_inputs_used(self):
        return {k: CONSTS[k] for k in self._const_handles}


def setup_common(ctx):
    p, nc = ctx.p, ctx.nc
    t = {}
    t["ident_bf"] = ctx.sb("ident_bf", (128, 128), BF16)
    t["ident_f"] = ctx.sb("ident_f", (128, 128), F32)
    t["mask_ui"] = ctx.sb("mask_ui", (128, 128), F32)
    t["mask_us"] = ctx.sb("mask_us", (128, 128), F32)
    t["nmask_ls"] = ctx.sb("nmask_ls", (128, 128), F32)
    t["triu_f"] = ctx.sb("triu_f", (128, 128), F32)
    for k, sb in [("c_ident_bf", t["ident_bf"]), ("c_ident_f", t["ident_f"]),
                  ("c_mask_ui", t["mask_ui"]), ("c_mask_us", t["mask_us"]),
                  ("c_nmask_ls", t["nmask_ls"]), ("c_triu_f", t["triu_f"])]:
        hh = ctx.const_in(k)
        p.op("sp", lambda e, sb=sb, hh=hh: e.dma_start(out=sb[:], in_=hh),
             writes=[("const", k)], dma=True)
    for k in ("ident_bf", "ident_f", "mask_ui", "mask_us", "nmask_ls", "triu_f"):
        t[k] = t[k][:]
    t["ones_r_bf"] = ctx.sb("ones_r_bf", (1, 128), BF16)
    t["ones_r_f"] = ctx.sb("ones_r_f", (1, 128), F32)
    t["ones_c_bf"] = ctx.sb("ones_c_bf", (128, 1), BF16)
    t["ones_c_f"] = ctx.sb("ones_c_f", (128, 1), F32)
    t["consts"] = ctx.sb("constcols", (128, 8), F32)
    p.op("dve", lambda e: e.memset(t["ones_r_bf"][:], 1.0), writes=[("const", "o1")])
    p.op("dve", lambda e: e.memset(t["ones_r_f"][:], 1.0), writes=[("const", "o2")])
    p.op("dve", lambda e: e.memset(t["ones_c_bf"][:], 1.0), writes=[("const", "o3")])
    p.op("dve", lambda e: e.memset(t["ones_c_f"][:], 1.0), writes=[("const", "o4")])
    p.op("dve", lambda e: e.memset(t["consts"][:, 0:1], 1e-6), writes=[("const", "c0")])
    p.op("dve", lambda e: e.memset(t["consts"][:, 1:2], 1e-5), writes=[("const", "c1")])
    p.op("dve", lambda e: e.memset(t["consts"][:, 2:3], 1.0), writes=[("const", "c2")])
    p.op("dve", lambda e: e.memset(t["consts"][:, 3:4], 0.0), writes=[("const", "c3")])
    p.op("dve", lambda e: e.memset(t["consts"][:, 4:5], -1.0), writes=[("const", "c4")])
    for k in ("ones_r_bf", "ones_r_f", "ones_c_bf", "ones_c_f", "consts"):
        t[k] = t[k][:]
    t["eps6"] = t["consts"][:, 0:1]
    t["eps5"] = t["consts"][:, 1:2]
    t["one"] = t["consts"][:, 2:3]
    t["zero"] = t["consts"][:, 3:4]
    t["neg1"] = t["consts"][:, 4:5]
    return t


def rck(*a):
    return tuple(a)



def emit_A0(ctx, t, io):
    p = ctx.p
    nc = ctx.nc
    xt = ctx.sb("a0_x", (128, 1024), F32)
    xn = ctx.sb("a0_xn", (128, 1024), F32)
    sq = ctx.sb("a0_sq", (128, 1024), F32)
    xnb = ctx.sb("a0_xnb", (128, 1024), BF16)
    hb = ctx.sb("a0_hb", (128, 1024), BF16)
    st = ctx.sb("a0_st", (128, 16), F32)
    w1rep = ctx.sb("a0_w1rep", (128, 1024), F32)
    w1row = ctx.sb("a0_w1row", (1, 1024), F32)
    sxn = ctx.sb("a0_sxn", (128, 8 * 512), BF16)
    sh = ctx.sb("a0_sh", (128, 8 * 512), BF16)
    PB, PT = ctx.psum, ctx.psb

    p.op("sp", lambda e: e.dma_start(out=w1row[:], in_=io["L0_attn_norm1_w"][None, :]),
         writes=[("w1row",)], dma=True)
    for seg in range(2):
        p.op("pe", lambda e, seg=seg: e.matmul(
            PB[0][:, 0:512], t["ones_r_f"][0:1, 0:128],
            w1row[0:1, seg * 512:(seg + 1) * 512], start=True, stop=True),
            reads=[("w1row",), ("const", "o2")], writes=[("pb", 0)])
        p.op("dve", lambda e, seg=seg: e.tensor_copy(
            w1rep[:, seg * 512:(seg + 1) * 512], PB[0][:, 0:512]),
            reads=[("pb", 0)], writes=[("w1rep", seg)])

    for b in range(B_PER_CORE):
        for grp in range(4):
            for t4 in range(4):
                tt = grp * 4 + t4
                r0 = tt * 128
                p.op("sp", lambda e, b=b, r0=r0: e.dma_start(
                    out=xt[:], in_=io["x"][b, r0:r0 + 128, :]),
                    writes=[("a0x",)], dma=True)
                p.op("dve", lambda e: e.reduce_sum(st[:, 0:1], xt[:], AX.X),
                     reads=[("a0x",)], writes=[("st", 0)])
                p.op("act", lambda e: e.activation(sq[:], xt[:], AF.Square,
                                                   accum_out=st[:, 1:2]),
                     reads=[("a0x",)], writes=[("st", 1), ("a0sq",)])
                p.op("dve", lambda e: e.tensor_scalar(
                    st[:, 2:3], st[:, 0:1], -1.0 / 1024, None, ALU.mult),
                    reads=[("st", 0)], writes=[("st", 2)])
                p.op("dve", lambda e: e.tensor_scalar(
                    st[:, 3:4], st[:, 1:2], 1.0 / 1024, None, ALU.mult),
                    reads=[("st", 1)], writes=[("st", 3)])
                p.op("dve", lambda e: e.scalar_tensor_tensor(
                    st[:, 4:5], st[:, 2:3], 1.0, st[:, 2:3], ALU.mult, ALU.mult),
                    reads=[("st", 2)], writes=[("st", 4)])
                p.op("dve", lambda e: e.tensor_sub(st[:, 5:6], st[:, 3:4], st[:, 4:5]),
                     reads=[("st", 3), ("st", 4)], writes=[("st", 5)])
                p.op("act", lambda e: e.activation(st[:, 6:7], st[:, 5:6], AF.Ln,
                                                   bias=t["eps5"], scale=1.0),
                     reads=[("st", 5), ("const", "c1")], writes=[("st", 6)])
                p.op("act", lambda e: e.activation(st[:, 7:8], st[:, 6:7], AF.Exp,
                                                   scale=-0.5),
                     reads=[("st", 6)], writes=[("st", 7)])
                p.op("dve", lambda e: e.tensor_scalar(
                    xn[:], xt[:], st[:, 2:3], st[:, 7:8], ALU.add, ALU.mult),
                    reads=[("a0x",), ("st", 2), ("st", 7)], writes=[("a0xn",)])
                p.op("sp", lambda e, b=b, r0=r0: e.dma_start(
                    out=io["xn0_tm"][b, r0:r0 + 128, :], in_=xn[:]),
                    reads=[("a0xn",)], writes=[("d_xn0_tm", b, tt)], dma=True)
                p.op("dve", lambda e: e.tensor_copy(xnb[:], xn[:]),
                     reads=[("a0xn",)], writes=[("a0xnb",)])
                p.op("act", lambda e: e.activation(sq[:], xn[:], AF.Square,
                                                   accum_out=st[:, 8:9]),
                     reads=[("a0xn",)], writes=[("st", 8), ("a0sq",)])
                p.op("act", lambda e: e.activation(st[:, 9:10], st[:, 8:9], AF.Ln,
                                                   bias=t["eps6"], scale=1.0 / 1024),
                     reads=[("st", 8), ("const", "c0")], writes=[("st", 9)])
                p.op("act", lambda e: e.activation(st[:, 10:11], st[:, 9:10], AF.Exp,
                                                   scale=-0.5),
                     reads=[("st", 9)], writes=[("st", 10)])
                p.op("dve", lambda e: e.scalar_tensor_tensor(
                    hb[:], xn[:], st[:, 10:11], w1rep[:], ALU.mult, ALU.mult),
                    reads=[("a0xn",), ("st", 10), ("w1rep", 0), ("w1rep", 1)],
                    writes=[("a0hb",)])
                for j in range(8):
                    c0 = j * 512 + t4 * 128
                    p.op("pe", lambda e, j=j: e.transpose(
                        PT[0][:, 0:128], xnb[:, j * 128:(j + 1) * 128], t["ident_bf"]),
                        reads=[("a0xnb",), ("const", "c_ident_bf")],
                        writes=[("pt", 0)])
                    p.op("dve", lambda e, c0=c0: e.tensor_copy(
                        sxn[:, c0:c0 + 128], PT[0][:, 0:128]),
                        reads=[("pt", 0)], writes=[("sxn", j, t4)])
                    p.op("pe", lambda e, j=j: e.transpose(
                        PT[1][:, 0:128], hb[:, j * 128:(j + 1) * 128], t["ident_bf"]),
                        reads=[("a0hb",), ("const", "c_ident_bf")],
                        writes=[("pt", 1)])
                    p.op("act", lambda e, c0=c0: e.activation(
                        sh[:, c0:c0 + 128], PT[1][:, 0:128], AF.Copy),
                        reads=[("pt", 1)], writes=[("sh", j, t4)])
            for j in range(8):
                cc = grp * 512
                p.op("sp", lambda e, b=b, j=j, cc=cc: e.dma_start(
                    out=io["xn0_fm"][b, j * 128:(j + 1) * 128, cc:cc + 512],
                    in_=sxn[:, j * 512:(j + 1) * 512]),
                    reads=[("sxn", j, 0), ("sxn", j, 1), ("sxn", j, 2), ("sxn", j, 3)],
                    writes=[("d_xn0_fm", b, j, grp)], dma=True)
                p.op("sp", lambda e, b=b, j=j, cc=cc: e.dma_start(
                    out=io["h_fm"][b, j * 128:(j + 1) * 128, cc:cc + 512],
                    in_=sh[:, j * 512:(j + 1) * 512]),
                    reads=[("sh", j, 0), ("sh", j, 1), ("sh", j, 2), ("sh", j, 3)],
                    writes=[("d_h_fm", b, j, grp)], dma=True)
    p.barrier()


def emit_A1(ctx, t, io):
    p = ctx.p
    PB, PT = ctx.psum, ctx.psb
    W = 1024
    NC_W = W // 128

    wq = ctx.sb("a1_wq", (128, 8 * KD), BF16)
    wk = ctx.sb("a1_wk", (128, 8 * KD), BF16)
    wv = ctx.sb("a1_wv", (128, 8 * VD), BF16)
    wb = ctx.sb("a1_wb", (128, 8 * 8), BF16)
    wa = ctx.sb("a1_wa", (128, 8 * 8), BF16)
    for nm, sb, hh, ncols in [("wq", wq, "L0_attn_Wq", KD), ("wk", wk, "L0_attn_Wk", KD),
                              ("wv", wv, "L0_attn_Wv", VD), ("wb", wb, "L0_attn_Wb", 8),
                              ("wa", wa, "L0_attn_Wa", 8)]:
        src = io[hh].rearrange("(c p) n -> p c n", p=128)
        p.op("pool", lambda e, sb=sb, src=src: e.dma_start(out=sb[:], in_=src),
             writes=[("w", nm)], dma=True)
    cwq = ctx.sb("a1_cwq", (96, 32), F32)
    cwk = ctx.sb("a1_cwk", (96, 32), F32)
    cwv = ctx.sb("a1_cwv", (128, 48), F32)
    p.op("sp", lambda e: e.dma_start(
        out=cwq[:], in_=io["L0_attn_conv_q_w"].rearrange("(h c) k -> c h k", c=96)),
        writes=[("w", "cwq")], dma=True)
    p.op("sp", lambda e: e.dma_start(
        out=cwk[:], in_=io["L0_attn_conv_k_w"].rearrange("(h c) k -> c h k", c=96)),
        writes=[("w", "cwk")], dma=True)
    p.op("sp", lambda e: e.dma_start(
        out=cwv[:], in_=io["L0_attn_conv_v_w"].rearrange("(f c) k -> c f k", c=128)),
        writes=[("w", "cwv")], dma=True)
    alr = ctx.sb("a1_alr", (1, 8), F32)
    dtr = ctx.sb("a1_dtr", (1, 8), F32)
    row64 = ctx.sb("a1_row64", (1, 128), F32)
    dtb64 = ctx.sb("a1_dtb64", (128, 64), F32)
    negA64 = ctx.sb("a1_negA64", (128, 64), F32)
    p.op("sp", lambda e: e.dma_start(out=alr[:], in_=io["L0_attn_A_log"][None, :]),
         writes=[("w", "alr")], dma=True)
    p.op("sp", lambda e: e.dma_start(out=dtr[:], in_=io["L0_attn_dt_bias"][None, :]),
         writes=[("w", "dtr")], dma=True)
    for c in range(8):
        p.op("dve", lambda e, c=c: e.tensor_copy(row64[0:1, c * 8:(c + 1) * 8], dtr[:]),
             reads=[("w", "dtr")], writes=[("row64", c)])
    p.op("pe", lambda e: e.matmul(PB[0][:, 0:64], t["ones_r_f"][0:1, 0:128],
                                  row64[0:1, 0:64], start=True, stop=True),
         reads=[("row64", c) for c in range(8)] + [("const", "o2")], writes=[("pb", 0)])
    p.op("dve", lambda e: e.tensor_copy(dtb64[:], PB[0][:, 0:64]),
         reads=[("pb", 0)], writes=[("w", "dtb64")])
    alneg = ctx.sb("a1_alneg", (1, 8), F32)
    p.op("act", lambda e: e.activation(alneg[:], alr[:], AF.Exp),
         reads=[("w", "alr")], writes=[("w", "alne")])
    p.op("dve", lambda e: e.tensor_scalar(alneg[:], alneg[:], -1.0, None, ALU.mult),
         reads=[("w", "alne")], writes=[("w", "alneg")])
    for c in range(8):
        p.op("dve", lambda e, c=c: e.tensor_copy(row64[0:1, c * 8:(c + 1) * 8], alneg[:]),
             reads=[("w", "alneg")], writes=[("row64", c)])
    p.op("pe", lambda e: e.matmul(PB[0][:, 0:64], t["ones_r_f"][0:1, 0:128],
                                  row64[0:1, 0:64], start=True, stop=True),
         reads=[("row64", c) for c in range(8)] + [("const", "o2")], writes=[("pb", 0)])
    p.op("dve", lambda e: e.tensor_copy(negA64[:], PB[0][:, 0:64]),
         reads=[("pb", 0)], writes=[("w", "negA64")])

    hfm = ctx.sb("a1_hfm", (128, 8 * W), BF16)
    zc = ctx.sb("a1_zc", (128, 4 * (W + 3)), F32)
    ca = ctx.sb("a1_ca", (128, W), F32)
    qh = ctx.sb("a1_qh", (96, 8 * W), BF16)
    kh = ctx.sb("a1_kh", (96, 8 * W), BF16)
    vrot = ctx.sb("a1_vrot", (128, W), BF16)
    vtm = ctx.sb("a1_vtm", (128, 8 * VD), BF16)
    sq96 = ctx.sb("a1_sq96", (96, W), BF16)
    zbst = ctx.sb("a1_zbst", (8, W), BF16)
    zast = ctx.sb("a1_zast", (8, W), BF16)
    braw = ctx.sb("a1_braw", (128, 64), F32)
    araw = ctx.sb("a1_araw", (128, 64), F32)
    beta_tm = ctx.sb("a1_beta", (128, 64), F32)
    lnb_tm = ctx.sb("a1_lnb", (128, 64), F32)
    g_tm = ctx.sb("a1_g", (128, 64), F32)
    G_tm = ctx.sb("a1_G", (128, 64), F32)
    lk_tm = ctx.sb("a1_lk", (128, 64), F32)
    lq_tm = ctx.sb("a1_lq", (128, 64), F32)
    lkb_tm = ctx.sb("a1_lkb", (128, 64), F32)
    tmp64 = ctx.sb("a1_tmp64", (128, 64), F32)
    lnc8 = ctx.sb("a1_lnc8", (128, 8), F32)
    tq = ctx.sb("a1_tq", (96, 24), F32)
    tkk = ctx.sb("a1_tk", (96, 24), F32)
    tv = ctx.sb("a1_tv", (128, 36), F32)
    Sb = ctx.sb("a1_S", (96, 8 * DV), BF16)
    ost = ctx.sb("a1_ost", (128, VD), BF16)
    scanbufs = {}
    LNS = math.log(DK ** -0.5)

    def proj_qk(b, wdw, wsb, outsb, which):
        cw = cwq if which == "q" else cwk
        tail = tq if which == "q" else tkk
        dst = lq_tm if which == "q" else lk_tm
        if wdw == 0:
            p.op("dve", lambda e, tail=tail: e.memset(tail[:], 0.0),
                 writes=[("tail", which)])
        for h8 in range(8):
            slot = h8 % 4
            base = slot * (W + 3)
            for half in range(2):
                bk = (h8 * 2 + half) % 2
                for dc in range(8):
                    p.op("pe", lambda e, h8=h8, half=half, dc=dc, bk=bk, wsb=wsb:
                         e.matmul(PB[bk][0:96, 0:512],
                                  wsb[:, dc * KD + h8 * 96: dc * KD + h8 * 96 + 96],
                                  hfm[:, dc * W + half * 512: dc * W + (half + 1) * 512],
                                  start=(dc == 0), stop=(dc == 7)),
                         reads=[("w", "wq" if which == "q" else "wk"), ("hfm", dc)],
                         writes=[("pb", bk)],
                         hoist=("aQK", b, wdw, which, h8, half))
                col0 = base + 3 + half * 512
                if half == 0:
                    p.op("dve", lambda e, col0=col0, bk=bk: e.tensor_copy(
                        zc[0:96, col0:col0 + 512], PB[bk][0:96, 0:512]),
                        reads=[("pb", bk)], writes=[("zc", which, slot, 0)])
                else:
                    p.op("act", lambda e, col0=col0, bk=bk: e.activation(
                        zc[0:96, col0:col0 + 512], PB[bk][0:96, 0:512], AF.Copy),
                        reads=[("pb", bk)], writes=[("zc", which, slot, 1)])
            rd = [("zc", which, slot, 0), ("zc", which, slot, 1),
                  ("w", "cw" + which), ("tail", which)]
            p.op("dve", lambda e, base=base, h8=h8, tail=tail: e.tensor_copy(
                zc[0:96, base:base + 3], tail[:, h8 * 3:(h8 + 1) * 3]),
                reads=rd, writes=[("zc", which, slot, "pre")])
            p.op("dve", lambda e, base=base, h8=h8, tail=tail: e.tensor_copy(
                tail[:, h8 * 3:(h8 + 1) * 3], zc[0:96, base + W:base + W + 3]),
                reads=rd, writes=[("tail", which, h8)])
            rd2 = rd + [("zc", which, slot, "pre")]
            p.op("dve", lambda e, base=base, cw=cw, h8=h8: e.tensor_scalar(
                ca[0:96, 0:W], zc[0:96, base:base + W], cw[:, h8 * 4:h8 * 4 + 1],
                None, ALU.mult), reads=rd2, writes=[("ca",)])
            for tap in range(1, 4):
                p.op("dve", lambda e, base=base, cw=cw, h8=h8, tap=tap:
                     e.scalar_tensor_tensor(
                         ca[0:96, 0:W], zc[0:96, base + tap:base + tap + W],
                         cw[:, h8 * 4 + tap:h8 * 4 + tap + 1], ca[0:96, 0:W],
                         ALU.mult, ALU.add),
                     reads=rd2 + [("ca",)], writes=[("ca",)])
            p.op("act", lambda e, h8=h8, outsb=outsb: e.activation(
                outsb[0:96, h8 * W:(h8 + 1) * W], ca[0:96, 0:W], AF.Silu),
                reads=[("ca",)], writes=[("qkh", which, h8)])
            p.op("act", lambda e, h8=h8, outsb=outsb: e.activation(
                sq96[:], outsb[0:96, h8 * W:(h8 + 1) * W], AF.Square),
                reads=[("qkh", which, h8)], writes=[("sq96",)])
            for c in range(NC_W):
                p.op("pe", lambda e, c=c: e.matmul(
                    PB[2][:, c:c + 1], sq96[0:96, c * 128:(c + 1) * 128],
                    t["ones_c_bf"][0:96, :], start=True, stop=True),
                    reads=[("sq96",), ("const", "o3")], writes=[("pb", 2)])
            p.op("act", lambda e: e.activation(
                lnc8[:], PB[2][:, 0:8], AF.Ln, bias=t["eps6"], scale=1.0),
                reads=[("pb", 2), ("const", "c0")], writes=[("lncols",)])
            dstv = dst[:, h8::8]
            if which == "q":
                p.op("dve", lambda e, dstv=dstv: e.tensor_scalar(
                    dstv, lnc8[:], -0.5, LNS, ALU.mult, ALU.add),
                    reads=[("lncols",)], writes=[("lt", which, h8)])
            else:
                p.op("dve", lambda e, dstv=dstv: e.tensor_scalar(
                    dstv, lnc8[:], -0.5, None, ALU.mult),
                    reads=[("lncols",)], writes=[("lt", which, h8)])

    for b in range(B_PER_CORE):
        p.op("dve", lambda e: e.memset(Sb[:], 0.0), writes=[("Sb", h) for h in range(8)])
        for wdw in range(2):
            w0 = wdw * W
            for dc in range(8):
                p.op("sp", lambda e, b=b, dc=dc, w0=w0: e.dma_start(
                    out=hfm[:, dc * W:(dc + 1) * W],
                    in_=io["h_fm"][b, dc * 128:(dc + 1) * 128, w0:w0 + W]),
                    reads=[("d_h_fm", b, dc, w0 // 512), ("d_h_fm", b, dc, w0 // 512 + 1)],
                    writes=[("hfm", dc)], dma=True)
            proj_qk(b, wdw, wq, qh, "q")
            proj_qk(b, wdw, wk, kh, "k")
            if wdw == 0:
                p.op("dve", lambda e: e.memset(tv[:], 0.0), writes=[("tail", "v")])
            for fc in range(12):
                slot = fc % 4
                base = slot * (W + 3)
                for half in range(2):
                    bk = (fc * 2 + half) % 2
                    for dc in range(8):
                        p.op("pe", lambda e, fc=fc, half=half, dc=dc, bk=bk:
                             e.matmul(PB[bk][:, 0:512],
                                      wv[:, dc * VD + fc * 128: dc * VD + fc * 128 + 128],
                                      hfm[:, dc * W + half * 512: dc * W + (half + 1) * 512],
                                      start=(dc == 0), stop=(dc == 7)),
                             reads=[("w", "wv"), ("hfm", dc)], writes=[("pb", bk)],
                             hoist=("aV", b, w0, fc, half))
                    col0 = base + 3 + half * 512
                    p.op("dve" if half == 0 else "act",
                         (lambda e, col0=col0, bk=bk: e.tensor_copy(
                             zc[:, col0:col0 + 512], PB[bk][:, 0:512]))
                         if half == 0 else
                         (lambda e, col0=col0, bk=bk: e.activation(
                             zc[:, col0:col0 + 512], PB[bk][:, 0:512], AF.Copy)),
                         reads=[("pb", bk)], writes=[("zc", "v", slot, half)])
                rd = [("zc", "v", slot, 0), ("zc", "v", slot, 1), ("w", "cwv"),
                      ("tail", "v")]
                p.op("dve", lambda e, base=base, fc=fc: e.tensor_copy(
                    zc[:, base:base + 3], tv[:, fc * 3:(fc + 1) * 3]),
                    reads=rd, writes=[("zc", "v", slot, "pre")])
                p.op("dve", lambda e, base=base, fc=fc: e.tensor_copy(
                    tv[:, fc * 3:(fc + 1) * 3], zc[:, base + W:base + W + 3]),
                    reads=rd, writes=[("tail", "v", fc)])
                rd2 = rd + [("zc", "v", slot, "pre")]
                p.op("dve", lambda e, base=base, fc=fc: e.tensor_scalar(
                    ca[:, 0:W], zc[:, base:base + W], cwv[:, fc * 4:fc * 4 + 1],
                    None, ALU.mult), reads=rd2, writes=[("ca",)])
                for tap in range(1, 4):
                    p.op("dve", lambda e, base=base, fc=fc, tap=tap:
                         e.scalar_tensor_tensor(
                             ca[:, 0:W], zc[:, base + tap:base + tap + W],
                             cwv[:, fc * 4 + tap:fc * 4 + tap + 1], ca[:, 0:W],
                             ALU.mult, ALU.add),
                         reads=rd2 + [("ca",)], writes=[("ca",)])
                p.op("act", lambda e: e.activation(vrot[:], ca[:, 0:W], AF.Silu),
                     reads=[("ca",)], writes=[("vrot",)])
                for tc in range(8):
                    p.op("pe", lambda e, tc=tc: e.transpose(
                        PT[tc % 2][:, 0:128], vrot[:, tc * 128:(tc + 1) * 128],
                        t["ident_bf"]),
                        reads=[("vrot",), ("const", "c_ident_bf")],
                        writes=[("pt", tc % 2)])
                    dcol = tc * VD + fc * 128
                    p.op("dve" if tc % 2 == 0 else "act",
                         (lambda e, dcol=dcol, tc=tc: e.tensor_copy(
                             vtm[:, dcol:dcol + 128], PT[tc % 2][:, 0:128]))
                         if tc % 2 == 0 else
                         (lambda e, dcol=dcol, tc=tc: e.activation(
                             vtm[:, dcol:dcol + 128], PT[tc % 2][:, 0:128],
                             AF.Copy)),
                         reads=[("pt", tc % 2)], writes=[("vtm", tc, fc)])
            for which, wsb, dst in [("b", wb, zbst), ("a", wa, zast)]:
                for half in range(2):
                    for dc in range(8):
                        p.op("pe", lambda e, wsb=wsb, dc=dc, half=half: e.matmul(
                            PB[3][0:8, 0:512], wsb[:, dc * 8:(dc + 1) * 8],
                            hfm[:, dc * W + half * 512: dc * W + (half + 1) * 512],
                            start=(dc == 0), stop=(dc == 7)),
                            hoist=("aBA", b, w0, which, half),
                            reads=[("w", "w" + which), ("hfm", dc)], writes=[("pb", 3)])
                    p.op("dve", lambda e, dst=dst, half=half: e.tensor_copy(
                        dst[:, half * 512:(half + 1) * 512], PB[3][0:8, 0:512]),
                        reads=[("pb", 3)], writes=[("zba", which, half)])
                for c in range(NC_W):
                    p.op("pe", lambda e, dst=dst, c=c: e.transpose(
                        PT[0][:, 0:8], dst[0:8, c * 128:(c + 1) * 128],
                        t["ident_bf"][0:8, 0:8]),
                        reads=[("zba", which, 0), ("zba", which, 1),
                               ("const", "c_ident_bf")],
                        writes=[("pt", 0)])
                    tgt = braw if which == "b" else araw
                    p.op("dve", lambda e, tgt=tgt, c=c: e.tensor_copy(
                        tgt[:, c * 8:(c + 1) * 8], PT[0][:, 0:8]),
                        reads=[("pt", 0)], writes=[("traw", which, c)])
            rba = [("traw", "b", c) for c in range(NC_W)]
            raa = [("traw", "a", c) for c in range(NC_W)]
            p.op("act", lambda e: e.activation(beta_tm[:], braw[:], AF.Sigmoid),
                 reads=rba, writes=[("beta",)])
            p.op("act", lambda e: e.activation(lnb_tm[:], beta_tm[:], AF.Ln),
                 reads=[("beta",)], writes=[("lnb",)])
            p.op("dve", lambda e: e.tensor_add(araw[:], araw[:], dtb64[:]),
                 reads=raa + [("w", "dtb64")], writes=[("gx",)])
            p.op("act", lambda e: e.activation(tmp64[:], araw[:], AF.Abs),
                 reads=[("gx",)], writes=[("gabs",)])
            p.op("act", lambda e: e.activation(tmp64[:], tmp64[:], AF.Exp, scale=-1.0),
                 reads=[("gabs",)], writes=[("gabs",)])
            p.op("act", lambda e: e.activation(tmp64[:], tmp64[:], AF.Ln,
                                               bias=t["one"], scale=1.0),
                 reads=[("gabs",)], writes=[("gabs",)])
            p.op("dve", lambda e: e.tensor_scalar(g_tm[:], araw[:], t["zero"],
                                                  None, ALU.max),
                 reads=[("gx",), ("const", "c3")], writes=[("grelu",)])
            p.op("dve", lambda e: e.tensor_add(g_tm[:], g_tm[:], tmp64[:]),
                 reads=[("grelu",), ("gabs",)], writes=[("gsp",)])
            p.op("dve", lambda e: e.tensor_mul(g_tm[:], g_tm[:], negA64[:]),
                 reads=[("gsp",), ("w", "negA64")], writes=[("g",)])
            p.op("dve", lambda e: e.tensor_add(lkb_tm[:], lk_tm[:], lnb_tm[:]),
                 reads=[("lt", "k", h) for h in range(8)] + [("lnb",)],
                 writes=[("lklnb",)])
            if "dbg" in io and b == 0 and wdw == 0:
                p.op("sp", lambda e: e.dma_start(out=io["dbg"][0:96, :],
                                                 in_=qh[:]),
                     reads=[("qkh", "q", h) for h in range(8)],
                     writes=[("d_dbg", 0)], dma=True)
                p.op("sp", lambda e: e.dma_start(out=io["dbg"][96:192, :],
                                                 in_=kh[:]),
                     reads=[("qkh", "k", h) for h in range(8)],
                     writes=[("d_dbg", 1)], dma=True)
                p.op("sp", lambda e: e.dma_start(out=io["dbg2"][:, 0:64],
                                                 in_=beta_tm[:]),
                     reads=[("beta",)], writes=[("d_dbg", 2)], dma=True)
                p.op("sp", lambda e: e.dma_start(out=io["dbg2"][:, 64:128],
                                                 in_=g_tm[:]),
                     reads=[("g",)], writes=[("d_dbg", 3)], dma=True)
                p.op("sp", lambda e: e.dma_start(out=io["dbg2"][:, 128:192],
                                                 in_=lk_tm[:]),
                     reads=[("lt", "k", h) for h in range(8)],
                     writes=[("d_dbg", 4)], dma=True)
                p.op("sp", lambda e: e.dma_start(out=io["dbg2"][:, 192:256],
                                                 in_=lq_tm[:]),
                     reads=[("lt", "q", h) for h in range(8)],
                     writes=[("d_dbg", 5)], dma=True)
                p.op("sp", lambda e: e.dma_start(out=io["dbg3"][:, :],
                                                 in_=vtm[:]),
                     reads=[("vtm", tc, fc) for tc in range(8) for fc in range(12)],
                     writes=[("d_dbg", 6)], dma=True)
            a = scanbufs
            a.update(dict(hfm=hfm, qh=qh, kh=kh, vtm=vtm, beta_tm=beta_tm,
                          g_tm=g_tm, G_tm=G_tm, lk_tm=lk_tm, lq_tm=lq_tm,
                          lkb_tm=lkb_tm, Sb=Sb, ost=ost))
            emit_scan_window(ctx, t, io, b, w0, a)
    p.barrier()


def emit_scan_window(ctx, t, io, b, w0, a):
    p = ctx.p
    PB, PT = ctx.psum, ctx.psb
    W = 1024
    gcc = a.setdefault("_gcc", ctx.sb("a1_gcc", (8, 1), F32))
    gcr = a.setdefault("_gcr", ctx.sb("a1_gcr", (1, 8), F32))
    if "_Dn" not in a:
        a["_Dn"] = [ctx.sb("a1_Dn", (128, 128), BF16) for _ in range(2)]
        a["_Dnt"] = [ctx.sb("a1_Dnt", (128, 128), BF16) for _ in range(2)]
        a["_Dqk"] = [ctx.sb("a1_Dqk", (128, 128), BF16) for _ in range(2)]
        a["_N"] = [ctx.sb("a1_N", (128, 128), BF16) for _ in range(2)]
        a["_NT"] = [ctx.sb("a1_NT", (128, 128), BF16) for _ in range(2)]
        a["_X"] = [ctx.sb("a1_X", (128, 128), BF16) for _ in range(2)]
        a["_U"] = [ctx.sb("a1_U", (128, DV), BF16) for _ in range(2)]
        a["_Usc"] = [ctx.sb("a1_Usc", (128, DV), BF16) for _ in range(2)]
        a["_ktm"] = [ctx.sb("a1_ktm", (128, 96), BF16) for _ in range(2)]
        a["_tmpv"] = [ctx.sb("a1_tmpv", (128, DV), F32) for _ in range(2)]
        a["_o2t"] = [ctx.sb("a1_o2t", (128, DV), F32) for _ in range(2)]
        a["_tmask"] = [ctx.sb("a1_tmask", (128, 256), F32) for _ in range(2)]
        a["_cch"] = ctx.sb("a1_cch", (128, 96), F32)
        a["_rows"] = ctx.sb("a1_rows", (8, 3 * 128), F32)
        a["_basis"] = ctx.sb("a1_basis", (8, 8 * 128), F32)
        hh = ctx.const_in("c_basis8")
        p.op("sp", lambda e, hh=hh: e.dma_start(out=a["_basis"][:], in_=hh),
             writes=[("bas", h) for h in range(8)], dma=True)
    cch = a["_cch"]
    rows = a["_rows"]
    basis = a["_basis"]

    def chunk_pre(c):
        c8 = c * 8
        pby = 2 + (c % 2)
        yield ("pe", lambda e, c8=c8, pby=pby: e.matmul(
            PB[pby][:, 0:8], t["triu_f"], a["g_tm"][:, c8:c8 + 8],
            start=True, stop=True),
            [("g",), ("const", "c_triu_f")], [("pb", pby)])
        yield ("dve", lambda e, c8=c8, pby=pby: e.tensor_copy(
            a["G_tm"][:, c8:c8 + 8], PB[pby][:, 0:8]),
            [("pb", pby)], [("G", c)])
        yield ("pe", lambda e, c8=c8, pby=pby: e.matmul(
            PB[pby][0:8, 0:1], a["g_tm"][:, c8:c8 + 8], t["ones_c_f"],
            start=True, stop=True),
            [("g",), ("const", "o4")], [("pb", pby)])
        yield ("dve", lambda e, pby=pby: e.tensor_copy(gcc[:], PB[pby][0:8, 0:1]),
               [("pb", pby)], [("gcc",)])
        yield ("pe", lambda e, pby=pby: e.transpose(
            PB[pby][0:1, 0:8], gcc[:], t["ident_f"][0:8, 0:8]),
            [("gcc",), ("const", "c_ident_f")], [("pb", pby)])
        yield ("dve", lambda e, pby=pby: e.tensor_copy(gcr[:], PB[pby][0:1, 0:8]),
               [("pb", pby)], [("gcrow",)])
        G8 = a["G_tm"][:, c8:c8 + 8]
        lk8 = a["lk_tm"][:, c8:c8 + 8]
        lq8 = a["lq_tm"][:, c8:c8 + 8]
        lkb8 = a["lkb_tm"][:, c8:c8 + 8]
        rGl = [("G", c)] + [("lt", "k", h) for h in range(8)] + \
              [("lt", "q", h) for h in range(8)] + [("lklnb",)]
        yield ("dve", lambda e, G8=G8, lk8=lk8: e.tensor_add(
            cch[:, 0:8], G8, lk8), rGl, [("cc", c, 0)])
        yield ("dve", lambda e, G8=G8, lkb8=lkb8: e.tensor_add(
            cch[:, 8:16], G8, lkb8), rGl, [("cc", c, 1)])
        yield ("dve", lambda e, G8=G8, lk8=lk8: e.tensor_sub(
            cch[:, 16:24], G8, lk8), rGl, [("cc", c, 2)])
        yield ("dve", lambda e, G8=G8, lq8=lq8: e.tensor_add(
            cch[:, 24:32], G8, lq8), rGl, [("cc", c, 3)])
        yield ("dve", lambda e: e.tensor_scalar(
            cch[:, 32:40], cch[:, 16:24], -1.0, None, ALU.mult),
            [("cc", c, 2)], [("cc", c, 4)])
        yield ("pe", lambda e: e.matmul(
            PB[pby][:, 0:8], t["ones_r_f"][0:1, 0:128], gcr[:],
            start=True, stop=True),
            [("gcrow",), ("const", "o2")], [("pb", pby)])
        yield ("dve", lambda e, pby=pby: e.tensor_copy(
            cch[:, 40:48], PB[pby][:, 0:8]), [("pb", pby)], [("cc", c, 5)])
        yield ("dve", lambda e: e.tensor_sub(
            cch[:, 48:56], cch[:, 40:48], cch[:, 16:24]),
            [("cc", c, 5), ("cc", c, 2)], [("cc", c, 6)])
        yield ("act", lambda e: e.activation(cch[:, 56:64], cch[:, 48:56], AF.Exp),
               [("cc", c, 6)], [("cc", c, 7)])
        yield ("act", lambda e: e.activation(cch[:, 64:72], cch[:, 0:8], AF.Exp),
               [("cc", c, 0)], [("cc", c, 8)])
        yield ("dve", lambda e: e.tensor_scalar(
            cch[:, 64:72], cch[:, 64:72], -1.0, None, ALU.mult),
            [("cc", c, 8)], [("cc", c, 8)])
        yield ("act", lambda e: e.activation(cch[:, 72:80], cch[:, 24:32], AF.Exp),
               [("cc", c, 3)], [("cc", c, 9)])
        yield ("act", lambda e: e.activation(cch[:, 80:88], cch[:, 40:48], AF.Exp),
               [("cc", c, 5)], [("cc", c, 10)])
        for ri, coff, kk in ((0, 8, "rowA"), (1, 16, "rowB"), (2, 24, "rowQ")):
            yield ("pe", lambda e, ri=ri, coff=coff, pby=pby: e.transpose(
                PB[pby][0:8, 0:128], cch[:, coff:coff + 8], t["ident_f"]),
                [("cc", c, coff // 8), ("const", "c_ident_f")], [("pb", pby)])
            yield ("dve", lambda e, ri=ri, pby=pby: e.tensor_copy(
                rows[:, ri * 128:(ri + 1) * 128], PB[pby][0:8, 0:128]),
                [("pb", pby)], [("rows", c, ri)])

    def instance(c, h):
        c8 = c * 8
        pr = (c8 + h) % 2
        xbk, ybk, zbk = (0, 2, 4) if pr == 0 else (1, 3, 5)
        Dn, Dnt, Dqk = a["_Dn"][pr], a["_Dnt"][pr], a["_Dqk"][pr]
        Ntl, NTl, Xt = a["_N"][pr], a["_NT"][pr], a["_X"][pr]
        Ut, Usc, ktm = a["_U"][pr], a["_Usc"][pr], a["_ktm"][pr]
        tmpv, o2t, tmask = a["_tmpv"][pr], a["_o2t"][pr], a["_tmask"][pr]
        ptb = PT[pr]
        bcol = a["beta_tm"][:, c8 + h:c8 + h + 1]
        khs = a["kh"][0:96, h * W + c * 128: h * W + c * 128 + 128]
        qhs = a["qh"][0:96, h * W + c * 128: h * W + c * 128 + 128]
        vslice = a["vtm"][:, c * VD + h * DV: c * VD + h * DV + DV]
        Ss = a["Sb"][0:96, h * DV:(h + 1) * DV]
        K = lambda *kk: tuple(kk) + (pr,)
        colA = cch[:, 8 + h:9 + h]
        negB = cch[:, 32 + h:33 + h]
        e6c = cch[:, 56 + h:57 + h]
        negEGlk = cch[:, 64 + h:65 + h]
        expQ = cch[:, 72 + h:73 + h]
        eGc = cch[:, 80 + h:81 + h]
        for ri, mk, dd, dkey, bias, sc, mkey in (
                (0, "c_mask_us", Dnt, "Dnt", negB, 1.0, ("cc", c, 4)),
                (1, "c_nmask_ls", Dn, "Dn", colA, -1.0, ("cc", c, 1)),
                (2, "c_mask_ui", Dqk, "Dqk", negB, 1.0, ("cc", c, 4))):
            yield ("pe", lambda e, ri=ri, h=h, ybk=ybk: e.matmul(
                PB[ybk][:, 0:128], basis[:, h * 128:(h + 1) * 128],
                rows[:, ri * 128:(ri + 1) * 128], start=True, stop=True),
                [("rows", c, ri), ("bas", h)], [("pb", ybk)])
            yield ("dve", lambda e, tmask=tmask, ybk=ybk, mk=mk: e.tensor_add(
                tmask[:, 0:128], PB[ybk][:, 0:128], t[mk[2:]]),
                [("pb", ybk), ("const", mk)], [K("tm")])
            yield ("act", lambda e, dd=dd, tmask=tmask, bias=bias, sc=sc:
                   e.activation(dd[:], tmask[:, 0:128], AF.Exp, bias=bias,
                                scale=sc),
                   [K("tm"), mkey], [K(dkey)])
        yield ("pe", lambda e, khs=khs, xbk=xbk: e.matmul(
            PB[xbk][:, 0:128], khs, khs, start=True, stop=True),
            [("qkh", "k", h)], [("pb", xbk)])
        yield ("pe", lambda e, khs=khs, qhs=qhs, zbk=zbk: e.matmul(
            PB[zbk][:, 0:128], khs, qhs, start=True, stop=True),
            [("qkh", "k", h), ("qkh", "q", h)], [("pb", zbk)])
        yield ("dve", lambda e, Ntl=Ntl, xbk=xbk, Dn=Dn: e.scalar_tensor_tensor(
            Ntl[:], PB[xbk][:, 0:128], t["neg1"], Dn[:], ALU.mult, ALU.mult),
            [("pb", xbk), K("Dn"), ("const", "c4")], [K("Ntl")])
        yield ("dve", lambda e, NTl=NTl, xbk=xbk, Dnt=Dnt: e.scalar_tensor_tensor(
            NTl[:], PB[xbk][:, 0:128], t["neg1"], Dnt[:], ALU.mult, ALU.mult),
            [("pb", xbk), K("Dnt"), ("const", "c4")], [K("NTl")])
        yield ("dve", lambda e, Xt=Xt, zbk=zbk, Dqk=Dqk: e.tensor_mul(
            Xt[:], PB[zbk][:, 0:128], Dqk[:]),
            [("pb", zbk), K("Dqk")], [K("Xt")])
        yield ("pe", lambda e, khs=khs, Ss=Ss, zbk=zbk: e.matmul(
            PB[zbk][:, 0:DV], khs, Ss, start=True, stop=True),
            [("qkh", "k", h), ("Sb", h)], [("pb", zbk)])
        yield ("dve", lambda e, vslice=vslice, tmpv=tmpv, zbk=zbk, negEGlk=negEGlk:
               e.scalar_tensor_tensor(
                   tmpv[:], PB[zbk][:, 0:DV], negEGlk, vslice, ALU.mult, ALU.add),
               [("pb", zbk), ("cc", c, 8)] + [("vtm", c, fc) for fc in range(12)],
               [K("tmpv")])
        yield ("dve", lambda e, bcol=bcol, Ut=Ut, tmpv=tmpv: e.tensor_scalar(
            Ut[:], tmpv[:], bcol, None, ALU.mult),
            [K("tmpv"), ("beta",)], [K("Ut")])
        for k in range(7):
            yield ("pe", lambda e, NTl=NTl, Ut=Ut, zbk=zbk: e.matmul(
                PB[zbk][:, 0:DV], NTl[:], Ut[:], start=True, stop=True),
                [K("NTl"), K("Ut")], [("pb", zbk)])
            if k < 6:
                yield ("pe", lambda e, Ntl=Ntl, NTl=NTl, xbk=xbk: e.matmul(
                    PB[xbk][:, 0:128], Ntl[:], NTl[:], start=True, stop=True),
                    [K("Ntl"), K("NTl")], [("pb", xbk)])
                yield ("pe", lambda e, Ntl=Ntl, NTl=NTl, ybk=ybk: e.matmul(
                    PB[ybk][:, 0:128], NTl[:], Ntl[:], start=True, stop=True),
                    [K("Ntl"), K("NTl")], [("pb", ybk)])
            yield ("dve", lambda e, Ut=Ut, zbk=zbk: e.tensor_add(
                Ut[:], PB[zbk][:, 0:DV], Ut[:]),
                [("pb", zbk)], [K("Ut")])
            if k < 6:
                yield ("act", lambda e, NTl=NTl, xbk=xbk: e.activation(
                    NTl[:], PB[xbk][:, 0:128], AF.Copy),
                    [("pb", xbk)], [K("NTl")])
                yield ("act", lambda e, Ntl=Ntl, ybk=ybk: e.activation(
                    Ntl[:], PB[ybk][:, 0:128], AF.Copy),
                    [("pb", ybk)], [K("Ntl")])
        yield ("pe", lambda e, qhs=qhs, Ss=Ss, xbk=xbk: e.matmul(
            PB[xbk][:, 0:DV], qhs, Ss, start=True, stop=True),
            [("qkh", "q", h), ("Sb", h)], [("pb", xbk)])
        yield ("pe", lambda e, Xt=Xt, Ut=Ut, ybk=ybk: e.matmul(
            PB[ybk][:, 0:DV], Xt[:], Ut[:], start=True, stop=True),
            [K("Xt"), K("Ut")], [("pb", ybk)])
        yield ("dve", lambda e, o2t=o2t, xbk=xbk, expQ=expQ: e.tensor_scalar(
            o2t[:], PB[xbk][:, 0:DV], expQ, None, ALU.mult),
            [("pb", xbk), ("cc", c, 9)], [K("o2t")])
        yield ("dve", lambda e, h=h, o2t=o2t, ybk=ybk: e.tensor_add(
            a["ost"][:, h * DV:(h + 1) * DV], PB[ybk][:, 0:DV], o2t[:]),
            [("pb", ybk), K("o2t")], [("ost", h)])
        yield ("pe", lambda e, khs=khs, ptb=ptb: e.transpose(
            ptb[:, 0:96], khs, t["ident_bf"][0:96, 0:96]),
            [("qkh", "k", h), ("const", "c_ident_bf")], [("pt", pr)])
        yield ("dve", lambda e, ktm=ktm, ptb=ptb: e.tensor_copy(
            ktm[:, 0:96], ptb[:, 0:96]),
            [("pt", pr)], [K("ktm")])
        yield ("dve", lambda e, Usc=Usc, Ut=Ut, e6c=e6c: e.tensor_scalar(
            Usc[:], Ut[:], e6c, None, ALU.mult),
            [K("Ut"), ("cc", c, 7)], [K("Usc")])
        yield ("pe", lambda e, ktm=ktm, Usc=Usc, zbk=zbk: e.matmul(
            PB[zbk][0:96, 0:DV], ktm[:, 0:96], Usc[:], start=True, stop=True),
            [K("ktm"), K("Usc")], [("pb", zbk)])
        yield ("dve", lambda e, Ss=Ss, zbk=zbk, eGc=eGc: e.scalar_tensor_tensor(
            Ss, Ss, eGc[0:96, :], PB[zbk][0:96, 0:DV], ALU.mult, ALU.add),
            [("pb", zbk), ("cc", c, 10), ("Sb", h)], [("Sb", h)])

    def run(gen):
        for eng, fn, rd, wr in gen:
            p.op(eng, fn, reads=rd, writes=wr)

    def interleave(g1, g2):
        it1, it2 = iter(g1), iter(g2)
        while True:
            done = 0
            for it in (it1, it2):
                try:
                    eng, fn, rd, wr = next(it)
                    p.op(eng, fn, reads=rd, writes=wr)
                except StopIteration:
                    done += 1
            if done == 2:
                break

    for c in range(8):
        run(chunk_pre(c))
        for hp in range(4):
            interleave(instance(c, 2 * hp), instance(c, 2 * hp + 1))
        p.op("sp", lambda e, b=b, c=c: e.dma_start(
            out=io["o_st"][b, w0 + c * 128:w0 + c * 128 + 128, :],
            in_=a["ost"][:]),
            reads=[("ost", h) for h in range(8)],
            writes=[("d_ost", b, (w0 + c * 128) // 128)], dma=True)


def emit_A2(ctx, t, io):
    p = ctx.p
    PB, PT = ctx.psum, ctx.psb
    wg = ctx.sb("a2_wg", (128, 8 * VD), BF16)
    wo = ctx.sb("a2_wo", (128, 12 * D), BF16)
    p.op("pool", lambda e: e.dma_start(
        out=wg[:], in_=io["L0_attn_Wg"].rearrange("(c p) n -> p c n", p=128)),
        writes=[("w", "wg")], dma=True)
    p.op("pool", lambda e: e.dma_start(
        out=wo[:], in_=io["L0_attn_Wo"].rearrange("(c p) n -> p c n", p=128)),
        writes=[("w", "wo")], dma=True)
    onr = ctx.sb("a2_onr", (1, VD), F32)
    onw1536 = ctx.sb("a2_onw", (128, VD), F32)
    w2row = ctx.sb("a2_w2row", (1, D), F32)
    w2rep = ctx.sb("a2_w2rep", (128, D), F32)
    p.op("sp", lambda e: e.dma_start(out=onr[0:1, 0:DV],
                                     in_=io["L0_attn_onorm_w"][None, :]),
         writes=[("onr", 0)], dma=True)
    for h in range(1, 8):
        p.op("dve", lambda e, h=h: e.tensor_copy(onr[0:1, h * DV:(h + 1) * DV],
                                                 onr[0:1, 0:DV]),
             reads=[("onr", 0)], writes=[("onr", h)])
    for seg in range(3):
        p.op("pe", lambda e, seg=seg: e.matmul(
            PB[0][:, 0:512], t["ones_r_f"][0:1, 0:128],
            onr[0:1, seg * 512:(seg + 1) * 512], start=True, stop=True),
            reads=[("onr", h) for h in range(8)] + [("const", "o2")],
            writes=[("pb", 0)])
        p.op("dve", lambda e, seg=seg: e.tensor_copy(
            onw1536[:, seg * 512:(seg + 1) * 512], PB[0][:, 0:512]),
            reads=[("pb", 0)], writes=[("onw",)])
    p.op("sp", lambda e: e.dma_start(out=w2row[:], in_=io["L0_attn_norm2_w"][None, :]),
         writes=[("w2row",)], dma=True)
    for seg in range(2):
        p.op("pe", lambda e, seg=seg: e.matmul(
            PB[0][:, 0:512], t["ones_r_f"][0:1, 0:128],
            w2row[0:1, seg * 512:(seg + 1) * 512], start=True, stop=True),
            reads=[("w2row",), ("const", "o2")], writes=[("pb", 0)])
        p.op("dve", lambda e, seg=seg: e.tensor_copy(
            w2rep[:, seg * 512:(seg + 1) * 512], PB[0][:, 0:512]),
            reads=[("pb", 0)], writes=[("w2rep",)])

    ot = ctx.sb("a2_ot", (128, VD), BF16)
    sqt = ctx.sb("a2_sq", (128, VD), BF16)
    hfmt = ctx.sb("a2_hfmt", (128, 8 * 128), BF16)
    gsl = ctx.sb("a2_gsl", (128, VD), BF16)
    og = ctx.sb("a2_og", (128, VD), BF16)
    ogfm = ctx.sb("a2_ogfm", (128, 12 * 128), BF16)
    xnt = ctx.sb("a2_xnt", (128, D), F32)
    xm = ctx.sb("a2_xm", (128, D), F32)
    h2b = ctx.sb("a2_h2b", (128, D), BF16)
    st = ctx.sb("a2_st", (128, 12), F32)
    stage = ctx.sb("a2_stage", (128, 8 * 512), BF16)

    for b in range(B_PER_CORE):
        for grp in range(4):
            for t4 in range(4):
                tt = grp * 4 + t4
                r0 = tt * 128
                p.op("sp", lambda e, b=b, r0=r0: e.dma_start(
                    out=ot[:], in_=io["o_st"][b, r0:r0 + 128, :]),
                    reads=[("d_ost", b, r0 // 128)], writes=[("a2ot",)], dma=True)
                for dc in range(8):
                    p.op("sp", lambda e, b=b, dc=dc, r0=r0: e.dma_start(
                        out=hfmt[:, dc * 128:(dc + 1) * 128],
                        in_=io["h_fm"][b, dc * 128:(dc + 1) * 128, r0:r0 + 128]),
                        reads=[("d_h_fm", b, dc, r0 // 512)],
                        writes=[("a2hf", dc)], dma=True)
                p.op("sp", lambda e, b=b, r0=r0: e.dma_start(
                    out=xnt[:], in_=io["xn0_tm"][b, r0:r0 + 128, :]),
                    reads=[("d_xn0_tm", b, tt)], writes=[("a2xn",)], dma=True)
                p.op("act", lambda e: e.activation(sqt[:], ot[:], AF.Square),
                     reads=[("a2ot",)], writes=[("a2sq",)])
                p.op("dve", lambda e: e.reduce_sum(
                    st[:, 0:8], sqt.rearrange("p (h v) -> p h v", v=DV), AX.X),
                    reads=[("a2sq",)], writes=[("a2st", 0)])
                p.op("act", lambda e: e.activation(st[:, 0:8], st[:, 0:8], AF.Ln,
                                                   bias=t["eps6"], scale=1.0 / DV),
                     reads=[("a2st", 0), ("const", "c0")], writes=[("a2st", 1)])
                p.op("act", lambda e: e.activation(st[:, 0:8], st[:, 0:8], AF.Exp,
                                                   scale=-0.5),
                     reads=[("a2st", 1)], writes=[("a2st", 2)])
                for fb in range(3):
                    for dc in range(8):
                        p.op("pe", lambda e, fb=fb, dc=dc: e.matmul(
                            PB[fb][:, 0:512], hfmt[:, dc * 128:(dc + 1) * 128],
                            wg[:, dc * VD + fb * 512: dc * VD + (fb + 1) * 512],
                            start=(dc == 0), stop=(dc == 7)),
                            reads=[("a2hf", dc), ("w", "wg")], writes=[("pb", fb)],
                            hoist=("a2G", b, tt, fb))
                    p.op("act", lambda e, fb=fb: e.activation(
                        gsl[:, fb * 512:(fb + 1) * 512], PB[fb][:, 0:512], AF.Silu),
                        reads=[("pb", fb)], writes=[("a2gsl", fb)])
                p.op("dve", lambda e: e.tensor_mul(gsl[:], gsl[:], onw1536[:]),
                     reads=[("a2gsl", 0), ("a2gsl", 1), ("a2gsl", 2), ("onw",)],
                     writes=[("a2gsl2",)])
                for h in range(8):
                    p.op("dve", lambda e, h=h: e.tensor_scalar(
                        og[:, h * DV:(h + 1) * DV], ot[:, h * DV:(h + 1) * DV],
                        st[:, h:h + 1], None, ALU.mult),
                        reads=[("a2ot",), ("a2st", 2)], writes=[("a2og", h)])
                p.op("dve", lambda e: e.tensor_mul(og[:], og[:], gsl[:]),
                     reads=[("a2og", h) for h in range(8)] + [("a2gsl2",)],
                     writes=[("a2ogf",)])
                for vc in range(12):
                    p.op("pe", lambda e, vc=vc: e.transpose(
                        PT[0][:, 0:128], og[:, vc * 128:(vc + 1) * 128],
                        t["ident_bf"]),
                        reads=[("a2ogf",), ("const", "c_ident_bf")],
                        writes=[("pt", 0)])
                    eng = "dve" if vc % 2 == 0 else "act"
                    if eng == "dve":
                        p.op("dve", lambda e, vc=vc: e.tensor_copy(
                            ogfm[:, vc * 128:(vc + 1) * 128], PT[0][:, 0:128]),
                            reads=[("pt", 0)], writes=[("a2ogfm", vc)])
                    else:
                        p.op("act", lambda e, vc=vc: e.activation(
                            ogfm[:, vc * 128:(vc + 1) * 128], PT[0][:, 0:128],
                            AF.Copy),
                            reads=[("pt", 0)], writes=[("a2ogfm", vc)])
                for dh in range(2):
                    for vc in range(12):
                        p.op("pe", lambda e, dh=dh, vc=vc: e.matmul(
                            PB[3 + dh][:, 0:512], ogfm[:, vc * 128:(vc + 1) * 128],
                            wo[:, vc * D + dh * 512: vc * D + (dh + 1) * 512],
                            start=(vc == 0), stop=(vc == 11)),
                            reads=[("a2ogfm", vc), ("w", "wo")],
                            writes=[("pb", 3 + dh)],
                            hoist=("a2Wo", b, tt, dh))
                    p.op("dve", lambda e, dh=dh: e.tensor_add(
                        xm[:, dh * 512:(dh + 1) * 512], PB[3 + dh][:, 0:512],
                        xnt[:, dh * 512:(dh + 1) * 512]),
                        reads=[("pb", 3 + dh), ("a2xn",)], writes=[("a2xm", dh)])
                p.op("sp", lambda e, b=b, r0=r0: e.dma_start(
                    out=io["xmid"][b, r0:r0 + 128, :], in_=xm[:]),
                    reads=[("a2xm", 0), ("a2xm", 1)],
                    writes=[("d_xmid", b, r0 // 128)], dma=True)
                p.op("act", lambda e: e.activation(sqt[:, 0:D], xm[:], AF.Square,
                                                   accum_out=st[:, 8:9]),
                     reads=[("a2xm", 0), ("a2xm", 1)], writes=[("a2st", 3), ("a2sq",)])
                p.op("act", lambda e: e.activation(st[:, 9:10], st[:, 8:9], AF.Ln,
                                                   bias=t["eps6"], scale=1.0 / D),
                     reads=[("a2st", 3), ("const", "c0")], writes=[("a2st", 4)])
                p.op("act", lambda e: e.activation(st[:, 10:11], st[:, 9:10], AF.Exp,
                                                   scale=-0.5),
                     reads=[("a2st", 4)], writes=[("a2st", 5)])
                p.op("dve", lambda e: e.scalar_tensor_tensor(
                    h2b[:], xm[:], st[:, 10:11], w2rep[:], ALU.mult, ALU.mult),
                    reads=[("a2xm", 0), ("a2xm", 1), ("a2st", 5), ("w2rep",)],
                    writes=[("a2h2",)])
                for j in range(8):
                    c0 = j * 512 + t4 * 128
                    p.op("pe", lambda e, j=j: e.transpose(
                        PT[1][:, 0:128], h2b[:, j * 128:(j + 1) * 128],
                        t["ident_bf"]),
                        reads=[("a2h2",), ("const", "c_ident_bf")],
                        writes=[("pt", 1)])
                    p.op("act", lambda e, c0=c0: e.activation(
                        stage[:, c0:c0 + 128], PT[1][:, 0:128], AF.Copy),
                        reads=[("pt", 1)], writes=[("a2stg", j, t4)])
            for j in range(8):
                cc = grp * 512
                p.op("sp", lambda e, b=b, j=j, cc=cc: e.dma_start(
                    out=io["h2fm"][b, j * 128:(j + 1) * 128, cc:cc + 512],
                    in_=stage[:, j * 512:(j + 1) * 512]),
                    reads=[("a2stg", j, 0), ("a2stg", j, 1), ("a2stg", j, 2),
                           ("a2stg", j, 3)],
                    writes=[("d_h2fm", b, j, grp)], dma=True)
    p.barrier()


def emit_A3(ctx, t, io):
    p = ctx.p
    PB, PT = ctx.psum, ctx.psb
    wd = ctx.sb("a3_wd", (128, 32 * D), BF16)
    p.op("pool", lambda e: e.dma_start(
        out=wd[:], in_=io["L0_attn_Wdown"].rearrange("(c p) n -> p c n", p=128)),
        writes=[("w", "wd")], dma=True)
    wg_t = [ctx.sb("a3_wg", (128, 8 * 128), BF16) for _ in range(2)]
    wu_t = [ctx.sb("a3_wu", (128, 8 * 128), BF16) for _ in range(2)]
    h2w = ctx.sb("a3_h2w", (128, 8 * 1024), BF16)
    intfm = ctx.sb("a3_int", (128, 32 * 1024), BF16)
    gs = [ctx.sb("a3_gs", (128, 512), BF16) for _ in range(2)]
    xmt = ctx.sb("a3_xm", (128, D), F32)
    s0t = ctx.sb("a3_s0", (128, D), BF16)

    for b in range(B_PER_CORE):
        for sb2 in range(2):
            w0 = sb2 * 1024
            for dc in range(8):
                p.op("sp", lambda e, b=b, dc=dc, w0=w0: e.dma_start(
                    out=h2w[:, dc * 1024:(dc + 1) * 1024],
                    in_=io["h2fm"][b, dc * 128:(dc + 1) * 128, w0:w0 + 1024]),
                    reads=[("d_h2fm", b, dc, w0 // 512), ("d_h2fm", b, dc, w0 // 512 + 1)],
                    writes=[("h2w", dc)], dma=True)
            for nc32 in range(32):
                par = nc32 % 2
                src_g = io["L0_attn_Wgate"].rearrange("(c p) n -> p c n", p=128)
                src_u = io["L0_attn_Wup"].rearrange("(c p) n -> p c n", p=128)
                p.op("pool", lambda e, par=par, nc32=nc32, src_g=src_g: e.dma_start(
                    out=wg_t[par][:], in_=src_g[:, :, nc32 * 128:(nc32 + 1) * 128]),
                    writes=[("wgt", par)], dma=True)
                p.op("pool", lambda e, par=par, nc32=nc32, src_u=src_u: e.dma_start(
                    out=wu_t[par][:], in_=src_u[:, :, nc32 * 128:(nc32 + 1) * 128]),
                    writes=[("wut", par)], dma=True)
                for tb in range(2):
                    bg0 = ((nc32 * 2 + tb) % 2) * 2
                    for dc in range(8):
                        p.op("pe", lambda e, par=par, dc=dc, tb=tb, bg0=bg0: e.matmul(
                            PB[bg0][:, 0:512], wg_t[par][:, dc * 128:(dc + 1) * 128],
                            h2w[:, dc * 1024 + tb * 512: dc * 1024 + (tb + 1) * 512],
                            start=(dc == 0), stop=(dc == 7)),
                            reads=[("wgt", par), ("h2w", dc)], writes=[("pb", bg0)],
                            hoist=("a3GU", b, sb2, nc32, tb))
                        p.op("pe", lambda e, par=par, dc=dc, tb=tb, bg0=bg0: e.matmul(
                            PB[bg0 + 1][:, 0:512],
                            wu_t[par][:, dc * 128:(dc + 1) * 128],
                            h2w[:, dc * 1024 + tb * 512: dc * 1024 + (tb + 1) * 512],
                            start=(dc == 0), stop=(dc == 7)),
                            reads=[("wut", par), ("h2w", dc)],
                            writes=[("pb", bg0 + 1)],
                            hoist=("a3GU", b, sb2, nc32, tb))
                    p.op("act", lambda e, tb=tb, bg0=bg0: e.activation(
                        gs[tb][:], PB[bg0][:, 0:512], AF.Silu),
                        reads=[("pb", bg0)], writes=[("a3gs", tb)])
                    p.op("dve", lambda e, nc32=nc32, tb=tb, bg0=bg0: e.tensor_mul(
                        intfm[:, nc32 * 1024 + tb * 512: nc32 * 1024 + (tb + 1) * 512],
                        gs[tb][:], PB[bg0 + 1][:, 0:512]),
                        reads=[("a3gs", tb), ("pb", bg0 + 1)],
                        writes=[("int", nc32, tb)])
            for t8 in range(8):
                r0 = w0 + t8 * 128
                p.op("sp", lambda e, b=b, r0=r0: e.dma_start(
                    out=xmt[:], in_=io["xmid"][b, r0:r0 + 128, :]),
                    reads=[("d_xmid", b, r0 // 128)], writes=[("a3xm",)], dma=True)
                for dh in range(2):
                    bw = (4 if t8 % 2 == 0 else 2) + dh
                    for nc32 in range(32):
                        p.op("pe", lambda e, nc32=nc32, t8=t8, dh=dh, bw=bw: e.matmul(
                            PB[bw][:, 0:512],
                            intfm[:, nc32 * 1024 + t8 * 128: nc32 * 1024 + t8 * 128 + 128],
                            wd[:, nc32 * D + dh * 512: nc32 * D + (dh + 1) * 512],
                            start=(nc32 == 0), stop=(nc32 == 31)),
                            reads=[("int", nc32, t8 // 4), ("w", "wd")],
                            writes=[("pb", bw)],
                            hoist=("a3Wd", b, sb2, t8, dh))
                    p.op("dve", lambda e, dh=dh, bw=bw: e.tensor_add(
                        s0t[:, dh * 512:(dh + 1) * 512], PB[bw][:, 0:512],
                        xmt[:, dh * 512:(dh + 1) * 512]),
                        reads=[("pb", bw), ("a3xm",)], writes=[("a3s0", dh)])
                p.op("sp", lambda e, b=b, r0=r0: e.dma_start(
                    out=io["s0"][b, r0:r0 + 128, :], in_=s0t[:]),
                    reads=[("a3s0", 0), ("a3s0", 1)],
                    writes=[("d_s0", b, r0 // 128)], dma=True)
    p.barrier()



W_SHAPES_A = {
    "L0_attn_Wq": (D, KD), "L0_attn_Wk": (D, KD), "L0_attn_Wv": (D, VD),
    "L0_attn_Wb": (D, 8), "L0_attn_Wa": (D, 8),
    "L0_attn_A_log": (1, 8), "L0_attn_dt_bias": (1, 8),
    "L0_attn_conv_q_w": (KD, 4), "L0_attn_conv_k_w": (KD, 4),
    "L0_attn_conv_v_w": (VD, 4),
    "L0_attn_Wg": (D, VD), "L0_attn_onorm_w": (1, DV),
    "L0_attn_Wo": (VD, D), "L0_attn_norm1_w": (1, D), "L0_attn_norm2_w": (1, D),
    "L0_attn_Wgate": (D, IFF), "L0_attn_Wup": (D, IFF), "L0_attn_Wdown": (IFF, D),
}


def _mk_ctx(nc, es):
    p = Prog(nc)
    ctx = Ctx(nc, p, es)
    ctx.init_psum()
    ctx.init_sems()
    return ctx


def _finish(ctx, t, final_keys):
    ctx.p.op("dve", lambda e: e.memset(t["consts"][:, 7:8], 7.0),
             reads=final_keys, writes=[("fin",)])


def _emit_all(ctx, blk):
    ctx.p.emit(blk, ctx.sems)


def build_launch_A(phases=(0, 1, 2, 3), debug_out=()):
    from contextlib import ExitStack
    es = ExitStack()
    nc = bass.Bass("TRN2", target_bir_lowering=False)
    io = {}
    io["x"] = nc.dram_tensor("x", [B_PER_CORE, T, D], F32, kind="ExternalInput").ap()
    for nm, shp in W_SHAPES_A.items():
        io[nm] = nc.dram_tensor(nm, list(shp), F32, kind="ExternalInput").ap()
    io["s0"] = nc.dram_tensor("s0", [B_PER_CORE, T, D], BF16, kind="ExternalOutput").ap()
    io["xn0_tm"] = nc.dram_tensor("xn0_tm", [B_PER_CORE, T, D], F32,
                                  kind="ExternalOutput").ap()
    io["xn0_fm"] = nc.dram_tensor("xn0_fm", [B_PER_CORE, D, T], BF16,
                                  kind="ExternalOutput").ap()
    internals = {"h_fm": ([B_PER_CORE, D, T], BF16),
                 "o_st": ([B_PER_CORE, T, VD], BF16),
                 "xmid": ([B_PER_CORE, T, D], F32),
                 "h2fm": ([B_PER_CORE, D, T], BF16)}
    for nm, (shp, dt) in internals.items():
        kind = "ExternalOutput" if nm in debug_out else "Internal"
        io[nm] = nc.dram_tensor(nm, shp, dt, kind=kind).ap()
    if "dbg" in debug_out:
        io["dbg"] = nc.dram_tensor("dbg", [256, 8192], BF16, kind="ExternalOutput").ap()
        io["dbg2"] = nc.dram_tensor("dbg2", [128, 256], F32, kind="ExternalOutput").ap()
        io["dbg3"] = nc.dram_tensor("dbg3", [128, 8 * VD], BF16, kind="ExternalOutput").ap()
    ctx = _mk_ctx(nc, es)
    blk = es.enter_context(nc.Block())
    t = setup_common(ctx)
    for ph, fn in ((0, emit_A0), (1, emit_A1), (2, emit_A2), (3, emit_A3)):
        if ph in phases:
            ctx.push()
            fn(ctx, t, io)
            ctx.pop()
    fk = []
    for b in range(B_PER_CORE):
        if 3 in phases:
            fk += [("d_s0", b, i) for i in range(NT)]
        fk += [("d_xn0_tm", b, i) for i in range(NT)]
        fk += [("d_xn0_fm", b, j, g) for j in range(8) for g in range(4)]
        if 1 in phases:
            fk += [("d_ost", b, i) for i in range(NT)]
        if 2 in phases:
            fk += [("d_xmid", b, i) for i in range(NT)]
    _finish(ctx, t, fk)
    _emit_all(ctx, blk)
    consts = ctx.const_inputs_used()
    return nc, es, consts, ctx


def prep_inputs_A(inputs, xs):
    m = {"x": np.ascontiguousarray(xs)}
    for nm, shp in W_SHAPES_A.items():
        m[nm] = np.ascontiguousarray(inputs[nm]).reshape(shp)
    return m



W_SHAPES_B = {
    "L0_ffn_Wq": (D, D), "L0_ffn_bq": (1, D), "L0_ffn_Wk": (D, D),
    "L0_ffn_bk": (1, D), "L0_ffn_Wv": (D, D), "L0_ffn_bv": (1, D),
    "L0_ffn_Wo": (D, D), "L0_ffn_bo": (1, D),
    "L0_ffn_g1": (1, D), "L0_ffn_be1": (1, D), "L0_ffn_g2": (1, D),
    "L0_ffn_be2": (1, D),
    "L0_ffn_Wg": (D, 8), "L0_ffn_bg": (1, 8),
    "L0_ffn_We": (8, D, D), "L0_ffn_bexp": (1, 8 * D),
}


def _repl_row(ctx, t, rowap, dst, ncols, rkey, wkey, bank=0):
    p = ctx.p
    PB = ctx.psum
    nseg = (ncols + 511) // 512
    for seg in range(nseg):
        c0 = seg * 512
        cw = min(512, ncols - c0)
        p.op("pe", lambda e, c0=c0, cw=cw: e.matmul(
            PB[bank][:, 0:cw], t["ones_r_f"][0:1, 0:128], rowap[0:1, c0:c0 + cw],
            start=True, stop=True),
            reads=list(rkey) + [("const", "o2")], writes=[("pb", bank)])
        p.op("dve", lambda e, c0=c0, cw=cw: e.tensor_copy(
            dst[:, c0:c0 + cw], PB[bank][:, 0:cw]),
            reads=[("pb", bank)], writes=[wkey])


def emit_B(ctx, t, io):
    p = ctx.p
    PB, PT = ctx.psum, ctx.psb

    wgm = ctx.sb("b_wgm", (128, 8 * 8), F32)
    p.op("sp", lambda e: e.dma_start(
        out=wgm[:], in_=io["L0_ffn_Wg"].rearrange("(c p) n -> p c n", p=128)),
        writes=[("w", "wgm")], dma=True)
    berow = ctx.sb("b_berow", (1, 8 * D), BF16)
    p.op("pool", lambda e: e.dma_start(out=berow[:], in_=io["L0_ffn_bexp"]),
         writes=[("w", "berow")], dma=True)
    reps = {}
    for nm in ("bv", "bo", "g1", "be1", "g2", "be2"):
        reps[nm] = ctx.sb("b_rep" + nm, (128, D), F32)
    bgrep = ctx.sb("b_repbg", (128, 8), F32)
    bqc = ctx.sb("b_bqc", (64, 16), F32)
    bkc = ctx.sb("b_bkc", (64, 16), F32)
    ctx.push()
    rows = {}
    for nm in ("bq", "bk", "bv", "bo", "g1", "be1", "g2", "be2"):
        rows[nm] = ctx.sb("b_r" + nm, (1, D), F32)
        p.op("sp", lambda e, nm=nm: e.dma_start(out=rows[nm][:],
                                                in_=io["L0_ffn_" + nm]),
             writes=[("row", nm)], dma=True)
    bgrow = ctx.sb("b_bgrow", (1, 8), F32)
    p.op("sp", lambda e: e.dma_start(out=bgrow[:], in_=io["L0_ffn_bg"]),
         writes=[("row", "bg")], dma=True)
    for nm in ("bv", "bo", "g1", "be1", "g2", "be2"):
        _repl_row(ctx, t, rows[nm][:], reps[nm], D, [("row", nm)], ("rep", nm))
    _repl_row(ctx, t, bgrow[:], bgrep, 8, [("row", "bg")], ("rep", "bg"))
    for nm, dst in (("bq", bqc), ("bk", bkc)):
        for h in range(16):
            p.op("pe", lambda e, nm=nm, h=h: e.transpose(
                PB[0][0:64, 0:1], rows[nm][0:1, h * 64:(h + 1) * 64],
                t["ident_f"][0:1, 0:1]),
                reads=[("row", nm), ("const", "c_ident_f")], writes=[("pb", 0)])
            p.op("dve", lambda e, dst=dst, h=h: e.tensor_copy(
                dst[:, h:h + 1], PB[0][0:64, 0:1]),
                reads=[("pb", 0)], writes=[("bc", nm, h)])

    p.barrier()
    ctx.pop()
    vtm = ctx.sb("b_vtm", (128, NT * 16 * 65), BF16)
    ctxb = ctx.sb("b_ctxb", (128, NT * D), BF16)
    cmbb = ctx.sb("b_cmb", (128, NT * 8), F32)
    st = ctx.sb("b_st", (128, 24), F32)
    gl = ctx.sb("b_gl", (128, 8), F32)
    glx = ctx.sb("b_glx", (128, 8), F32)
    e1t = ctx.sb("b_e1", (128, 8), F32)
    e2t = ctx.sb("b_e2", (128, 8), F32)
    sq = ctx.sb("b_sq", (128, D), F32)
    at = ctx.sb("b_at", (128, D), F32)
    xm0 = ctx.sb("b_xm0", (128, D), F32)
    xmb = ctx.sb("b_xmb", (128, D), BF16)
    xnt = ctx.sb("b_xnt", (128, D), F32)

    for b in range(B_PER_CORE):
        ctx.push()
        wvf = ctx.sb("b_wvf", (128, 8 * D), BF16)
        xnw = ctx.sb("b_xnw", (128, 8 * 512), BF16)
        p.op("pool", lambda e: e.dma_start(
            out=wvf[:], in_=io["L0_ffn_Wv"].rearrange("(c p) n -> p c n", p=128)),
            writes=[("w", "wv", b)], dma=True)
        p.op("dve", lambda e: e.memset(
            vtm.rearrange("p (n h c) -> p n h c", h=16, c=65)[:, :, :, 64:65], 1.0),
            writes=[("vone",)])
        for grp in range(4):
            for dc in range(8):
                p.op("sp", lambda e, b=b, dc=dc, grp=grp: e.dma_start(
                    out=xnw[:, dc * 512:(dc + 1) * 512],
                    in_=io["xn0_fm"][b, dc * 128:(dc + 1) * 128,
                                     grp * 512:(grp + 1) * 512]),
                    writes=[("xnw", dc)], dma=True)
            for t4 in range(4):
                tt = grp * 4 + t4
                for half in range(2):
                    bk = half
                    for dc in range(8):
                        p.op("pe", lambda e, dc=dc, t4=t4, half=half, bk=bk: e.matmul(
                            PB[bk][:, 0:512],
                            xnw[:, dc * 512 + t4 * 128: dc * 512 + t4 * 128 + 128],
                            wvf[:, dc * D + half * 512: dc * D + (half + 1) * 512],
                            start=(dc == 0), stop=(dc == 7)),
                            reads=[("xnw", dc), ("w", "wv", b)],
                            writes=[("pb", bk)],
                            hoist=("bV", b, grp, t4, half))
                    dst = vtm.rearrange("p (n h c) -> p n h c", h=16, c=65)[
                        :, tt, half * 8:(half + 1) * 8, 0:64]
                    src = PB[bk].rearrange("p (h c) -> p h c", c=64)
                    brep = reps["bv"].rearrange("p (h c) -> p h c", c=64)[
                        :, half * 8:(half + 1) * 8, :]
                    p.op("dve", lambda e, dst=dst, src=src, brep=brep: e.tensor_add(
                        dst, src, brep),
                        reads=[("pb", bk), ("rep", "bv"), ("vone",)],
                        writes=[("vtm", b, tt, half)])
        p.barrier()
        ctx.pop()
        ctx.push()
        qfh = [ctx.sb("b_qfh", (64, T), BF16) for _ in range(2)]
        kfh = [ctx.sb("b_kfh", (64, T), BF16) for _ in range(2)]
        ebuf = ctx.sb("b_ebuf", (128, 16 * 512), BF16)
        rcpf = [ctx.sb("b_rcpf", (1, 512), F32) for _ in range(2)]
        rcpb = [ctx.sb("b_rcpb", (1, 512), BF16) for _ in range(2)]
        drs = [ctx.sb("b_drs", (64, 512), BF16) for _ in range(2)]
        xnr = ctx.sb("b_xnr", (128, 8 * T), BF16)
        wqs = [ctx.sb("b_wqs", (128, 8 * 64), BF16) for _ in range(2)]
        wks = [ctx.sb("b_wks", (128, 8 * 64), BF16) for _ in range(2)]
        for dc in range(8):
            p.op("sp", lambda e, b=b, dc=dc: e.dma_start(
                out=xnr[:, dc * T:(dc + 1) * T],
                in_=io["xn0_fm"][b, dc * 128:(dc + 1) * 128, :]),
                writes=[("xnr", dc)], dma=True)
        for h in range(16):
            hp = h % 2
            for which, wsbs, dstfs, bc, wnm in (
                    ("q", wqs, qfh, bqc, "L0_ffn_Wq"),
                    ("k", wks, kfh, bkc, "L0_ffn_Wk")):
                wsb = wsbs[hp]
                dstf = dstfs[hp]
                p.op("pool", lambda e, wsb=wsb, wnm=wnm, h=h: e.dma_start(
                    out=wsb[:],
                    in_=io[wnm].rearrange("(c p) n -> p c n", p=128)[
                        :, :, h * 64:(h + 1) * 64]),
                    writes=[("ws", which, hp)], dma=True)
                for grp in range(4):
                    bk = 2 + (grp % 2)
                    for dc in range(8):
                        p.op("pe", lambda e, wsb=wsb, dc=dc, grp=grp, bk=bk: e.matmul(
                            PB[bk][0:64, 0:512], wsb[:, dc * 64:(dc + 1) * 64],
                            xnr[:, dc * T + grp * 512: dc * T + (grp + 1) * 512],
                            start=(dc == 0), stop=(dc == 7)),
                            reads=[("xnr", dc), ("ws", which, hp)],
                            writes=[("pb", bk)],
                            hoist=("bQK", b, h, which, grp))
                    p.op("dve", lambda e, dstf=dstf, grp=grp, bc=bc, h=h, bk=bk:
                         e.tensor_scalar(
                             dstf[:, grp * 512:(grp + 1) * 512], PB[bk][0:64, 0:512],
                             bc[:, h:h + 1], None, ALU.add),
                         reads=[("pb", bk),
                                ("bc", "bq" if which == "q" else "bk", h)],
                         writes=[("qk", which, grp, hp)])
            for qb in range(4):
                for kt in range(16):
                    bk = 4 + (kt % 2)
                    p.op("pe", lambda e, kt=kt, qb=qb, bk=bk, hp=hp: e.matmul(
                        PB[bk][:, 0:512], kfh[hp][0:64, kt * 128:(kt + 1) * 128],
                        qfh[hp][0:64, qb * 512:(qb + 1) * 512], start=True,
                        stop=True),
                        reads=[("qk", "q", qb, hp), ("qk", "k", kt // 4, hp)],
                        writes=[("pb", bk)],
                        hoist=("bS", b, h, qb))
                    p.op("act", lambda e, kt=kt, bk=bk: e.activation(
                        ebuf[:, kt * 512:(kt + 1) * 512], PB[bk][:, 0:512],
                        AF.Exp, scale=0.125),
                        reads=[("pb", bk)], writes=[("eb", kt)])
                bk = qb % 2
                rb = 2 + (qb % 2)
                p2 = (h * 4 + qb) % 2
                for kt in range(16):
                    lhs = vtm[:, kt * (16 * 65) + h * 65:
                              kt * (16 * 65) + h * 65 + 65]
                    p.op("pe", lambda e, kt=kt, lhs=lhs, bk=bk: e.matmul(
                        PB[bk][0:65, 0:512], lhs,
                        ebuf[:, kt * 512:(kt + 1) * 512],
                        start=(kt == 0), stop=(kt == 15)),
                        reads=[("eb", kt)] +
                        [("vtm", b, kt, hf) for hf in range(2)] + [("vone",)],
                        writes=[("pb", bk)],
                        hoist=("bAV", b, h, qb))
                p.op("dve", lambda e, bk=bk, p2=p2: e.reciprocal(
                    rcpf[p2][:], PB[bk][64:65, 0:512]),
                    reads=[("pb", bk)], writes=[("rcf", p2)])
                p.op("act", lambda e, p2=p2: e.activation(
                    rcpb[p2][:], rcpf[p2][:], AF.Copy),
                    reads=[("rcf", p2)], writes=[("rcb", p2)])
                p.op("pe", lambda e, rb=rb, p2=p2: e.matmul(
                    PB[rb][0:64, 0:512], t["ones_r_bf"][0:1, 0:64],
                    rcpb[p2][:], start=True, stop=True),
                    reads=[("rcb", p2), ("const", "o1")], writes=[("pb", rb)])
                p.op("act", lambda e, rb=rb, p2=p2: e.activation(
                    drs[p2][:], PB[rb][0:64, 0:512], AF.Copy),
                    reads=[("pb", rb)], writes=[("drs", p2)])
                p.op("dve", lambda e, h=h, qb=qb, bk=bk, p2=p2: e.tensor_mul(
                    ctxb[64 * (h % 2):64 * (h % 2) + 64,
                         (h // 2) * T + qb * 512:(h // 2) * T + qb * 512 + 512],
                    PB[bk][0:64, 0:512], drs[p2][:]),
                    reads=[("pb", bk), ("drs", p2)],
                    writes=[("ctxT", h, qb)])
        p.barrier()
        ctx.pop()
        ctx.push()
        wof = ctx.sb("b_wof", (128, 8 * D), BF16)
        cfm = ctx.sb("b_cfm", (128, 8 * 128), BF16)
        xmffm = ctx.sb("b_xmffm", (128, 8 * 128), F32)
        p.op("pool", lambda e: e.dma_start(
            out=wof[:], in_=io["L0_ffn_Wo"].rearrange("(c p) n -> p c n", p=128)),
            writes=[("w", "wo", b)], dma=True)
        for tt in range(NT):
            r0 = tt * 128
            for half in range(2):
                for dc in range(8):
                    p.op("pe", lambda e, tt=tt, dc=dc, half=half: e.matmul(
                        PB[half][:, 0:512],
                        ctxb[:, dc * T + tt * 128: dc * T + tt * 128 + 128],
                        wof[:, dc * D + half * 512: dc * D + (half + 1) * 512],
                        start=(dc == 0), stop=(dc == 7)),
                        reads=[("ctxT", 2 * dc, tt // 4),
                               ("ctxT", 2 * dc + 1, tt // 4), ("w", "wo", b)],
                        writes=[("pb", half)],
                        hoist=("bWo", b, tt, half))
                p.op("dve", lambda e, half=half: e.tensor_add(
                    at[:, half * 512:(half + 1) * 512], PB[half][:, 0:512],
                    reps["bo"][:, half * 512:(half + 1) * 512]),
                    reads=[("pb", half), ("rep", "bo")], writes=[("at", half)])
            p.op("dve", lambda e: e.reduce_sum(st[:, 0:1], at[:], AX.X),
                 reads=[("at", 0), ("at", 1)], writes=[("st", 0)])
            p.op("act", lambda e: e.activation(sq[:], at[:], AF.Square,
                                               accum_out=st[:, 1:2]),
                 reads=[("at", 0), ("at", 1)], writes=[("st", 1), ("bsq",)])
            p.op("dve", lambda e: e.tensor_scalar(
                st[:, 2:3], st[:, 0:1], -1.0 / D, None, ALU.mult),
                reads=[("st", 0)], writes=[("st", 2)])
            p.op("dve", lambda e: e.tensor_scalar(
                st[:, 3:4], st[:, 1:2], 1.0 / D, None, ALU.mult),
                reads=[("st", 1)], writes=[("st", 3)])
            p.op("dve", lambda e: e.scalar_tensor_tensor(
                st[:, 4:5], st[:, 2:3], 1.0, st[:, 2:3], ALU.mult, ALU.mult),
                reads=[("st", 2)], writes=[("st", 4)])
            p.op("dve", lambda e: e.tensor_sub(st[:, 5:6], st[:, 3:4], st[:, 4:5]),
                 reads=[("st", 3), ("st", 4)], writes=[("st", 5)])
            p.op("act", lambda e: e.activation(st[:, 6:7], st[:, 5:6], AF.Ln,
                                               bias=t["eps5"], scale=1.0),
                 reads=[("st", 5), ("const", "c1")], writes=[("st", 6)])
            p.op("act", lambda e: e.activation(st[:, 7:8], st[:, 6:7], AF.Exp,
                                               scale=-0.5),
                 reads=[("st", 6)], writes=[("st", 7)])
            p.op("dve", lambda e: e.tensor_scalar(
                at[:], at[:], st[:, 2:3], st[:, 7:8], ALU.add, ALU.mult),
                reads=[("at", 0), ("at", 1), ("st", 2), ("st", 7)],
                writes=[("atn",)])
            p.op("dve", lambda e: e.tensor_mul(at[:], at[:], reps["g1"][:]),
                 reads=[("atn",), ("rep", "g1")], writes=[("atn2",)])
            p.op("dve", lambda e: e.tensor_add(at[:], at[:], reps["be1"][:]),
                 reads=[("atn2",), ("rep", "be1")], writes=[("atn3",)])
            p.op("sp", lambda e, b=b, r0=r0: e.dma_start(
                out=xnt[:], in_=io["xn0_tm"][b, r0:r0 + 128, :]),
                writes=[("bxn",)], dma=True)
            p.op("dve", lambda e: e.tensor_add(xm0[:], at[:], xnt[:]),
                 reads=[("atn3",), ("bxn",)], writes=[("xm0",)])
            p.op("dve", lambda e: e.tensor_copy(xmb[:], xm0[:]),
                 reads=[("xm0",)], writes=[("xmb",)])
            for j in range(8):
                p.op("pe", lambda e, j=j: e.transpose(
                    PT[1][:, 0:128], xmb[:, j * 128:(j + 1) * 128], t["ident_bf"]),
                    reads=[("xmb",), ("const", "c_ident_bf")], writes=[("pt", 1)])
                p.op("dve", lambda e, j=j: e.tensor_copy(
                    cfm[:, j * 128:(j + 1) * 128], PT[1][:, 0:128]),
                    reads=[("pt", 1)], writes=[("cfm", j)])
                p.op("sp", lambda e, b=b, j=j, tt=tt: e.dma_start(
                    out=io["xm0fm"][b, j * 128:(j + 1) * 128,
                                    tt * 128:(tt + 1) * 128],
                    in_=cfm[:, j * 128:(j + 1) * 128]),
                    reads=[("cfm", j)], writes=[("d_xm0fm", b, j, tt)], dma=True)
                p.op("pe", lambda e, j=j: e.transpose(
                    PB[2][:, 0:128], xm0[:, j * 128:(j + 1) * 128], t["ident_f"]),
                    reads=[("xm0",), ("const", "c_ident_f")], writes=[("pb", 2)])
                p.op("act", lambda e, j=j: e.activation(
                    xmffm[:, j * 128:(j + 1) * 128], PB[2][:, 0:128], AF.Copy),
                    reads=[("pb", 2)], writes=[("xmffm", j)])
            for dc in range(8):
                p.op("pe", lambda e, dc=dc: e.matmul(
                    PB[3][:, 0:8], xmffm[:, dc * 128:(dc + 1) * 128],
                    wgm[:, dc * 8:(dc + 1) * 8], start=(dc == 0), stop=(dc == 7)),
                    reads=[("xmffm", dc), ("w", "wgm")], writes=[("pb", 3)],
                    hoist=("bGl", b, tt))
            p.op("dve", lambda e: e.tensor_add(gl[:], PB[3][:, 0:8], bgrep[:]),
                 reads=[("pb", 3), ("rep", "bg")], writes=[("gl",)])
            p.op("dve", lambda e: e.reduce_max(st[:, 8:9], gl[:], AX.X),
                 reads=[("gl",)], writes=[("st", 8)])
            p.op("dve", lambda e: e.tensor_scalar(
                st[:, 9:10], st[:, 8:9], -1.0, None, ALU.mult),
                reads=[("st", 8)], writes=[("st", 9)])
            p.op("act", lambda e: e.activation(glx[:], gl[:], AF.Exp,
                                               bias=st[:, 9:10],
                                               accum_out=st[:, 10:11]),
                 reads=[("gl",), ("st", 9)], writes=[("glx",), ("st", 10)])
            p.op("dve", lambda e: e.reciprocal(st[:, 11:12], st[:, 10:11]),
                 reads=[("st", 10)], writes=[("st", 11)])
            p.op("dve", lambda e: e.tensor_scalar(
                glx[:], glx[:], st[:, 11:12], None, ALU.mult),
                reads=[("glx",), ("st", 11)], writes=[("probs",)])
            p.op("dve", lambda e: e.reduce_max(st[:, 12:13], glx[:], AX.X),
                 reads=[("probs",)], writes=[("st", 12)])
            p.op("dve", lambda e: e.tensor_scalar(
                e1t[:], glx[:], st[:, 12:13], None, ALU.is_ge),
                reads=[("probs",), ("st", 12)], writes=[("e1",)])
            p.op("dve", lambda e: e.scalar_tensor_tensor(
                e2t[:], e1t[:], -1e9, glx[:], ALU.mult, ALU.add),
                reads=[("e1",), ("probs",)], writes=[("glm",)])
            p.op("dve", lambda e: e.reduce_max(st[:, 13:14], e2t[:], AX.X),
                 reads=[("glm",)], writes=[("st", 13)])
            p.op("dve", lambda e: e.tensor_scalar(
                e2t[:], e2t[:], st[:, 13:14], None, ALU.is_ge),
                reads=[("glm",), ("st", 13)], writes=[("e2",)])
            p.op("dve", lambda e: e.tensor_scalar(
                e1t[:], e1t[:], st[:, 12:13], None, ALU.mult),
                reads=[("e1",), ("st", 12)], writes=[("e1v",)])
            p.op("dve", lambda e, tt=tt: e.scalar_tensor_tensor(
                cmbb[:, tt * 8:(tt + 1) * 8], e2t[:], st[:, 13:14], e1t[:],
                ALU.mult, ALU.add),
                reads=[("e2",), ("st", 13), ("e1v",)], writes=[("cmb", tt)])
            p.op("sp", lambda e, b=b, r0=r0: e.dma_start(
                out=io["xmid0"][b, r0:r0 + 128, :], in_=xm0[:]),
                reads=[("xm0",)], writes=[("d_xmid0", b, tt)], dma=True)
        p.barrier()
        ctx.pop()
        ctx.push()
        xmbfm = ctx.sb("b_xmbfm", (128, 8 * 1024), BF16)
        accb = ctx.sb("b_acc", (128, 8 * D), BF16)
        wet = [ctx.sb("b_wet", (128, 8 * D), BF16) for _ in range(2)]
        for hp in range(2):
            t0h = hp * 8
            for dc in range(8):
                p.op("sp", lambda e, b=b, dc=dc, hp=hp: e.dma_start(
                    out=xmbfm[:, dc * 1024:(dc + 1) * 1024],
                    in_=io["xm0fm"][b, dc * 128:(dc + 1) * 128,
                                    hp * 1024:(hp + 1) * 1024]),
                    reads=[("d_xm0fm", b, dc, t0h + i) for i in range(8)],
                    writes=[("xmw", dc)], dma=True)
            p.op("dve", lambda e: e.memset(accb[:], 0.0),
                 writes=[("acc", i, hf) for i in range(8) for hf in range(2)])
            if hp == 0:
                p.op("pool", lambda e: e.dma_start(
                    out=wet[0][:],
                    in_=io["L0_ffn_We"][0].rearrange("(c p) n -> p c n", p=128)),
                    writes=[("wet", 0)], dma=True)
            for ee in range(8):
                par = ee % 2
                if ee + 1 < 8:
                    p.op("pool", lambda e, ee=ee: e.dma_start(
                        out=wet[(ee + 1) % 2][:],
                        in_=io["L0_ffn_We"][ee + 1].rearrange(
                            "(c p) n -> p c n", p=128)),
                        writes=[("wet", (ee + 1) % 2)], dma=True)
                elif hp == 0:
                    p.op("pool", lambda e: e.dma_start(
                        out=wet[1 if False else (8 % 2)][:],
                        in_=io["L0_ffn_We"][0].rearrange("(c p) n -> p c n", p=128)),
                        writes=[("wet", 0)], dma=True)
                for ti in range(8):
                    tt = t0h + ti
                    for half in range(2):
                        bk = (ti % 2) * 2 + half
                        for dc in range(8):
                            p.op("pe", lambda e, ti=ti, dc=dc, half=half, bk=bk,
                                 par=par: e.matmul(
                                     PB[bk][:, 0:512],
                                     xmbfm[:, dc * 1024 + ti * 128:
                                           dc * 1024 + ti * 128 + 128],
                                     wet[par][:, dc * D + half * 512:
                                              dc * D + (half + 1) * 512],
                                     start=(dc == 0), stop=False),
                                 reads=[("xmw", dc), ("wet", par)],
                                 writes=[("pb", bk)],
                                 hoist=("bMoE", b, hp, ee, ti, half))
                        p.op("pe", lambda e, ee=ee, half=half, bk=bk: e.matmul(
                            PB[bk][:, 0:512], t["ones_r_bf"][0:1, 0:128],
                            berow[0:1, ee * D + half * 512:
                                  ee * D + (half + 1) * 512],
                            start=False, stop=True),
                            reads=[("w", "berow"), ("const", "o1")],
                            writes=[("pb", bk)],
                            hoist=("bMoE", b, hp, ee, ti, half))
                        p.op("dve", lambda e, ti=ti, tt=tt, half=half, bk=bk, ee=ee:
                             e.scalar_tensor_tensor(
                                 accb[:, ti * D + half * 512:
                                      ti * D + (half + 1) * 512],
                                 PB[bk][:, 0:512],
                                 cmbb[:, tt * 8 + ee: tt * 8 + ee + 1],
                                 accb[:, ti * D + half * 512:
                                      ti * D + (half + 1) * 512],
                                 ALU.mult, ALU.add),
                             reads=[("pb", bk), ("cmb", tt), ("acc", ti, half)],
                             writes=[("acc", ti, half)])
            for ti in range(8):
                tt = t0h + ti
                r0 = tt * 128
                accs = accb[:, ti * D:(ti + 1) * D]
                p.op("dve", lambda e, accs=accs: e.reduce_sum(st[:, 0:1], accs, AX.X),
                     reads=[("acc", ti, 0), ("acc", ti, 1)], writes=[("st", 0)])
                p.op("act", lambda e, accs=accs: e.activation(
                    sq[:], accs, AF.Square, accum_out=st[:, 1:2]),
                    reads=[("acc", ti, 0), ("acc", ti, 1)],
                    writes=[("st", 1), ("bsq",)])
                p.op("dve", lambda e: e.tensor_scalar(
                    st[:, 2:3], st[:, 0:1], -1.0 / D, None, ALU.mult),
                    reads=[("st", 0)], writes=[("st", 2)])
                p.op("dve", lambda e: e.tensor_scalar(
                    st[:, 3:4], st[:, 1:2], 1.0 / D, None, ALU.mult),
                    reads=[("st", 1)], writes=[("st", 3)])
                p.op("dve", lambda e: e.scalar_tensor_tensor(
                    st[:, 4:5], st[:, 2:3], 1.0, st[:, 2:3], ALU.mult, ALU.mult),
                    reads=[("st", 2)], writes=[("st", 4)])
                p.op("dve", lambda e: e.tensor_sub(st[:, 5:6], st[:, 3:4], st[:, 4:5]),
                     reads=[("st", 3), ("st", 4)], writes=[("st", 5)])
                p.op("act", lambda e: e.activation(st[:, 6:7], st[:, 5:6], AF.Ln,
                                                   bias=t["eps5"], scale=1.0),
                     reads=[("st", 5), ("const", "c1")], writes=[("st", 6)])
                p.op("act", lambda e: e.activation(st[:, 7:8], st[:, 6:7], AF.Exp,
                                                   scale=-0.5),
                     reads=[("st", 6)], writes=[("st", 7)])
                p.op("dve", lambda e, accs=accs: e.tensor_scalar(
                    at[:], accs, st[:, 2:3], st[:, 7:8], ALU.add, ALU.mult),
                    reads=[("acc", ti, 0), ("acc", ti, 1), ("st", 2), ("st", 7)],
                    writes=[("atn",)])
                p.op("dve", lambda e: e.tensor_mul(at[:], at[:], reps["g2"][:]),
                     reads=[("atn",), ("rep", "g2")], writes=[("atn2",)])
                p.op("dve", lambda e: e.tensor_add(at[:], at[:], reps["be2"][:]),
                     reads=[("atn2",), ("rep", "be2")], writes=[("atn3",)])
                p.op("sp", lambda e, b=b, r0=r0: e.dma_start(
                    out=xm0[:], in_=io["xmid0"][b, r0:r0 + 128, :]),
                    reads=[("d_xmid0", b, tt)], writes=[("xm0",)], dma=True)
                p.op("dve", lambda e: e.tensor_add(at[:], at[:], xm0[:]),
                     reads=[("atn3",), ("xm0",)], writes=[("f0",)])
                p.op("act", lambda e: e.activation(at[:], at[:], AF.Silu),
                     reads=[("f0",)], writes=[("g0t",)])
                p.op("dve", lambda e: e.tensor_copy(xmb[:], at[:]),
                     reads=[("g0t",)], writes=[("g0b",)])
                p.op("sp", lambda e, b=b, r0=r0: e.dma_start(
                    out=io["g0"][b, r0:r0 + 128, :], in_=xmb[:]),
                    reads=[("g0b",)], writes=[("d_g0", b, tt)], dma=True)
        p.barrier()
        ctx.pop()
    p.barrier()


def build_launch_B():
    from contextlib import ExitStack
    es = ExitStack()
    nc = bass.Bass("TRN2", target_bir_lowering=False)
    io = {}
    io["xn0_tm"] = nc.dram_tensor("xn0_tm", [B_PER_CORE, T, D], F32,
                                  kind="ExternalInput").ap()
    io["xn0_fm"] = nc.dram_tensor("xn0_fm", [B_PER_CORE, D, T], BF16,
                                  kind="ExternalInput").ap()
    for nm, shp in W_SHAPES_B.items():
        io[nm] = nc.dram_tensor(nm, list(shp), F32, kind="ExternalInput").ap()
    io["g0"] = nc.dram_tensor("g0", [B_PER_CORE, T, D], BF16,
                              kind="ExternalOutput").ap()
    io["xmid0"] = nc.dram_tensor("xmid0", [B_PER_CORE, T, D], F32,
                                 kind="Internal").ap()
    io["xm0fm"] = nc.dram_tensor("xm0fm", [B_PER_CORE, D, T], BF16,
                                 kind="Internal").ap()
    ctx = _mk_ctx(nc, es)
    blk = es.enter_context(nc.Block())
    t = setup_common(ctx)
    ctx.push()
    emit_B(ctx, t, io)
    ctx.pop()
    fk = []
    for b in range(B_PER_CORE):
        fk += [("d_g0", b, i) for i in range(NT)]
    _finish(ctx, t, fk)
    _emit_all(ctx, blk)
    return nc, es, ctx.const_inputs_used(), ctx


def prep_inputs_B(inputs, xn0_tm, xn0_fm):
    m = {"xn0_tm": xn0_tm, "xn0_fm": xn0_fm}
    for nm, shp in W_SHAPES_B.items():
        m[nm] = np.ascontiguousarray(inputs[nm]).reshape(shp)
    return m



def emit_C(ctx, t, io):
    p = ctx.p
    PB, PT = ctx.psum, ctx.psb
    wg1 = ctx.sb("c_wg1", (128, 8 * 8), F32)
    p.op("sp", lambda e: e.dma_start(
        out=wg1[:], in_=io["L1_ffn_gate_w"].rearrange("(c p) n -> p c n", p=128)),
        writes=[("w", "wg1")], dma=True)
    xt = ctx.sb("c_x", (128, D), F32)
    s0t = ctx.sb("c_s0", (128, D), BF16)
    g0t = ctx.sb("c_g0", (128, D), BF16)
    x1t = ctx.sb("c_x1", (128, D), F32)
    xnt = ctx.sb("c_xn", (128, D), F32)
    xnb = ctx.sb("c_xnb", (128, D), BF16)
    xnffm = ctx.sb("c_xnffm", (128, 8 * 128), F32)
    sgt = ctx.sb("c_sg", (128, D), F32)
    st = ctx.sb("c_st", (128, 16), F32)
    glt = ctx.sb("c_gl", (128, 8), F32)
    stage = ctx.sb("c_stage", (128, 8 * 512), BF16)

    for b in range(B_PER_CORE):
        for grp in range(4):
            for t4 in range(4):
                tt = grp * 4 + t4
                r0 = tt * 128
                p.op("sp", lambda e, b=b, r0=r0: e.dma_start(
                    out=xt[:], in_=io["x"][b, r0:r0 + 128, :]),
                    writes=[("cx",)], dma=True)
                p.op("sp", lambda e, b=b, r0=r0: e.dma_start(
                    out=s0t[:], in_=io["s0"][b, r0:r0 + 128, :]),
                    writes=[("cs0",)], dma=True)
                p.op("sp", lambda e, b=b, r0=r0: e.dma_start(
                    out=g0t[:], in_=io["g0"][b, r0:r0 + 128, :]),
                    writes=[("cg0",)], dma=True)
                p.op("dve", lambda e: e.tensor_mul(sgt[:], s0t[:], g0t[:]),
                     reads=[("cs0",), ("cg0",)], writes=[("csg",)])
                p.op("dve", lambda e: e.tensor_add(x1t[:], xt[:], sgt[:]),
                     reads=[("cx",), ("csg",)], writes=[("cx1",)])
                p.op("sp", lambda e, b=b, r0=r0: e.dma_start(
                    out=io["x1_tm"][b, r0:r0 + 128, :], in_=x1t[:]),
                    reads=[("cx1",)], writes=[("d_x1", b, tt)], dma=True)
                p.op("dve", lambda e: e.reduce_sum(st[:, 0:1], x1t[:], AX.X),
                     reads=[("cx1",)], writes=[("st", 0)])
                p.op("act", lambda e: e.activation(sgt[:], x1t[:], AF.Square,
                                                   accum_out=st[:, 1:2]),
                     reads=[("cx1",)], writes=[("st", 1), ("csq",)])
                p.op("dve", lambda e: e.tensor_scalar(
                    st[:, 2:3], st[:, 0:1], -1.0 / D, None, ALU.mult),
                    reads=[("st", 0)], writes=[("st", 2)])
                p.op("dve", lambda e: e.tensor_scalar(
                    st[:, 3:4], st[:, 1:2], 1.0 / D, None, ALU.mult),
                    reads=[("st", 1)], writes=[("st", 3)])
                p.op("dve", lambda e: e.scalar_tensor_tensor(
                    st[:, 4:5], st[:, 2:3], 1.0, st[:, 2:3], ALU.mult, ALU.mult),
                    reads=[("st", 2)], writes=[("st", 4)])
                p.op("dve", lambda e: e.tensor_sub(st[:, 5:6], st[:, 3:4],
                                                   st[:, 4:5]),
                     reads=[("st", 3), ("st", 4)], writes=[("st", 5)])
                p.op("act", lambda e: e.activation(st[:, 6:7], st[:, 5:6], AF.Ln,
                                                   bias=t["eps5"], scale=1.0),
                     reads=[("st", 5), ("const", "c1")], writes=[("st", 6)])
                p.op("act", lambda e: e.activation(st[:, 7:8], st[:, 6:7], AF.Exp,
                                                   scale=-0.5),
                     reads=[("st", 6)], writes=[("st", 7)])
                p.op("dve", lambda e: e.tensor_scalar(
                    xnt[:], x1t[:], st[:, 2:3], st[:, 7:8], ALU.add, ALU.mult),
                    reads=[("cx1",), ("st", 2), ("st", 7)], writes=[("cxn",)])
                p.op("dve", lambda e: e.tensor_copy(xnb[:], xnt[:]),
                     reads=[("cxn",)], writes=[("cxnb",)])
                for j in range(8):
                    c0 = j * 512 + t4 * 128
                    p.op("pe", lambda e, j=j: e.transpose(
                        PT[0][:, 0:128], xnb[:, j * 128:(j + 1) * 128],
                        t["ident_bf"]),
                        reads=[("cxnb",), ("const", "c_ident_bf")],
                        writes=[("pt", 0)])
                    p.op("dve", lambda e, c0=c0: e.tensor_copy(
                        stage[:, c0:c0 + 128], PT[0][:, 0:128]),
                        reads=[("pt", 0)], writes=[("cstg", j, t4)])
                    p.op("pe", lambda e, j=j: e.transpose(
                        PB[2][:, 0:128], xnt[:, j * 128:(j + 1) * 128],
                        t["ident_f"]),
                        reads=[("cxn",), ("const", "c_ident_f")],
                        writes=[("pb", 2)])
                    p.op("act", lambda e, j=j: e.activation(
                        xnffm[:, j * 128:(j + 1) * 128], PB[2][:, 0:128], AF.Copy),
                        reads=[("pb", 2)], writes=[("cxnffm", j)])
                for dc in range(8):
                    p.op("pe", lambda e, dc=dc: e.matmul(
                        PB[3][:, 0:8], xnffm[:, dc * 128:(dc + 1) * 128],
                        wg1[:, dc * 8:(dc + 1) * 8], start=(dc == 0), stop=(dc == 7)),
                        reads=[("cxnffm", dc), ("w", "wg1")], writes=[("pb", 3)],
                        hoist=("cGl", b, tt))
                p.op("dve", lambda e: e.tensor_copy(glt[:], PB[3][:, 0:8]),
                     reads=[("pb", 3)], writes=[("cgl",)])
                p.op("sp", lambda e, b=b, r0=r0: e.dma_start(
                    out=io["glog"][b, r0:r0 + 128, :], in_=glt[:]),
                    reads=[("cgl",)], writes=[("d_glog", b, tt)], dma=True)
            for j in range(8):
                cc = grp * 512
                p.op("sp", lambda e, b=b, j=j, cc=cc: e.dma_start(
                    out=io["xn1_fm"][b, j * 128:(j + 1) * 128, cc:cc + 512],
                    in_=stage[:, j * 512:(j + 1) * 512]),
                    reads=[("cstg", j, 0), ("cstg", j, 1), ("cstg", j, 2),
                           ("cstg", j, 3)],
                    writes=[("d_xn1fm", b, j, grp)], dma=True)
    p.barrier()
    ctx.push()
    wq1 = ctx.sb("c_wq1", (128, 8 * D), BF16)
    wk1 = ctx.sb("c_wk1", (128, 8 * D), BF16)
    wv1 = ctx.sb("c_wv1", (128, 8 * D), BF16)
    for nm, sb in (("L1_attn_Wq", wq1), ("L1_attn_Wk", wk1), ("L1_attn_Wv", wv1)):
        p.op("pool", lambda e, sb=sb, nm=nm: e.dma_start(
            out=sb[:], in_=io[nm].rearrange("(c p) n -> p c n", p=128)),
            writes=[("w", nm)], dma=True)
    xnw = ctx.sb("c_xnw", (128, 8 * 512), BF16)
    for b in range(B_PER_CORE):
        for grp in range(4):
            for dc in range(8):
                p.op("sp", lambda e, b=b, dc=dc, grp=grp: e.dma_start(
                    out=xnw[:, dc * 512:(dc + 1) * 512],
                    in_=io["xn1_fm"][b, dc * 128:(dc + 1) * 128,
                                     grp * 512:(grp + 1) * 512]),
                    reads=[("d_xn1fm", b, dc, grp)], writes=[("cxnw", dc)], dma=True)
            for nm, wsb, onm in (("L1_attn_Wq", wq1, "q1fm"), ("L1_attn_Wk", wk1, "k1fm")):
                for fc in range(8):
                    for dc in range(8):
                        p.op("pe", lambda e, wsb=wsb, fc=fc, dc=dc: e.matmul(
                            PB[fc % 2][:, 0:512],
                            wsb[:, dc * D + fc * 128: dc * D + fc * 128 + 128],
                            xnw[:, dc * 512:(dc + 1) * 512],
                            start=(dc == 0), stop=(dc == 7)),
                            reads=[("cxnw", dc), ("w", nm)],
                            writes=[("pb", fc % 2)],
                            hoist=("cQK", b, grp, nm, fc))
                    p.op("dve" if fc % 2 == 0 else "act",
                         (lambda e, fc=fc: e.tensor_copy(
                             stage[:, 0:512], PB[fc % 2][:, 0:512]))
                         if fc % 2 == 0 else
                         (lambda e, fc=fc: e.activation(
                             stage[:, 0:512], PB[fc % 2][:, 0:512], AF.Copy)),
                         reads=[("pb", fc % 2)], writes=[("cqkst",)])
                    p.op("sp", lambda e, b=b, fc=fc, grp=grp, onm=onm: e.dma_start(
                        out=io[onm][b, fc * 128:(fc + 1) * 128,
                                    grp * 512:(grp + 1) * 512],
                        in_=stage[:, 0:512]),
                        reads=[("cqkst",)], writes=[("d_" + onm, b, fc, grp)],
                        dma=True)
            for t4 in range(4):
                tt = grp * 4 + t4
                for half in range(2):
                    bk = 2 + half
                    for dc in range(8):
                        p.op("pe", lambda e, dc=dc, t4=t4, half=half, bk=bk: e.matmul(
                            PB[bk][:, 0:512],
                            xnw[:, dc * 512 + t4 * 128: dc * 512 + t4 * 128 + 128],
                            wv1[:, dc * D + half * 512: dc * D + (half + 1) * 512],
                            start=(dc == 0), stop=(dc == 7)),
                            reads=[("cxnw", dc), ("w", "L1_attn_Wv")],
                            writes=[("pb", bk)],
                            hoist=("cV", b, grp, t4, half))
                    p.op("dve", lambda e, half=half, bk=bk: e.tensor_copy(
                        stage[:, 1024 + half * 512:1024 + (half + 1) * 512],
                        PB[bk][:, 0:512]),
                        reads=[("pb", bk)], writes=[("cvst", half)])
                p.op("sp", lambda e, b=b, tt=tt: e.dma_start(
                    out=io["v1tm"][b, tt * 128:(tt + 1) * 128, :],
                    in_=stage[:, 1024:2048]),
                    reads=[("cvst", 0), ("cvst", 1)],
                    writes=[("d_v1tm", b, tt)], dma=True)
    p.barrier()
    ctx.pop()



def emit_D(ctx, t, io):
    p = ctx.p
    PB, PT = ctx.psum, ctx.psb
    gt_em = ctx.sb("d_gtem", (8, T), F32)
    grepb = ctx.sb("d_grepb", (128, T), BF16)
    grow = ctx.sb("d_grow", (1, T), F32)
    grep = ctx.sb("d_grep", (128, T), F32)
    gates_tm = ctx.sb("d_gtm", (128, NT * 8), F32)
    ranks = ctx.sb("d_ranks", (128, NT * 8), F32)
    selb = ctx.sb("d_sel", (128, NT * 8), F32)
    rdump = ctx.sb("d_rdump", (128, T), BF16)
    rdump2 = ctx.sb("d_rdump2", (128, T), BF16)
    glt = ctx.sb("d_gl", (128, 8), F32)
    cst = ctx.sb("d_cst", (8, 4), F32)
    rl = ctx.sb("d_rl", (1, T), F32)
    st = ctx.sb("d_st", (128, 16), F32)
    erep = ctx.sb("d_erep", (128, 8), F32)
    pick = ctx.sb("d_pick", (128, 8), F32)
    keyt = ctx.sb("d_key", (128, 8), F32)
    keym = ctx.sb("d_keym", (128, 8), F32)
    isw = ctx.sb("d_isw", (128, 8), F32)
    ov5 = ctx.sb("d_ov5", (128, 8), F32)
    for ee in range(8):
        p.op("dve", lambda e, ee=ee: e.memset(erep[:, ee:ee + 1], float(ee)),
             writes=[("erep",)])
    for b in range(B_PER_CORE):
        for tt in range(NT):
            r0 = tt * 128
            p.op("sp", lambda e, b=b, r0=r0: e.dma_start(
                out=glt[:], in_=io["glog"][b, r0:r0 + 128, :]),
                writes=[("dgl",)], dma=True)
            p.op("dve", lambda e: e.tensor_scalar(glt[:], glt[:], 1e-6, None,
                                                  ALU.max),
                 reads=[("dgl",)], writes=[("dglc",)])
            p.op("act", lambda e: e.activation(glt[:], glt[:], AF.Ln),
                 reads=[("dglc",)], writes=[("dgln",)])
            p.op("pe", lambda e: e.transpose(PB[2][0:8, 0:128], glt[:],
                                             t["ident_f"]),
                 reads=[("dgln",), ("const", "c_ident_f")], writes=[("pb", 2)])
            p.op("dve", lambda e, tt=tt: e.tensor_copy(
                gt_em[:, tt * 128:(tt + 1) * 128], PB[2][0:8, 0:128]),
                reads=[("pb", 2)], writes=[("gtem", tt)])
        rall = [("gtem", i) for i in range(NT)]
        for it in range(8):
            p.op("act", lambda e: e.activation(gt_em[:], gt_em[:], AF.Exp),
                 reads=rall, writes=rall)
            p.op("dve", lambda e: e.reduce_sum(cst[:, 0:1], gt_em[0:8, :], AX.X),
                 reads=rall, writes=[("cs0",)])
            p.op("dve", lambda e: e.reciprocal(cst[:, 2:3], cst[:, 0:1]),
                 reads=[("cs0",)], writes=[("cs2",)])
            p.op("dve", lambda e: e.tensor_scalar(
                gt_em[:], gt_em[:], cst[:, 2:3], None, ALU.mult),
                reads=rall + [("cs2",)], writes=rall)
            for blkq in range(4):
                p.op("pe", lambda e, blkq=blkq: e.matmul(
                    PB[2][0:1, 0:512], t["ones_c_f"][0:8, :],
                    gt_em[0:8, blkq * 512:(blkq + 1) * 512],
                    start=True, stop=True),
                    reads=rall + [("const", "o4")], writes=[("pb", 2)])
                p.op("act", lambda e, blkq=blkq: e.activation(
                    rl[0:1, blkq * 512:(blkq + 1) * 512], PB[2][0:1, 0:512],
                    AF.Ln),
                    reads=[("pb", 2)], writes=[("rl", blkq)])
            p.op("act", lambda e: e.activation(gt_em[:], gt_em[:], AF.Ln),
                 reads=rall, writes=rall)
            for blkq in range(4):
                p.op("pe", lambda e, blkq=blkq: e.matmul(
                    PB[2][0:8, 0:512], t["ones_r_f"][0:1, 0:8],
                    rl[0:1, blkq * 512:(blkq + 1) * 512], start=True, stop=True),
                    reads=[("rl", blkq), ("const", "o2")], writes=[("pb", 2)])
                p.op("dve", lambda e, blkq=blkq: e.tensor_sub(
                    gt_em[0:8, blkq * 512:(blkq + 1) * 512],
                    gt_em[0:8, blkq * 512:(blkq + 1) * 512], PB[2][0:8, 0:512]),
                    reads=rall + [("pb", 2)], writes=rall)
        p.op("act", lambda e: e.activation(gt_em[:], gt_em[:], AF.Exp),
             reads=rall, writes=rall)
        for tt in range(NT):
            p.op("pe", lambda e, tt=tt: e.transpose(
                PB[2][:, 0:8], gt_em[0:8, tt * 128:(tt + 1) * 128],
                t["ident_f"][0:8, 0:8]),
                reads=rall + [("const", "c_ident_f")], writes=[("pb", 2)])
            p.op("dve", lambda e, tt=tt: e.tensor_copy(
                gates_tm[:, tt * 8:(tt + 1) * 8], PB[2][:, 0:8]),
                reads=[("pb", 2)], writes=[("gtm", tt)])
        for ee in range(8):
            p.op("sp", lambda e, ee=ee: e.dma_start(
                out=grow[:], in_=gt_em[ee:ee + 1, :]),
                reads=rall, writes=[("grow",)], dma=True)
            for blkq in range(4):
                p.op("pe", lambda e, blkq=blkq: e.matmul(
                    PB[3][:, 0:512], t["ones_r_f"][0:1, 0:128],
                    grow[0:1, blkq * 512:(blkq + 1) * 512], start=True, stop=True),
                    reads=[("grow",), ("const", "o2")], writes=[("pb", 3)])
                p.op("dve", lambda e, blkq=blkq: e.tensor_copy(
                    grepb[:, blkq * 512:(blkq + 1) * 512], PB[3][:, 0:512]),
                    reads=[("pb", 3)], writes=[("grep", blkq)])
            for tt in range(NT):
                eng = "dve"
                rdmp = rdump
                p.op(eng, lambda e, tt=tt, ee=ee, rdmp=rdmp: e.tensor_scalar(
                    rdmp[:], grepb[:], gates_tm[:, tt * 8 + ee: tt * 8 + ee + 1],
                    None, ALU.is_gt, ALU.add,
                    accum_out=ranks[:, tt * 8 + ee: tt * 8 + ee + 1]),
                    reads=[("grep", 0), ("grep", 1), ("grep", 2), ("grep", 3),
                           ("gtm", tt)],
                    writes=[("rk", tt, ee), ("rdump", eng)])
        for tt in range(NT):
            rks = ranks[:, tt * 8:(tt + 1) * 8]
            gts = gates_tm[:, tt * 8:(tt + 1) * 8]
            p.op("dve", lambda e, rks=rks: e.tensor_scalar(
                pick[:], rks, 256.0, None, ALU.is_lt),
                reads=[("rk", tt, ee) for ee in range(8)], writes=[("pick",)])
            p.op("dve", lambda e, rks=rks: e.tensor_scalar(
                keyt[:], rks, 8.0, None, ALU.mult),
                reads=[("rk", tt, ee) for ee in range(8)], writes=[("key",)])
            p.op("dve", lambda e: e.tensor_add(keyt[:], keyt[:], erep[:]),
                 reads=[("key",), ("erep",)], writes=[("key2",)])
            p.op("dve", lambda e: e.tensor_mul(keym[:], keyt[:], pick[:]),
                 reads=[("key2",), ("pick",)], writes=[("km1",)])
            p.op("dve", lambda e: e.tensor_scalar(st[:, 0:8], pick[:], -1.0,
                                                  None, ALU.add),
                 reads=[("pick",)], writes=[("km2",)])
            p.op("dve", lambda e: e.tensor_add(keym[:], keym[:], st[:, 0:8]),
                 reads=[("km1",), ("km2",)], writes=[("km",)])
            p.op("dve", lambda e: e.reduce_max(st[:, 8:9], keym[:], AX.X),
                 reads=[("km",)], writes=[("wk",)])
            p.op("dve", lambda e: e.tensor_scalar(
                isw[:], keym[:], st[:, 8:9], None, ALU.is_ge),
                reads=[("km",), ("wk",)], writes=[("isw",)])
            p.op("dve", lambda e, gts=gts: e.tensor_scalar(
                ov5[:], gts, 0.5, None, ALU.is_gt),
                reads=[("gtm", tt)], writes=[("ov5",)])
            p.op("dve", lambda e: e.tensor_mul(isw[:], isw[:], ov5[:]),
                 reads=[("isw",), ("ov5",)], writes=[("isw2",)])
            p.op("dve", lambda e, tt=tt: e.tensor_mul(
                selb[:, tt * 8:(tt + 1) * 8], isw[:], pick[:]),
                reads=[("isw2",), ("pick",)], writes=[("sel", tt)])
        if "dbgD" in io:
            p.op("sp", lambda e, b=b: e.dma_start(
                out=io["dbgD"][b, :, 0:NT * 8], in_=gates_tm[:]),
                reads=[("gtm", tt2) for tt2 in range(NT)],
                writes=[("d_dbgD", b, 0)], dma=True)
            p.op("sp", lambda e, b=b: e.dma_start(
                out=io["dbgD"][b, :, NT * 8:2 * NT * 8], in_=ranks[:]),
                reads=[("rk", tt2, ee2) for tt2 in range(NT) for ee2 in range(8)],
                writes=[("d_dbgD", b, 1)], dma=True)
            p.op("sp", lambda e, b=b: e.dma_start(
                out=io["dbgD"][b, :, 2 * NT * 8:3 * NT * 8], in_=selb[:]),
                reads=[("sel", tt2) for tt2 in range(NT)],
                writes=[("d_dbgD", b, 2)], dma=True)
        ctx.push()
        xmw = ctx.sb("d_xmw", (128, 8 * 1024), BF16)
        accb = ctx.sb("d_acc", (128, 8 * D), BF16)
        wet = [ctx.sb("d_wet", (128, 8 * D), BF16) for _ in range(2)]
        s0tile = ctx.sb("d_ytile", (128, D), BF16)
        for hp in range(2):
            t0h = hp * 8
            for dc in range(8):
                p.op("sp", lambda e, b=b, dc=dc, hp=hp: e.dma_start(
                    out=xmw[:, dc * 1024:(dc + 1) * 1024],
                    in_=io["xn1_fm"][b, dc * 128:(dc + 1) * 128,
                                     hp * 1024:(hp + 1) * 1024]),
                    writes=[("dxm", dc)], dma=True)
            p.op("dve", lambda e: e.memset(accb[:], 0.0),
                 writes=[("acc", i, hf) for i in range(8) for hf in range(2)])
            if hp == 0:
                p.op("pool", lambda e: e.dma_start(
                    out=wet[0][:],
                    in_=io["L1_ffn_experts"][0].rearrange("(c p) n -> p c n", p=128)),
                    writes=[("wet", 0)], dma=True)
            for ee in range(8):
                par = ee % 2
                if ee + 1 < 8:
                    p.op("pool", lambda e, ee=ee: e.dma_start(
                        out=wet[(ee + 1) % 2][:],
                        in_=io["L1_ffn_experts"][ee + 1].rearrange(
                            "(c p) n -> p c n", p=128)),
                        writes=[("wet", (ee + 1) % 2)], dma=True)
                elif hp == 0:
                    p.op("pool", lambda e: e.dma_start(
                        out=wet[0][:],
                        in_=io["L1_ffn_experts"][0].rearrange(
                            "(c p) n -> p c n", p=128)),
                        writes=[("wet", 0)], dma=True)
                for ti in range(8):
                    tt = t0h + ti
                    for half in range(2):
                        bk = (ti % 2) * 2 + half
                        for dc in range(8):
                            p.op("pe", lambda e, ti=ti, dc=dc, half=half, bk=bk,
                                 par=par: e.matmul(
                                     PB[bk][:, 0:512],
                                     xmw[:, dc * 1024 + ti * 128:
                                         dc * 1024 + ti * 128 + 128],
                                     wet[par][:, dc * D + half * 512:
                                              dc * D + (half + 1) * 512],
                                     start=(dc == 0), stop=(dc == 7)),
                                 reads=[("dxm", dc), ("wet", par)],
                                 writes=[("pb", bk)],
                                 hoist=("dMoE", b, hp, ee, ti, half))
                        p.op("dve", lambda e, ti=ti, tt=tt, half=half, bk=bk, ee=ee:
                             e.scalar_tensor_tensor(
                                 accb[:, ti * D + half * 512:
                                      ti * D + (half + 1) * 512],
                                 PB[bk][:, 0:512],
                                 selb[:, tt * 8 + ee: tt * 8 + ee + 1],
                                 accb[:, ti * D + half * 512:
                                      ti * D + (half + 1) * 512],
                                 ALU.mult, ALU.add),
                             reads=[("pb", bk), ("sel", tt), ("acc", ti, half)],
                             writes=[("acc", ti, half)])
            for ti in range(8):
                tt = t0h + ti
                r0 = tt * 128
                p.op("dve", lambda e, ti=ti: e.tensor_copy(
                    s0tile[:], accb[:, ti * D:(ti + 1) * D]),
                    reads=[("acc", ti, 0), ("acc", ti, 1)], writes=[("yt",)])
                p.op("sp", lambda e, b=b, r0=r0: e.dma_start(
                    out=io["y1"][b, r0:r0 + 128, :], in_=s0tile[:]),
                    reads=[("yt",)], writes=[("d_y1", b, r0 // 128)], dma=True)
        p.barrier()
        ctx.pop()
    p.barrier()



def emit_E(ctx, t, io):
    p = ctx.p
    PB, PT = ctx.psum, ctx.psb
    cms = {}
    for r in range(4):
        cms[r] = ctx.sb(f"e_cm{r}", (128, 512), F32)
        hh = ctx.const_in(f"c_causal{r}")
        p.op("sp", lambda e, r=r, hh=hh: e.dma_start(out=cms[r][:], in_=hh),
             writes=[("cm", r)], dma=True)
    q1f = ctx.sb("e_q1", (128, 8 * T), BF16)
    k1f = ctx.sb("e_k1", (128, 8 * T), BF16)
    vtm = ctx.sb("e_vtm", (128, NT * D), BF16)
    ebuf = ctx.sb("e_ebuf", (128, 16 * 512), BF16)
    tmsk = ctx.sb("e_tmsk", (128, 512), F32)
    s1t = ctx.sb("e_s1", (128, D), BF16)
    st = ctx.sb("e_st", (128, 4), F32)
    for b in range(B_PER_CORE):
        for dc in range(8):
            p.op("sp", lambda e, b=b, dc=dc: e.dma_start(
                out=q1f[:, dc * T:(dc + 1) * T],
                in_=io["q1fm"][b, dc * 128:(dc + 1) * 128, :]),
                writes=[("eq", dc)], dma=True)
            p.op("sp", lambda e, b=b, dc=dc: e.dma_start(
                out=k1f[:, dc * T:(dc + 1) * T],
                in_=io["k1fm"][b, dc * 128:(dc + 1) * 128, :]),
                writes=[("ek", dc)], dma=True)
        for tt in range(NT):
            p.op("sp", lambda e, b=b, tt=tt: e.dma_start(
                out=vtm[:, tt * D:(tt + 1) * D],
                in_=io["v1tm"][b, tt * 128:(tt + 1) * 128, :]),
                writes=[("ev", tt)], dma=True)
        for qb in range(4):
            ktmax = 4 * qb + 4
            for kt in range(ktmax):
                for dc in range(8):
                    p.op("pe", lambda e, kt=kt, qb=qb, dc=dc: e.matmul(
                        PB[0][:, 0:512],
                        k1f[:, dc * T + kt * 128: dc * T + kt * 128 + 128],
                        q1f[:, dc * T + qb * 512: dc * T + (qb + 1) * 512],
                        start=(dc == 0), stop=(dc == 7)),
                        reads=[("eq", dc), ("ek", dc)], writes=[("pb", 0)],
                        hoist=("eS", b, qb))
                r = kt - 4 * qb
                if 0 <= r:
                    p.op("dve", lambda e, r=r: e.tensor_add(
                        tmsk[:], PB[0][:, 0:512], cms[r][:]),
                        reads=[("pb", 0), ("cm", r)], writes=[("etm",)])
                    p.op("act", lambda e, kt=kt: e.activation(
                        ebuf[:, kt * 512:(kt + 1) * 512], tmsk[:], AF.Exp,
                        scale=1.0 / 32.0),
                        reads=[("etm",)], writes=[("eb", kt)])
                else:
                    p.op("act", lambda e, kt=kt: e.activation(
                        ebuf[:, kt * 512:(kt + 1) * 512], PB[0][:, 0:512], AF.Exp,
                        scale=1.0 / 32.0),
                        reads=[("pb", 0)], writes=[("eb", kt)])
            for qc in range(4):
                qg = qb * 4 + qc
                nkt = qg + 1
                for kt in range(nkt):
                    for half in range(2):
                        p.op("pe", lambda e, kt=kt, qc=qc, half=half, nkt=nkt:
                             e.matmul(
                                 PB[2 + half][:, 0:512],
                                 ebuf[:, kt * 512 + qc * 128:
                                      kt * 512 + qc * 128 + 128],
                                 vtm[:, kt * D + half * 512:
                                     kt * D + (half + 1) * 512],
                                 start=(kt == 0), stop=(kt == nkt - 1)),
                             reads=[("eb", kt), ("ev", kt)],
                             writes=[("pb", 2 + half)],
                             hoist=("eAV", b, qg))
                    p.op("pe", lambda e, kt=kt, qc=qc, nkt=nkt: e.matmul(
                        PB[4][:, 0:1],
                        ebuf[:, kt * 512 + qc * 128: kt * 512 + qc * 128 + 128],
                        t["ones_c_bf"], start=(kt == 0), stop=(kt == nkt - 1)),
                        reads=[("eb", kt), ("const", "o3")], writes=[("pb", 4)],
                        hoist=("eAV", b, qg))
                p.op("dve", lambda e: e.reciprocal(st[:, 0:1], PB[4][:, 0:1]),
                     reads=[("pb", 4)], writes=[("erc",)])
                for half in range(2):
                    p.op("dve", lambda e, half=half: e.tensor_scalar(
                        s1t[:, half * 512:(half + 1) * 512],
                        PB[2 + half][:, 0:512], st[:, 0:1], None, ALU.mult),
                        reads=[("pb", 2 + half), ("erc",)],
                        writes=[("es1", half)])
                p.op("sp", lambda e, b=b, qg=qg: e.dma_start(
                    out=io["s1"][b, qg * 128:(qg + 1) * 128, :], in_=s1t[:]),
                    reads=[("es1", 0), ("es1", 1)],
                    writes=[("d_s1", b, qg)], dma=True)
    p.barrier()



def emit_F(ctx, t, io):
    p = ctx.p
    x1t = ctx.sb("f_x1", (128, D), F32)
    s1t = ctx.sb("f_s1", (128, D), BF16)
    y1t = ctx.sb("f_y1", (128, D), BF16)
    sil = ctx.sb("f_sil", (128, D), F32)
    ot = ctx.sb("f_o", (128, D), F32)
    for b in range(B_PER_CORE):
        for tt in range(NT):
            r0 = tt * 128
            p.op("sp", lambda e, b=b, r0=r0: e.dma_start(
                out=x1t[:], in_=io["x1_tm"][b, r0:r0 + 128, :]),
                writes=[("fx1",)], dma=True)
            p.op("sp", lambda e, b=b, r0=r0: e.dma_start(
                out=s1t[:], in_=io["s1"][b, r0:r0 + 128, :]),
                writes=[("fs1",)], dma=True)
            p.op("sp", lambda e, b=b, r0=r0: e.dma_start(
                out=y1t[:], in_=io["y1"][b, r0:r0 + 128, :]),
                writes=[("fy1",)], dma=True)
            p.op("act", lambda e: e.activation(sil[:], y1t[:], AF.Silu),
                 reads=[("fy1",)], writes=[("fsil",)])
            p.op("dve", lambda e: e.tensor_mul(sil[:], sil[:], s1t[:]),
                 reads=[("fsil",), ("fs1",)], writes=[("fsg",)])
            p.op("dve", lambda e: e.tensor_add(ot[:], x1t[:], sil[:]),
                 reads=[("fx1",), ("fsg",)], writes=[("fo",)])
            p.op("sp", lambda e, b=b, r0=r0: e.dma_start(
                out=io["out"][b, r0:r0 + 128, :], in_=ot[:]),
                reads=[("fo",)], writes=[("d_out", b, tt)], dma=True)
    p.barrier()


def build_launch_C():
    from contextlib import ExitStack
    es = ExitStack()
    nc = bass.Bass("TRN2", target_bir_lowering=False)
    io = {}
    io["x"] = nc.dram_tensor("x", [B_PER_CORE, T, D], F32, kind="ExternalInput").ap()
    io["s0"] = nc.dram_tensor("s0", [B_PER_CORE, T, D], BF16, kind="ExternalInput").ap()
    io["g0"] = nc.dram_tensor("g0", [B_PER_CORE, T, D], BF16, kind="ExternalInput").ap()
    for nm, shp in (("L1_attn_Wq", (D, D)), ("L1_attn_Wk", (D, D)),
                    ("L1_attn_Wv", (D, D)), ("L1_ffn_gate_w", (D, 8))):
        io[nm] = nc.dram_tensor(nm, list(shp), F32, kind="ExternalInput").ap()
    io["x1_tm"] = nc.dram_tensor("x1_tm", [B_PER_CORE, T, D], F32,
                                 kind="ExternalOutput").ap()
    io["xn1_fm"] = nc.dram_tensor("xn1_fm", [B_PER_CORE, D, T], BF16,
                                  kind="ExternalOutput").ap()
    io["glog"] = nc.dram_tensor("glog", [B_PER_CORE, T, 8], F32,
                                kind="ExternalOutput").ap()
    io["q1fm"] = nc.dram_tensor("q1fm", [B_PER_CORE, D, T], BF16,
                                kind="ExternalOutput").ap()
    io["k1fm"] = nc.dram_tensor("k1fm", [B_PER_CORE, D, T], BF16,
                                kind="ExternalOutput").ap()
    io["v1tm"] = nc.dram_tensor("v1tm", [B_PER_CORE, T, D], BF16,
                                kind="ExternalOutput").ap()
    ctx = _mk_ctx(nc, es)
    blk = es.enter_context(nc.Block())
    t = setup_common(ctx)
    ctx.push()
    emit_C(ctx, t, io)
    ctx.pop()
    fk = []
    for b in range(B_PER_CORE):
        fk += [("d_x1", b, i) for i in range(NT)]
        fk += [("d_glog", b, i) for i in range(NT)]
        fk += [("d_xn1fm", b, j, g) for j in range(8) for g in range(4)]
        fk += [("d_q1fm", b, j, g) for j in range(8) for g in range(4)]
        fk += [("d_k1fm", b, j, g) for j in range(8) for g in range(4)]
        fk += [("d_v1tm", b, i) for i in range(NT)]
    _finish(ctx, t, fk)
    _emit_all(ctx, blk)
    return nc, es, ctx.const_inputs_used(), ctx


def build_launch_D():
    from contextlib import ExitStack
    es = ExitStack()
    nc = bass.Bass("TRN2", target_bir_lowering=False)
    io = {}
    io["xn1_fm"] = nc.dram_tensor("xn1_fm", [B_PER_CORE, D, T], BF16,
                                  kind="ExternalInput").ap()
    io["glog"] = nc.dram_tensor("glog", [B_PER_CORE, T, 8], F32,
                                kind="ExternalInput").ap()
    io["L1_ffn_experts"] = nc.dram_tensor("L1_ffn_experts", [8, D, D], F32,
                                          kind="ExternalInput").ap()
    io["y1"] = nc.dram_tensor("y1", [B_PER_CORE, T, D], BF16,
                              kind="ExternalOutput").ap()
    import os as _os
    if _os.environ.get("DBG_D"):
        io["dbgD"] = nc.dram_tensor("dbgD", [B_PER_CORE, 128, 3 * NT * 8], F32,
                                    kind="ExternalOutput").ap()
    ctx = _mk_ctx(nc, es)
    blk = es.enter_context(nc.Block())
    t = setup_common(ctx)
    ctx.push()
    emit_D(ctx, t, io)
    ctx.pop()
    fk = []
    for b in range(B_PER_CORE):
        fk += [("d_y1", b, i) for i in range(NT)]
    _finish(ctx, t, fk)
    _emit_all(ctx, blk)
    return nc, es, ctx.const_inputs_used(), ctx


def build_launch_E():
    from contextlib import ExitStack
    es = ExitStack()
    nc = bass.Bass("TRN2", target_bir_lowering=False)
    io = {}
    io["q1fm"] = nc.dram_tensor("q1fm", [B_PER_CORE, D, T], BF16,
                                kind="ExternalInput").ap()
    io["k1fm"] = nc.dram_tensor("k1fm", [B_PER_CORE, D, T], BF16,
                                kind="ExternalInput").ap()
    io["v1tm"] = nc.dram_tensor("v1tm", [B_PER_CORE, T, D], BF16,
                                kind="ExternalInput").ap()
    io["s1"] = nc.dram_tensor("s1", [B_PER_CORE, T, D], BF16,
                              kind="ExternalOutput").ap()
    ctx = _mk_ctx(nc, es)
    blk = es.enter_context(nc.Block())
    t = setup_common(ctx)
    ctx.push()
    emit_E(ctx, t, io)
    ctx.pop()
    fk = []
    for b in range(B_PER_CORE):
        fk += [("d_s1", b, i) for i in range(NT)]
    _finish(ctx, t, fk)
    _emit_all(ctx, blk)
    return nc, es, ctx.const_inputs_used(), ctx


def build_launch_F():
    from contextlib import ExitStack
    es = ExitStack()
    nc = bass.Bass("TRN2", target_bir_lowering=False)
    io = {}
    io["x1_tm"] = nc.dram_tensor("x1_tm", [B_PER_CORE, T, D], F32,
                                 kind="ExternalInput").ap()
    io["s1"] = nc.dram_tensor("s1", [B_PER_CORE, T, D], BF16,
                              kind="ExternalInput").ap()
    io["y1"] = nc.dram_tensor("y1", [B_PER_CORE, T, D], BF16,
                              kind="ExternalInput").ap()
    io["out"] = nc.dram_tensor("out", [B_PER_CORE, T, D], F32,
                               kind="ExternalOutput").ap()
    ctx = _mk_ctx(nc, es)
    blk = es.enter_context(nc.Block())
    t = setup_common(ctx)
    ctx.push()
    emit_F(ctx, t, io)
    ctx.pop()
    fk = []
    for b in range(B_PER_CORE):
        fk += [("d_out", b, i) for i in range(NT)]
    _finish(ctx, t, fk)
    _emit_all(ctx, blk)
    return nc, es, ctx.const_inputs_used(), ctx



_BUILD_CACHE = {}


def _get_build(name):
    if name not in _BUILD_CACHE:
        _BUILD_CACHE[name] = {"A": build_launch_A, "B": build_launch_B,
                              "C": build_launch_C, "D": build_launch_D,
                              "E": build_launch_E, "F": build_launch_F,
                              "FUSED": build_fused}[name]()
    return _BUILD_CACHE[name]


def kernel_unfused(**inputs):
    x = np.ascontiguousarray(np.asarray(inputs["L0_attn_x"], dtype=np.float32))
    cores = list(range(N_CORES))
    xs = [x[c * B_PER_CORE:(c + 1) * B_PER_CORE] for c in cores]

    nc, es, consts, ctx = _get_build("A")
    maps = []
    for c in cores:
        m = prep_inputs_A(inputs, xs[c])
        m.update(consts)
        maps.append(m)
    resA = run_bass_kernel_spmd(nc, maps, core_ids=cores).results
    nc, es, consts, ctx = _get_build("B")
    maps = []
    for c in cores:
        m = prep_inputs_B(inputs, resA[c]["xn0_tm"], resA[c]["xn0_fm"])
        m.update(consts)
        maps.append(m)
    resB = run_bass_kernel_spmd(nc, maps, core_ids=cores).results
    nc, es, consts, ctx = _get_build("C")
    maps = []
    for c in cores:
        m = {"x": xs[c], "s0": resA[c]["s0"], "g0": resB[c]["g0"],
             "L1_attn_Wq": np.ascontiguousarray(inputs["L1_attn_Wq"]),
             "L1_attn_Wk": np.ascontiguousarray(inputs["L1_attn_Wk"]),
             "L1_attn_Wv": np.ascontiguousarray(inputs["L1_attn_Wv"]),
             "L1_ffn_gate_w": np.ascontiguousarray(
                 np.asarray(inputs["L1_ffn_gate_w"]).reshape(D, 8))}
        m.update(consts)
        maps.append(m)
    resC = run_bass_kernel_spmd(nc, maps, core_ids=cores).results
    nc, es, consts, ctx = _get_build("D")
    maps = []
    for c in cores:
        m = {"xn1_fm": resC[c]["xn1_fm"], "glog": resC[c]["glog"],
             "L1_ffn_experts": np.ascontiguousarray(inputs["L1_ffn_experts"])}
        m.update(consts)
        maps.append(m)
    resD = run_bass_kernel_spmd(nc, maps, core_ids=cores).results
    nc, es, consts, ctx = _get_build("E")
    maps = []
    for c in cores:
        m = {"q1fm": resC[c]["q1fm"], "k1fm": resC[c]["k1fm"],
             "v1tm": resC[c]["v1tm"]}
        m.update(consts)
        maps.append(m)
    resE = run_bass_kernel_spmd(nc, maps, core_ids=cores).results
    nc, es, consts, ctx = _get_build("F")
    maps = []
    for c in cores:
        m = {"x1_tm": resC[c]["x1_tm"], "s1": resE[c]["s1"], "y1": resD[c]["y1"]}
        m.update(consts)
        maps.append(m)
    resF = run_bass_kernel_spmd(nc, maps, core_ids=cores).results
    out = np.concatenate([resF[c]["out"] for c in cores], axis=0)
    return out.astype(np.float32)



def build_fused():
    from contextlib import ExitStack
    es = ExitStack()
    nc = bass.Bass("TRN2", target_bir_lowering=False)
    io = {}
    io["x"] = nc.dram_tensor("x", [B_PER_CORE, T, D], F32, kind="ExternalInput").ap()
    for nm, shp in W_SHAPES_A.items():
        io[nm] = nc.dram_tensor(nm, list(shp), F32, kind="ExternalInput").ap()
    for nm, shp in W_SHAPES_B.items():
        io[nm] = nc.dram_tensor(nm, list(shp), F32, kind="ExternalInput").ap()
    for nm, shp in (("L1_attn_Wq", (D, D)), ("L1_attn_Wk", (D, D)),
                    ("L1_attn_Wv", (D, D)), ("L1_ffn_gate_w", (D, 8)),
                    ("L1_ffn_experts", (8, D, D))):
        io[nm] = nc.dram_tensor(nm, list(shp), F32, kind="ExternalInput").ap()
    io["out"] = nc.dram_tensor("out", [B_PER_CORE, T, D], F32,
                               kind="ExternalOutput").ap()
    scratch = {
        "xn0_tm": ([B_PER_CORE, T, D], F32), "xn0_fm": ([B_PER_CORE, D, T], BF16),
        "h_fm": ([B_PER_CORE, D, T], BF16), "o_st": ([B_PER_CORE, T, VD], BF16),
        "xmid": ([B_PER_CORE, T, D], F32), "h2fm": ([B_PER_CORE, D, T], BF16),
        "s0": ([B_PER_CORE, T, D], BF16), "g0": ([B_PER_CORE, T, D], BF16),
        "xmid0": ([B_PER_CORE, T, D], F32), "xm0fm": ([B_PER_CORE, D, T], BF16),
        "x1_tm": ([B_PER_CORE, T, D], F32), "xn1_fm": ([B_PER_CORE, D, T], BF16),
        "glog": ([B_PER_CORE, T, 8], F32), "q1fm": ([B_PER_CORE, D, T], BF16),
        "k1fm": ([B_PER_CORE, D, T], BF16), "v1tm": ([B_PER_CORE, T, D], BF16),
        "y1": ([B_PER_CORE, T, D], BF16), "s1": ([B_PER_CORE, T, D], BF16),
    }
    for nm, (shp, dt) in scratch.items():
        io[nm] = nc.dram_tensor(nm, shp, dt, kind="Internal").ap()
    ctx = _mk_ctx(nc, es)
    blk = es.enter_context(nc.Block())
    t = setup_common(ctx)
    for fn in (emit_A0, emit_A1, emit_A2, emit_A3, emit_B, emit_C, emit_D,
               emit_E, emit_F):
        ctx.push()
        fn(ctx, t, io)
        ctx.pop()
    fk = []
    for b in range(B_PER_CORE):
        fk += [("d_out", b, i) for i in range(NT)]
    _finish(ctx, t, fk)
    _emit_all(ctx, blk)
    return nc, es, ctx.const_inputs_used(), ctx


def prep_inputs_fused(inputs, xs):
    m = {"x": np.ascontiguousarray(xs)}
    for nm, shp in list(W_SHAPES_A.items()) + list(W_SHAPES_B.items()):
        m[nm] = np.ascontiguousarray(np.asarray(inputs[nm],
                                                dtype=np.float32)).reshape(shp)
    for nm, shp in (("L1_attn_Wq", (D, D)), ("L1_attn_Wk", (D, D)),
                    ("L1_attn_Wv", (D, D)), ("L1_ffn_gate_w", (D, 8)),
                    ("L1_ffn_experts", (8, D, D))):
        m[nm] = np.ascontiguousarray(np.asarray(inputs[nm],
                                                dtype=np.float32)).reshape(shp)
    return m


def kernel_fused(**inputs):
    x = np.ascontiguousarray(np.asarray(inputs["L0_attn_x"], dtype=np.float32))
    cores = list(range(N_CORES))
    nc, es, consts, ctx = _get_build("FUSED")
    maps = []
    for c in cores:
        m = prep_inputs_fused(inputs, x[c * B_PER_CORE:(c + 1) * B_PER_CORE])
        m.update(consts)
        maps.append(m)
    res = run_bass_kernel_spmd(nc, maps, core_ids=cores).results
    return np.concatenate([res[c]["out"] for c in cores], axis=0).astype(np.float32)


def kernel(**inputs):
    return kernel_fused(**inputs)


# version 12 (final)
# speedup vs baseline: 1.0522x; 1.0522x over previous
import math
import numpy as np
import ml_dtypes
import concourse.bass as bass
import concourse.mybir as mybir
from concourse.bass_utils import run_bass_kernel_spmd

F32 = mybir.dt.float32
BF16 = mybir.dt.bfloat16
AF = mybir.ActivationFunctionType
ALU = mybir.AluOpType
AX = mybir.AxisListType
bfnp = ml_dtypes.bfloat16

N_CORES = 8
B_PER_CORE = 2
T = 2048
D = 1024
TILE = 128
NT = T // TILE

H0, DK, DV = 8, 96, 192
KD, VD = H0 * DK, H0 * DV
IFF = 4096
H1, DH1 = 16, 64

ENGINES = ("pe", "dve", "act", "pool", "sp")
ENG_ATTR = {"pe": "tensor", "dve": "vector", "act": "scalar",
            "pool": "gpsimd", "sp": "sync"}


class Prog:

    def __init__(self, nc):
        self.nc = nc
        self.ops = {e: [] for e in ENGINES}
        self.val = {e: 0 for e in ENGINES}
        self.waited = {e: {d: 0 for d in ENGINES} for e in ENGINES}
        self.pending = {e: {} for e in ENGINES}
        self.bufs = {}
        self.n_ops = 0

    def barrier(self):
        snap = dict(self.val)
        for e in ENGINES:
            for d in ENGINES:
                if d == e and d not in ("sp", "pool"):
                    continue
                if snap[d] > self.waited[e][d]:
                    self.pending[e][d] = max(self.pending[e].get(d, 0), snap[d])

    def _buf(self, key):
        return self.bufs.setdefault(key, {"w": None, "r": {}})

    def op(self, eng, fn, reads=(), writes=(), dma=False):
        assert eng in ENGINES, eng
        waits = {}
        if self.pending[eng]:
            for d, v in self.pending[eng].items():
                if v > self.waited[eng][d]:
                    waits[d] = max(waits.get(d, 0), v)
            self.pending[eng] = {}

        def need_wait(d, v, d_dma):
            if d == eng and eng == "pe" and not d_dma and not dma:
                return False
            return self.waited[eng][d] < v

        for key in tuple(reads) + tuple(writes):
            b = self._buf(key)
            if b["w"] is not None:
                d, v, d_dma = b["w"]
                if need_wait(d, v, d_dma):
                    waits[d] = max(waits.get(d, 0), v)
        for key in writes:
            b = self._buf(key)
            for d, (v, d_dma) in b["r"].items():
                if need_wait(d, v, d_dma):
                    waits[d] = max(waits.get(d, 0), v)
        for d in list(waits):
            if d in ("sp", "pool") and waits[d] < self.val[d]:
                waits[d] = self.val[d]
        for d, v in waits.items():
            self.waited[eng][d] = v
        inc = 16 if dma else 1
        self.val[eng] += inc
        v_after = self.val[eng]
        self.ops[eng].append((dict(waits), fn, inc))
        for key in reads:
            self._buf(key)["r"][eng] = (v_after, dma)
        for key in writes:
            b = self._buf(key)
            b["w"] = (eng, v_after, dma)
            b["r"] = {}
        self.n_ops += 1
        return v_after

    def alias(self, newkey, oldkeys):
        b = self._buf(newkey)
        for ok in oldkeys:
            if ok in self.bufs:
                ob = self.bufs[ok]
                if ob["w"] is not None:
                    if b["w"] is None or ob["w"][1] > b["w"][1]:
                        pass
                for d, rv in ob["r"].items():
                    if d not in b["r"] or rv[0] > b["r"][d][0]:
                        b["r"][d] = rv
                if ob["w"] is not None:
                    d, v, dd = ob["w"]
                    if d not in b["r"] or v > b["r"][d][0]:
                        b["r"][d] = (v, dd)

    def emit(self, blk, sems):
        for e in ENGINES:
            ops = self.ops[e]
            if not ops:
                continue

            def make(eng_ops, eng_name):
                def body(engine):
                    for waits, fn, inc in eng_ops:
                        for d, v in waits.items():
                            engine.wait_ge(sems[d], v)
                        ins = fn(engine)
                        ins.then_inc(sems[eng_name], inc)
                return body

            getattr(blk, ENG_ATTR[e])(make(ops, e))


def host_consts():
    c = {}
    c["c_ident_bf"] = np.eye(TILE, dtype=bfnp)
    c["c_ident_f"] = np.eye(TILE, dtype=np.float32)
    jj, ii = np.meshgrid(np.arange(TILE), np.arange(TILE), indexing="ij")
    c["c_mask_ui"] = np.where(jj <= ii, 0.0, -30000.0).astype(np.float32)
    c["c_mask_us"] = np.where(jj < ii, 0.0, -30000.0).astype(np.float32)
    c["c_mask_li"] = np.where(jj >= ii, 0.0, -30000.0).astype(np.float32)
    c["c_mask_ls"] = np.where(jj > ii, 0.0, -30000.0).astype(np.float32)
    c["c_nmask_ls"] = np.where(jj > ii, 0.0, 30000.0).astype(np.float32)
    c["c_triu_f"] = np.triu(np.ones((TILE, TILE), np.float32))
    bas = np.zeros((8, 8 * 128), np.float32)
    for h in range(8):
        bas[h, h * 128:(h + 1) * 128] = 1.0
    c["c_basis8"] = bas
    kk = np.arange(TILE)[:, None]
    qq = np.arange(512)[None, :]
    for r in range(4):
        c[f"c_causal{r}"] = np.where(TILE * r + kk <= qq, 0.0,
                                     -30000.0).astype(np.float32)
    return c


CONSTS = host_consts()


class Ctx:

    def __init__(self, nc, p, es):
        self.nc = nc
        self.p = p
        self.es = es
        self.uid = 0
        self._const_handles = {}
        self._bank_keys = [("psbank", i) for i in range(8)]
        self.psum = []
        self.sems = {}

    def name(self, base):
        self.uid += 1
        return f"{base}_{self.uid}"

    def sb(self, base, shape, dtype):
        es = self._stack[-1] if getattr(self, "_stack", None) else self.es
        if not hasattr(self, "ledger"):
            self.ledger = []
        nbytes = int(np.prod(shape[1:])) * (2 if dtype == BF16 else 4)
        self.ledger.append((base, tuple(shape), nbytes,
                            len(getattr(self, "_stack", []) or [])))
        try:
            return es.enter_context(
                self.nc.sbuf_tensor(self.name(base), list(shape), dtype))
        except Exception:
            live = {}
            for nm, shp, nb, lvl in self.ledger:
                live[nm] = (shp, nb, lvl)
            print("LEDGER (all ever allocated):")
            for nm, shp, nb, lvl in self.ledger:
                print(f"  {nm} {shp} {nb}B lvl{lvl}")
            raise

    def push(self):
        from contextlib import ExitStack
        if not hasattr(self, "_stack"):
            self._stack = []
        self._stack.append(ExitStack())

    def pop(self):
        if hasattr(self, "ledger"):
            self.ledger.append(("__POP__", (), 0, len(self._stack)))
        self._stack.pop().close()

    def init_psum(self):
        self.psum = [self.es.enter_context(
            self.nc.psum_tensor(self.name("ps"), [128, 512], F32))
            for _ in range(6)]
        self.psb = [self.es.enter_context(
            self.nc.psum_tensor(self.name("psb"), [128, 1024], BF16))
            for _ in range(2)]

    def init_sems(self):
        for e in ENGINES:
            self.sems[e] = self.es.enter_context(
                self.nc.semaphore(self.name("sem_" + e)))

    def const_in(self, key):
        if key not in self._const_handles:
            v = CONSTS[key]
            dt = BF16 if v.dtype == bfnp else F32
            self._const_handles[key] = self.nc.dram_tensor(
                key, list(v.shape), dt, kind="ExternalInput").ap()
        return self._const_handles[key]

    def const_inputs_used(self):
        return {k: CONSTS[k] for k in self._const_handles}


def setup_common(ctx):
    p, nc = ctx.p, ctx.nc
    t = {}
    t["ident_bf"] = ctx.sb("ident_bf", (128, 128), BF16)
    t["ident_f"] = ctx.sb("ident_f", (128, 128), F32)
    t["mask_ui"] = ctx.sb("mask_ui", (128, 128), F32)
    t["mask_us"] = ctx.sb("mask_us", (128, 128), F32)
    t["nmask_ls"] = ctx.sb("nmask_ls", (128, 128), F32)
    t["triu_f"] = ctx.sb("triu_f", (128, 128), F32)
    for k, sb in [("c_ident_bf", t["ident_bf"]), ("c_ident_f", t["ident_f"]),
                  ("c_mask_ui", t["mask_ui"]), ("c_mask_us", t["mask_us"]),
                  ("c_nmask_ls", t["nmask_ls"]), ("c_triu_f", t["triu_f"])]:
        hh = ctx.const_in(k)
        p.op("sp", lambda e, sb=sb, hh=hh: e.dma_start(out=sb[:], in_=hh),
             writes=[("const", k)], dma=True)
    for k in ("ident_bf", "ident_f", "mask_ui", "mask_us", "nmask_ls", "triu_f"):
        t[k] = t[k][:]
    t["ones_r_bf"] = ctx.sb("ones_r_bf", (1, 128), BF16)
    t["ones_r_f"] = ctx.sb("ones_r_f", (1, 128), F32)
    t["ones_c_bf"] = ctx.sb("ones_c_bf", (128, 1), BF16)
    t["ones_c_f"] = ctx.sb("ones_c_f", (128, 1), F32)
    t["consts"] = ctx.sb("constcols", (128, 8), F32)
    p.op("dve", lambda e: e.memset(t["ones_r_bf"][:], 1.0), writes=[("const", "o1")])
    p.op("dve", lambda e: e.memset(t["ones_r_f"][:], 1.0), writes=[("const", "o2")])
    p.op("dve", lambda e: e.memset(t["ones_c_bf"][:], 1.0), writes=[("const", "o3")])
    p.op("dve", lambda e: e.memset(t["ones_c_f"][:], 1.0), writes=[("const", "o4")])
    p.op("dve", lambda e: e.memset(t["consts"][:, 0:1], 1e-6), writes=[("const", "c0")])
    p.op("dve", lambda e: e.memset(t["consts"][:, 1:2], 1e-5), writes=[("const", "c1")])
    p.op("dve", lambda e: e.memset(t["consts"][:, 2:3], 1.0), writes=[("const", "c2")])
    p.op("dve", lambda e: e.memset(t["consts"][:, 3:4], 0.0), writes=[("const", "c3")])
    p.op("dve", lambda e: e.memset(t["consts"][:, 4:5], -1.0), writes=[("const", "c4")])
    for k in ("ones_r_bf", "ones_r_f", "ones_c_bf", "ones_c_f", "consts"):
        t[k] = t[k][:]
    t["eps6"] = t["consts"][:, 0:1]
    t["eps5"] = t["consts"][:, 1:2]
    t["one"] = t["consts"][:, 2:3]
    t["zero"] = t["consts"][:, 3:4]
    t["neg1"] = t["consts"][:, 4:5]
    return t


def rck(*a):
    return tuple(a)



def emit_A0(ctx, t, io):
    p = ctx.p
    nc = ctx.nc
    xt = ctx.sb("a0_x", (128, 1024), F32)
    xn = ctx.sb("a0_xn", (128, 1024), F32)
    sq = ctx.sb("a0_sq", (128, 1024), F32)
    xnb = ctx.sb("a0_xnb", (128, 1024), BF16)
    hb = ctx.sb("a0_hb", (128, 1024), BF16)
    st = ctx.sb("a0_st", (128, 16), F32)
    w1rep = ctx.sb("a0_w1rep", (128, 1024), F32)
    w1row = ctx.sb("a0_w1row", (1, 1024), F32)
    sxn = ctx.sb("a0_sxn", (128, 8 * 512), BF16)
    sh = ctx.sb("a0_sh", (128, 8 * 512), BF16)
    PB, PT = ctx.psum, ctx.psb

    p.op("sp", lambda e: e.dma_start(out=w1row[:], in_=io["L0_attn_norm1_w"][None, :]),
         writes=[("w1row",)], dma=True)
    for seg in range(2):
        p.op("pe", lambda e, seg=seg: e.matmul(
            PB[0][:, 0:512], t["ones_r_f"][0:1, 0:128],
            w1row[0:1, seg * 512:(seg + 1) * 512], start=True, stop=True),
            reads=[("w1row",), ("const", "o2")], writes=[("pb", 0)])
        p.op("dve", lambda e, seg=seg: e.tensor_copy(
            w1rep[:, seg * 512:(seg + 1) * 512], PB[0][:, 0:512]),
            reads=[("pb", 0)], writes=[("w1rep", seg)])

    for b in range(B_PER_CORE):
        for grp in range(4):
            for t4 in range(4):
                tt = grp * 4 + t4
                r0 = tt * 128
                p.op("sp", lambda e, b=b, r0=r0: e.dma_start(
                    out=xt[:], in_=io["x"][b, r0:r0 + 128, :]),
                    writes=[("a0x",)], dma=True)
                p.op("dve", lambda e: e.reduce_sum(st[:, 0:1], xt[:], AX.X),
                     reads=[("a0x",)], writes=[("st", 0)])
                p.op("act", lambda e: e.activation(sq[:], xt[:], AF.Square,
                                                   accum_out=st[:, 1:2]),
                     reads=[("a0x",)], writes=[("st", 1), ("a0sq",)])
                p.op("dve", lambda e: e.tensor_scalar(
                    st[:, 2:3], st[:, 0:1], -1.0 / 1024, None, ALU.mult),
                    reads=[("st", 0)], writes=[("st", 2)])
                p.op("dve", lambda e: e.tensor_scalar(
                    st[:, 3:4], st[:, 1:2], 1.0 / 1024, None, ALU.mult),
                    reads=[("st", 1)], writes=[("st", 3)])
                p.op("dve", lambda e: e.scalar_tensor_tensor(
                    st[:, 4:5], st[:, 2:3], 1.0, st[:, 2:3], ALU.mult, ALU.mult),
                    reads=[("st", 2)], writes=[("st", 4)])
                p.op("dve", lambda e: e.tensor_sub(st[:, 5:6], st[:, 3:4], st[:, 4:5]),
                     reads=[("st", 3), ("st", 4)], writes=[("st", 5)])
                p.op("act", lambda e: e.activation(st[:, 6:7], st[:, 5:6], AF.Ln,
                                                   bias=t["eps5"], scale=1.0),
                     reads=[("st", 5), ("const", "c1")], writes=[("st", 6)])
                p.op("act", lambda e: e.activation(st[:, 7:8], st[:, 6:7], AF.Exp,
                                                   scale=-0.5),
                     reads=[("st", 6)], writes=[("st", 7)])
                p.op("dve", lambda e: e.tensor_scalar(
                    xn[:], xt[:], st[:, 2:3], st[:, 7:8], ALU.add, ALU.mult),
                    reads=[("a0x",), ("st", 2), ("st", 7)], writes=[("a0xn",)])
                p.op("sp", lambda e, b=b, r0=r0: e.dma_start(
                    out=io["xn0_tm"][b, r0:r0 + 128, :], in_=xn[:]),
                    reads=[("a0xn",)], writes=[("d_xn0_tm", b, tt)], dma=True)
                p.op("dve", lambda e: e.tensor_copy(xnb[:], xn[:]),
                     reads=[("a0xn",)], writes=[("a0xnb",)])
                p.op("act", lambda e: e.activation(sq[:], xn[:], AF.Square,
                                                   accum_out=st[:, 8:9]),
                     reads=[("a0xn",)], writes=[("st", 8), ("a0sq",)])
                p.op("act", lambda e: e.activation(st[:, 9:10], st[:, 8:9], AF.Ln,
                                                   bias=t["eps6"], scale=1.0 / 1024),
                     reads=[("st", 8), ("const", "c0")], writes=[("st", 9)])
                p.op("act", lambda e: e.activation(st[:, 10:11], st[:, 9:10], AF.Exp,
                                                   scale=-0.5),
                     reads=[("st", 9)], writes=[("st", 10)])
                p.op("dve", lambda e: e.scalar_tensor_tensor(
                    hb[:], xn[:], st[:, 10:11], w1rep[:], ALU.mult, ALU.mult),
                    reads=[("a0xn",), ("st", 10), ("w1rep", 0), ("w1rep", 1)],
                    writes=[("a0hb",)])
                for j in range(8):
                    c0 = j * 512 + t4 * 128
                    p.op("pe", lambda e, j=j: e.transpose(
                        PT[0][:, 0:128], xnb[:, j * 128:(j + 1) * 128], t["ident_bf"]),
                        reads=[("a0xnb",), ("const", "c_ident_bf")],
                        writes=[("pt", 0)])
                    p.op("dve", lambda e, c0=c0: e.tensor_copy(
                        sxn[:, c0:c0 + 128], PT[0][:, 0:128]),
                        reads=[("pt", 0)], writes=[("sxn", j, t4)])
                    p.op("pe", lambda e, j=j: e.transpose(
                        PT[1][:, 0:128], hb[:, j * 128:(j + 1) * 128], t["ident_bf"]),
                        reads=[("a0hb",), ("const", "c_ident_bf")],
                        writes=[("pt", 1)])
                    p.op("act", lambda e, c0=c0: e.activation(
                        sh[:, c0:c0 + 128], PT[1][:, 0:128], AF.Copy),
                        reads=[("pt", 1)], writes=[("sh", j, t4)])
            for j in range(8):
                cc = grp * 512
                p.op("sp", lambda e, b=b, j=j, cc=cc: e.dma_start(
                    out=io["xn0_fm"][b, j * 128:(j + 1) * 128, cc:cc + 512],
                    in_=sxn[:, j * 512:(j + 1) * 512]),
                    reads=[("sxn", j, 0), ("sxn", j, 1), ("sxn", j, 2), ("sxn", j, 3)],
                    writes=[("d_xn0_fm", b, j, grp)], dma=True)
                p.op("sp", lambda e, b=b, j=j, cc=cc: e.dma_start(
                    out=io["h_fm"][b, j * 128:(j + 1) * 128, cc:cc + 512],
                    in_=sh[:, j * 512:(j + 1) * 512]),
                    reads=[("sh", j, 0), ("sh", j, 1), ("sh", j, 2), ("sh", j, 3)],
                    writes=[("d_h_fm", b, j, grp)], dma=True)
    p.barrier()


def emit_A1(ctx, t, io):
    p = ctx.p
    PB, PT = ctx.psum, ctx.psb
    W = 1024
    NC_W = W // 128

    wq = ctx.sb("a1_wq", (128, 8 * KD), BF16)
    wk = ctx.sb("a1_wk", (128, 8 * KD), BF16)
    wv = ctx.sb("a1_wv", (128, 8 * VD), BF16)
    wb = ctx.sb("a1_wb", (128, 8 * 8), BF16)
    wa = ctx.sb("a1_wa", (128, 8 * 8), BF16)
    for nm, sb, hh, ncols in [("wq", wq, "L0_attn_Wq", KD), ("wk", wk, "L0_attn_Wk", KD),
                              ("wv", wv, "L0_attn_Wv", VD), ("wb", wb, "L0_attn_Wb", 8),
                              ("wa", wa, "L0_attn_Wa", 8)]:
        src = io[hh].rearrange("(c p) n -> p c n", p=128)
        p.op("pool", lambda e, sb=sb, src=src: e.dma_start(out=sb[:], in_=src),
             writes=[("w", nm)], dma=True)
    cwq = ctx.sb("a1_cwq", (96, 32), F32)
    cwk = ctx.sb("a1_cwk", (96, 32), F32)
    cwv = ctx.sb("a1_cwv", (128, 48), F32)
    p.op("sp", lambda e: e.dma_start(
        out=cwq[:], in_=io["L0_attn_conv_q_w"].rearrange("(h c) k -> c h k", c=96)),
        writes=[("w", "cwq")], dma=True)
    p.op("sp", lambda e: e.dma_start(
        out=cwk[:], in_=io["L0_attn_conv_k_w"].rearrange("(h c) k -> c h k", c=96)),
        writes=[("w", "cwk")], dma=True)
    p.op("sp", lambda e: e.dma_start(
        out=cwv[:], in_=io["L0_attn_conv_v_w"].rearrange("(f c) k -> c f k", c=128)),
        writes=[("w", "cwv")], dma=True)
    alr = ctx.sb("a1_alr", (1, 8), F32)
    dtr = ctx.sb("a1_dtr", (1, 8), F32)
    row64 = ctx.sb("a1_row64", (1, 128), F32)
    dtb64 = ctx.sb("a1_dtb64", (128, 64), F32)
    negA64 = ctx.sb("a1_negA64", (128, 64), F32)
    p.op("sp", lambda e: e.dma_start(out=alr[:], in_=io["L0_attn_A_log"][None, :]),
         writes=[("w", "alr")], dma=True)
    p.op("sp", lambda e: e.dma_start(out=dtr[:], in_=io["L0_attn_dt_bias"][None, :]),
         writes=[("w", "dtr")], dma=True)
    for c in range(8):
        p.op("dve", lambda e, c=c: e.tensor_copy(row64[0:1, c * 8:(c + 1) * 8], dtr[:]),
             reads=[("w", "dtr")], writes=[("row64", c)])
    p.op("pe", lambda e: e.matmul(PB[0][:, 0:64], t["ones_r_f"][0:1, 0:128],
                                  row64[0:1, 0:64], start=True, stop=True),
         reads=[("row64", c) for c in range(8)] + [("const", "o2")], writes=[("pb", 0)])
    p.op("dve", lambda e: e.tensor_copy(dtb64[:], PB[0][:, 0:64]),
         reads=[("pb", 0)], writes=[("w", "dtb64")])
    alneg = ctx.sb("a1_alneg", (1, 8), F32)
    p.op("act", lambda e: e.activation(alneg[:], alr[:], AF.Exp),
         reads=[("w", "alr")], writes=[("w", "alne")])
    p.op("dve", lambda e: e.tensor_scalar(alneg[:], alneg[:], -1.0, None, ALU.mult),
         reads=[("w", "alne")], writes=[("w", "alneg")])
    for c in range(8):
        p.op("dve", lambda e, c=c: e.tensor_copy(row64[0:1, c * 8:(c + 1) * 8], alneg[:]),
             reads=[("w", "alneg")], writes=[("row64", c)])
    p.op("pe", lambda e: e.matmul(PB[0][:, 0:64], t["ones_r_f"][0:1, 0:128],
                                  row64[0:1, 0:64], start=True, stop=True),
         reads=[("row64", c) for c in range(8)] + [("const", "o2")], writes=[("pb", 0)])
    p.op("dve", lambda e: e.tensor_copy(negA64[:], PB[0][:, 0:64]),
         reads=[("pb", 0)], writes=[("w", "negA64")])

    hfm = ctx.sb("a1_hfm", (128, 8 * W), BF16)
    zc = ctx.sb("a1_zc", (128, 4 * (W + 3)), F32)
    ca = ctx.sb("a1_ca", (128, W), F32)
    qh = ctx.sb("a1_qh", (96, 8 * W), BF16)
    kh = ctx.sb("a1_kh", (96, 8 * W), BF16)
    vrot = ctx.sb("a1_vrot", (128, W), BF16)
    vtm = ctx.sb("a1_vtm", (128, 8 * VD), BF16)
    sq96 = ctx.sb("a1_sq96", (96, W), BF16)
    zbst = ctx.sb("a1_zbst", (8, W), BF16)
    zast = ctx.sb("a1_zast", (8, W), BF16)
    braw = ctx.sb("a1_braw", (128, 64), F32)
    araw = ctx.sb("a1_araw", (128, 64), F32)
    beta_tm = ctx.sb("a1_beta", (128, 64), F32)
    lnb_tm = ctx.sb("a1_lnb", (128, 64), F32)
    g_tm = ctx.sb("a1_g", (128, 64), F32)
    G_tm = ctx.sb("a1_G", (128, 64), F32)
    lk_tm = ctx.sb("a1_lk", (128, 64), F32)
    lq_tm = ctx.sb("a1_lq", (128, 64), F32)
    lkb_tm = ctx.sb("a1_lkb", (128, 64), F32)
    tmp64 = ctx.sb("a1_tmp64", (128, 64), F32)
    lnc8 = ctx.sb("a1_lnc8", (128, 8), F32)
    tq = ctx.sb("a1_tq", (96, 24), F32)
    tkk = ctx.sb("a1_tk", (96, 24), F32)
    tv = ctx.sb("a1_tv", (128, 36), F32)
    Sb = ctx.sb("a1_S", (96, 8 * DV), BF16)
    ost = ctx.sb("a1_ost", (128, VD), BF16)
    scanbufs = {}
    LNS = math.log(DK ** -0.5)

    def proj_qk(b, wdw, wsb, outsb, which):
        cw = cwq if which == "q" else cwk
        tail = tq if which == "q" else tkk
        dst = lq_tm if which == "q" else lk_tm
        if wdw == 0:
            p.op("dve", lambda e, tail=tail: e.memset(tail[:], 0.0),
                 writes=[("tail", which)])
        for h8 in range(8):
            slot = h8 % 4
            base = slot * (W + 3)
            for half in range(2):
                bk = (h8 * 2 + half) % 2
                for dc in range(8):
                    p.op("pe", lambda e, h8=h8, half=half, dc=dc, bk=bk, wsb=wsb:
                         e.matmul(PB[bk][0:96, 0:512],
                                  wsb[:, dc * KD + h8 * 96: dc * KD + h8 * 96 + 96],
                                  hfm[:, dc * W + half * 512: dc * W + (half + 1) * 512],
                                  start=(dc == 0), stop=(dc == 7)),
                         reads=[("w", "wq" if which == "q" else "wk"), ("hfm", dc)],
                         writes=[("pb", bk)])
                col0 = base + 3 + half * 512
                if half == 0:
                    p.op("dve", lambda e, col0=col0, bk=bk: e.tensor_copy(
                        zc[0:96, col0:col0 + 512], PB[bk][0:96, 0:512]),
                        reads=[("pb", bk)], writes=[("zc", which, slot, 0)])
                else:
                    p.op("act", lambda e, col0=col0, bk=bk: e.activation(
                        zc[0:96, col0:col0 + 512], PB[bk][0:96, 0:512], AF.Copy),
                        reads=[("pb", bk)], writes=[("zc", which, slot, 1)])
            rd = [("zc", which, slot, 0), ("zc", which, slot, 1),
                  ("w", "cw" + which), ("tail", which)]
            p.op("dve", lambda e, base=base, h8=h8, tail=tail: e.tensor_copy(
                zc[0:96, base:base + 3], tail[:, h8 * 3:(h8 + 1) * 3]),
                reads=rd, writes=[("zc", which, slot, "pre")])
            p.op("dve", lambda e, base=base, h8=h8, tail=tail: e.tensor_copy(
                tail[:, h8 * 3:(h8 + 1) * 3], zc[0:96, base + W:base + W + 3]),
                reads=rd, writes=[("tail", which, h8)])
            rd2 = rd + [("zc", which, slot, "pre")]
            p.op("dve", lambda e, base=base, cw=cw, h8=h8: e.tensor_scalar(
                ca[0:96, 0:W], zc[0:96, base:base + W], cw[:, h8 * 4:h8 * 4 + 1],
                None, ALU.mult), reads=rd2, writes=[("ca",)])
            for tap in range(1, 4):
                p.op("dve", lambda e, base=base, cw=cw, h8=h8, tap=tap:
                     e.scalar_tensor_tensor(
                         ca[0:96, 0:W], zc[0:96, base + tap:base + tap + W],
                         cw[:, h8 * 4 + tap:h8 * 4 + tap + 1], ca[0:96, 0:W],
                         ALU.mult, ALU.add),
                     reads=rd2 + [("ca",)], writes=[("ca",)])
            p.op("act", lambda e, h8=h8, outsb=outsb: e.activation(
                outsb[0:96, h8 * W:(h8 + 1) * W], ca[0:96, 0:W], AF.Silu),
                reads=[("ca",)], writes=[("qkh", which, h8)])
            p.op("act", lambda e, h8=h8, outsb=outsb: e.activation(
                sq96[:], outsb[0:96, h8 * W:(h8 + 1) * W], AF.Square),
                reads=[("qkh", which, h8)], writes=[("sq96",)])
            for c in range(NC_W):
                p.op("pe", lambda e, c=c: e.matmul(
                    PB[2][:, c:c + 1], sq96[0:96, c * 128:(c + 1) * 128],
                    t["ones_c_bf"][0:96, :], start=True, stop=True),
                    reads=[("sq96",), ("const", "o3")], writes=[("pb", 2)])
            p.op("act", lambda e: e.activation(
                lnc8[:], PB[2][:, 0:8], AF.Ln, bias=t["eps6"], scale=1.0),
                reads=[("pb", 2), ("const", "c0")], writes=[("lncols",)])
            dstv = dst[:, h8::8]
            if which == "q":
                p.op("dve", lambda e, dstv=dstv: e.tensor_scalar(
                    dstv, lnc8[:], -0.5, LNS, ALU.mult, ALU.add),
                    reads=[("lncols",)], writes=[("lt", which, h8)])
            else:
                p.op("dve", lambda e, dstv=dstv: e.tensor_scalar(
                    dstv, lnc8[:], -0.5, None, ALU.mult),
                    reads=[("lncols",)], writes=[("lt", which, h8)])

    for b in range(B_PER_CORE):
        p.op("dve", lambda e: e.memset(Sb[:], 0.0), writes=[("Sb", h) for h in range(8)])
        for wdw in range(2):
            w0 = wdw * W
            for dc in range(8):
                p.op("sp", lambda e, b=b, dc=dc, w0=w0: e.dma_start(
                    out=hfm[:, dc * W:(dc + 1) * W],
                    in_=io["h_fm"][b, dc * 128:(dc + 1) * 128, w0:w0 + W]),
                    reads=[("d_h_fm", b, dc, w0 // 512), ("d_h_fm", b, dc, w0 // 512 + 1)],
                    writes=[("hfm", dc)], dma=True)
            proj_qk(b, wdw, wq, qh, "q")
            proj_qk(b, wdw, wk, kh, "k")
            if wdw == 0:
                p.op("dve", lambda e: e.memset(tv[:], 0.0), writes=[("tail", "v")])
            for fc in range(12):
                slot = fc % 4
                base = slot * (W + 3)
                for half in range(2):
                    bk = (fc * 2 + half) % 2
                    for dc in range(8):
                        p.op("pe", lambda e, fc=fc, half=half, dc=dc, bk=bk:
                             e.matmul(PB[bk][:, 0:512],
                                      wv[:, dc * VD + fc * 128: dc * VD + fc * 128 + 128],
                                      hfm[:, dc * W + half * 512: dc * W + (half + 1) * 512],
                                      start=(dc == 0), stop=(dc == 7)),
                             reads=[("w", "wv"), ("hfm", dc)], writes=[("pb", bk)])
                    col0 = base + 3 + half * 512
                    p.op("dve" if half == 0 else "act",
                         (lambda e, col0=col0, bk=bk: e.tensor_copy(
                             zc[:, col0:col0 + 512], PB[bk][:, 0:512]))
                         if half == 0 else
                         (lambda e, col0=col0, bk=bk: e.activation(
                             zc[:, col0:col0 + 512], PB[bk][:, 0:512], AF.Copy)),
                         reads=[("pb", bk)], writes=[("zc", "v", slot, half)])
                rd = [("zc", "v", slot, 0), ("zc", "v", slot, 1), ("w", "cwv"),
                      ("tail", "v")]
                p.op("dve", lambda e, base=base, fc=fc: e.tensor_copy(
                    zc[:, base:base + 3], tv[:, fc * 3:(fc + 1) * 3]),
                    reads=rd, writes=[("zc", "v", slot, "pre")])
                p.op("dve", lambda e, base=base, fc=fc: e.tensor_copy(
                    tv[:, fc * 3:(fc + 1) * 3], zc[:, base + W:base + W + 3]),
                    reads=rd, writes=[("tail", "v", fc)])
                rd2 = rd + [("zc", "v", slot, "pre")]
                p.op("dve", lambda e, base=base, fc=fc: e.tensor_scalar(
                    ca[:, 0:W], zc[:, base:base + W], cwv[:, fc * 4:fc * 4 + 1],
                    None, ALU.mult), reads=rd2, writes=[("ca",)])
                for tap in range(1, 4):
                    p.op("dve", lambda e, base=base, fc=fc, tap=tap:
                         e.scalar_tensor_tensor(
                             ca[:, 0:W], zc[:, base + tap:base + tap + W],
                             cwv[:, fc * 4 + tap:fc * 4 + tap + 1], ca[:, 0:W],
                             ALU.mult, ALU.add),
                         reads=rd2 + [("ca",)], writes=[("ca",)])
                p.op("act", lambda e: e.activation(vrot[:], ca[:, 0:W], AF.Silu),
                     reads=[("ca",)], writes=[("vrot",)])
                for tc in range(8):
                    p.op("pe", lambda e, tc=tc: e.transpose(
                        PT[tc % 2][:, 0:128], vrot[:, tc * 128:(tc + 1) * 128],
                        t["ident_bf"]),
                        reads=[("vrot",), ("const", "c_ident_bf")],
                        writes=[("pt", tc % 2)])
                    dcol = tc * VD + fc * 128
                    p.op("dve" if tc % 2 == 0 else "act",
                         (lambda e, dcol=dcol, tc=tc: e.tensor_copy(
                             vtm[:, dcol:dcol + 128], PT[tc % 2][:, 0:128]))
                         if tc % 2 == 0 else
                         (lambda e, dcol=dcol, tc=tc: e.activation(
                             vtm[:, dcol:dcol + 128], PT[tc % 2][:, 0:128],
                             AF.Copy)),
                         reads=[("pt", tc % 2)], writes=[("vtm", tc, fc)])
            for which, wsb, dst in [("b", wb, zbst), ("a", wa, zast)]:
                for half in range(2):
                    for dc in range(8):
                        p.op("pe", lambda e, wsb=wsb, dc=dc, half=half: e.matmul(
                            PB[3][0:8, 0:512], wsb[:, dc * 8:(dc + 1) * 8],
                            hfm[:, dc * W + half * 512: dc * W + (half + 1) * 512],
                            start=(dc == 0), stop=(dc == 7)),
                            reads=[("w", "w" + which), ("hfm", dc)], writes=[("pb", 3)])
                    p.op("dve", lambda e, dst=dst, half=half: e.tensor_copy(
                        dst[:, half * 512:(half + 1) * 512], PB[3][0:8, 0:512]),
                        reads=[("pb", 3)], writes=[("zba", which, half)])
                for c in range(NC_W):
                    p.op("pe", lambda e, dst=dst, c=c: e.transpose(
                        PT[0][:, 0:8], dst[0:8, c * 128:(c + 1) * 128],
                        t["ident_bf"][0:8, 0:8]),
                        reads=[("zba", which, 0), ("zba", which, 1),
                               ("const", "c_ident_bf")],
                        writes=[("pt", 0)])
                    tgt = braw if which == "b" else araw
                    p.op("dve", lambda e, tgt=tgt, c=c: e.tensor_copy(
                        tgt[:, c * 8:(c + 1) * 8], PT[0][:, 0:8]),
                        reads=[("pt", 0)], writes=[("traw", which, c)])
            rba = [("traw", "b", c) for c in range(NC_W)]
            raa = [("traw", "a", c) for c in range(NC_W)]
            p.op("act", lambda e: e.activation(beta_tm[:], braw[:], AF.Sigmoid),
                 reads=rba, writes=[("beta",)])
            p.op("act", lambda e: e.activation(lnb_tm[:], beta_tm[:], AF.Ln),
                 reads=[("beta",)], writes=[("lnb",)])
            p.op("dve", lambda e: e.tensor_add(araw[:], araw[:], dtb64[:]),
                 reads=raa + [("w", "dtb64")], writes=[("gx",)])
            p.op("act", lambda e: e.activation(tmp64[:], araw[:], AF.Abs),
                 reads=[("gx",)], writes=[("gabs",)])
            p.op("act", lambda e: e.activation(tmp64[:], tmp64[:], AF.Exp, scale=-1.0),
                 reads=[("gabs",)], writes=[("gabs",)])
            p.op("act", lambda e: e.activation(tmp64[:], tmp64[:], AF.Ln,
                                               bias=t["one"], scale=1.0),
                 reads=[("gabs",)], writes=[("gabs",)])
            p.op("dve", lambda e: e.tensor_scalar(g_tm[:], araw[:], t["zero"],
                                                  None, ALU.max),
                 reads=[("gx",), ("const", "c3")], writes=[("grelu",)])
            p.op("dve", lambda e: e.tensor_add(g_tm[:], g_tm[:], tmp64[:]),
                 reads=[("grelu",), ("gabs",)], writes=[("gsp",)])
            p.op("dve", lambda e: e.tensor_mul(g_tm[:], g_tm[:], negA64[:]),
                 reads=[("gsp",), ("w", "negA64")], writes=[("g",)])
            p.op("dve", lambda e: e.tensor_add(lkb_tm[:], lk_tm[:], lnb_tm[:]),
                 reads=[("lt", "k", h) for h in range(8)] + [("lnb",)],
                 writes=[("lklnb",)])
            if "dbg" in io and b == 0 and wdw == 0:
                p.op("sp", lambda e: e.dma_start(out=io["dbg"][0:96, :],
                                                 in_=qh[:]),
                     reads=[("qkh", "q", h) for h in range(8)],
                     writes=[("d_dbg", 0)], dma=True)
                p.op("sp", lambda e: e.dma_start(out=io["dbg"][96:192, :],
                                                 in_=kh[:]),
                     reads=[("qkh", "k", h) for h in range(8)],
                     writes=[("d_dbg", 1)], dma=True)
                p.op("sp", lambda e: e.dma_start(out=io["dbg2"][:, 0:64],
                                                 in_=beta_tm[:]),
                     reads=[("beta",)], writes=[("d_dbg", 2)], dma=True)
                p.op("sp", lambda e: e.dma_start(out=io["dbg2"][:, 64:128],
                                                 in_=g_tm[:]),
                     reads=[("g",)], writes=[("d_dbg", 3)], dma=True)
                p.op("sp", lambda e: e.dma_start(out=io["dbg2"][:, 128:192],
                                                 in_=lk_tm[:]),
                     reads=[("lt", "k", h) for h in range(8)],
                     writes=[("d_dbg", 4)], dma=True)
                p.op("sp", lambda e: e.dma_start(out=io["dbg2"][:, 192:256],
                                                 in_=lq_tm[:]),
                     reads=[("lt", "q", h) for h in range(8)],
                     writes=[("d_dbg", 5)], dma=True)
                p.op("sp", lambda e: e.dma_start(out=io["dbg3"][:, :],
                                                 in_=vtm[:]),
                     reads=[("vtm", tc, fc) for tc in range(8) for fc in range(12)],
                     writes=[("d_dbg", 6)], dma=True)
            a = scanbufs
            a.update(dict(hfm=hfm, qh=qh, kh=kh, vtm=vtm, beta_tm=beta_tm,
                          g_tm=g_tm, G_tm=G_tm, lk_tm=lk_tm, lq_tm=lq_tm,
                          lkb_tm=lkb_tm, Sb=Sb, ost=ost))
            emit_scan_window(ctx, t, io, b, w0, a)
    p.barrier()


def emit_scan_window(ctx, t, io, b, w0, a):
    p = ctx.p
    PB, PT = ctx.psum, ctx.psb
    W = 1024
    gcc = a.setdefault("_gcc", ctx.sb("a1_gcc", (8, 1), F32))
    gcr = a.setdefault("_gcr", ctx.sb("a1_gcr", (1, 8), F32))
    if "_Dn" not in a:
        a["_Dn"] = [ctx.sb("a1_Dn", (128, 128), BF16) for _ in range(2)]
        a["_Dnt"] = [ctx.sb("a1_Dnt", (128, 128), BF16) for _ in range(2)]
        a["_Dqk"] = [ctx.sb("a1_Dqk", (128, 128), BF16) for _ in range(2)]
        a["_N"] = [ctx.sb("a1_N", (128, 128), BF16) for _ in range(2)]
        a["_NT"] = [ctx.sb("a1_NT", (128, 128), BF16) for _ in range(2)]
        a["_X"] = [ctx.sb("a1_X", (128, 128), BF16) for _ in range(2)]
        a["_U"] = [ctx.sb("a1_U", (128, DV), BF16) for _ in range(2)]
        a["_Usc"] = [ctx.sb("a1_Usc", (128, DV), BF16) for _ in range(2)]
        a["_ktm"] = [ctx.sb("a1_ktm", (128, 96), BF16) for _ in range(2)]
        a["_tmpv"] = [ctx.sb("a1_tmpv", (128, DV), F32) for _ in range(2)]
        a["_o2t"] = [ctx.sb("a1_o2t", (128, DV), F32) for _ in range(2)]
        a["_tmask"] = [ctx.sb("a1_tmask", (128, 256), F32) for _ in range(2)]
        a["_cch"] = ctx.sb("a1_cch", (128, 96), F32)
        a["_rows"] = ctx.sb("a1_rows", (8, 3 * 128), F32)
        a["_basis"] = ctx.sb("a1_basis", (8, 8 * 128), F32)
        hh = ctx.const_in("c_basis8")
        p.op("sp", lambda e, hh=hh: e.dma_start(out=a["_basis"][:], in_=hh),
             writes=[("bas", h) for h in range(8)], dma=True)
    cch = a["_cch"]
    rows = a["_rows"]
    basis = a["_basis"]

    def chunk_pre(c):
        c8 = c * 8
        pby = 2 + (c % 2)
        yield ("pe", lambda e, c8=c8, pby=pby: e.matmul(
            PB[pby][:, 0:8], t["triu_f"], a["g_tm"][:, c8:c8 + 8],
            start=True, stop=True),
            [("g",), ("const", "c_triu_f")], [("pb", pby)])
        yield ("dve", lambda e, c8=c8, pby=pby: e.tensor_copy(
            a["G_tm"][:, c8:c8 + 8], PB[pby][:, 0:8]),
            [("pb", pby)], [("G", c)])
        yield ("pe", lambda e, c8=c8, pby=pby: e.matmul(
            PB[pby][0:8, 0:1], a["g_tm"][:, c8:c8 + 8], t["ones_c_f"],
            start=True, stop=True),
            [("g",), ("const", "o4")], [("pb", pby)])
        yield ("dve", lambda e, pby=pby: e.tensor_copy(gcc[:], PB[pby][0:8, 0:1]),
               [("pb", pby)], [("gcc",)])
        yield ("pe", lambda e, pby=pby: e.transpose(
            PB[pby][0:1, 0:8], gcc[:], t["ident_f"][0:8, 0:8]),
            [("gcc",), ("const", "c_ident_f")], [("pb", pby)])
        yield ("dve", lambda e, pby=pby: e.tensor_copy(gcr[:], PB[pby][0:1, 0:8]),
               [("pb", pby)], [("gcrow",)])
        G8 = a["G_tm"][:, c8:c8 + 8]
        lk8 = a["lk_tm"][:, c8:c8 + 8]
        lq8 = a["lq_tm"][:, c8:c8 + 8]
        lkb8 = a["lkb_tm"][:, c8:c8 + 8]
        rGl = [("G", c)] + [("lt", "k", h) for h in range(8)] + \
              [("lt", "q", h) for h in range(8)] + [("lklnb",)]
        yield ("dve", lambda e, G8=G8, lk8=lk8: e.tensor_add(
            cch[:, 0:8], G8, lk8), rGl, [("cc", c, 0)])
        yield ("dve", lambda e, G8=G8, lkb8=lkb8: e.tensor_add(
            cch[:, 8:16], G8, lkb8), rGl, [("cc", c, 1)])
        yield ("dve", lambda e, G8=G8, lk8=lk8: e.tensor_sub(
            cch[:, 16:24], G8, lk8), rGl, [("cc", c, 2)])
        yield ("dve", lambda e, G8=G8, lq8=lq8: e.tensor_add(
            cch[:, 24:32], G8, lq8), rGl, [("cc", c, 3)])
        yield ("dve", lambda e: e.tensor_scalar(
            cch[:, 32:40], cch[:, 16:24], -1.0, None, ALU.mult),
            [("cc", c, 2)], [("cc", c, 4)])
        yield ("pe", lambda e: e.matmul(
            PB[pby][:, 0:8], t["ones_r_f"][0:1, 0:128], gcr[:],
            start=True, stop=True),
            [("gcrow",), ("const", "o2")], [("pb", pby)])
        yield ("dve", lambda e, pby=pby: e.tensor_copy(
            cch[:, 40:48], PB[pby][:, 0:8]), [("pb", pby)], [("cc", c, 5)])
        yield ("dve", lambda e: e.tensor_sub(
            cch[:, 48:56], cch[:, 40:48], cch[:, 16:24]),
            [("cc", c, 5), ("cc", c, 2)], [("cc", c, 6)])
        yield ("act", lambda e: e.activation(cch[:, 56:64], cch[:, 48:56], AF.Exp),
               [("cc", c, 6)], [("cc", c, 7)])
        yield ("act", lambda e: e.activation(cch[:, 64:72], cch[:, 0:8], AF.Exp),
               [("cc", c, 0)], [("cc", c, 8)])
        yield ("dve", lambda e: e.tensor_scalar(
            cch[:, 64:72], cch[:, 64:72], -1.0, None, ALU.mult),
            [("cc", c, 8)], [("cc", c, 8)])
        yield ("act", lambda e: e.activation(cch[:, 72:80], cch[:, 24:32], AF.Exp),
               [("cc", c, 3)], [("cc", c, 9)])
        yield ("act", lambda e: e.activation(cch[:, 80:88], cch[:, 40:48], AF.Exp),
               [("cc", c, 5)], [("cc", c, 10)])
        for ri, coff, kk in ((0, 8, "rowA"), (1, 16, "rowB"), (2, 24, "rowQ")):
            yield ("pe", lambda e, ri=ri, coff=coff, pby=pby: e.transpose(
                PB[pby][0:8, 0:128], cch[:, coff:coff + 8], t["ident_f"]),
                [("cc", c, coff // 8), ("const", "c_ident_f")], [("pb", pby)])
            yield ("dve", lambda e, ri=ri, pby=pby: e.tensor_copy(
                rows[:, ri * 128:(ri + 1) * 128], PB[pby][0:8, 0:128]),
                [("pb", pby)], [("rows", c, ri)])

    def instance(c, h):
        c8 = c * 8
        pr = (c8 + h) % 2
        xbk, ybk, zbk = (0, 2, 4) if pr == 0 else (1, 3, 5)
        Dn, Dnt, Dqk = a["_Dn"][pr], a["_Dnt"][pr], a["_Dqk"][pr]
        Ntl, NTl, Xt = a["_N"][pr], a["_NT"][pr], a["_X"][pr]
        Ut, Usc, ktm = a["_U"][pr], a["_Usc"][pr], a["_ktm"][pr]
        tmpv, o2t, tmask = a["_tmpv"][pr], a["_o2t"][pr], a["_tmask"][pr]
        ptb = PT[pr]
        bcol = a["beta_tm"][:, c8 + h:c8 + h + 1]
        khs = a["kh"][0:96, h * W + c * 128: h * W + c * 128 + 128]
        qhs = a["qh"][0:96, h * W + c * 128: h * W + c * 128 + 128]
        vslice = a["vtm"][:, c * VD + h * DV: c * VD + h * DV + DV]
        Ss = a["Sb"][0:96, h * DV:(h + 1) * DV]
        K = lambda *kk: tuple(kk) + (pr,)
        colA = cch[:, 8 + h:9 + h]
        negB = cch[:, 32 + h:33 + h]
        e6c = cch[:, 56 + h:57 + h]
        negEGlk = cch[:, 64 + h:65 + h]
        expQ = cch[:, 72 + h:73 + h]
        eGc = cch[:, 80 + h:81 + h]
        for ri, mk, dd, dkey, bias, sc, mkey in (
                (0, "c_mask_us", Dnt, "Dnt", negB, 1.0, ("cc", c, 4)),
                (1, "c_nmask_ls", Dn, "Dn", colA, -1.0, ("cc", c, 1)),
                (2, "c_mask_ui", Dqk, "Dqk", negB, 1.0, ("cc", c, 4))):
            yield ("pe", lambda e, ri=ri, h=h, ybk=ybk: e.matmul(
                PB[ybk][:, 0:128], basis[:, h * 128:(h + 1) * 128],
                rows[:, ri * 128:(ri + 1) * 128], start=True, stop=True),
                [("rows", c, ri), ("bas", h)], [("pb", ybk)])
            yield ("dve", lambda e, tmask=tmask, ybk=ybk, mk=mk: e.tensor_add(
                tmask[:, 0:128], PB[ybk][:, 0:128], t[mk[2:]]),
                [("pb", ybk), ("const", mk)], [K("tm")])
            yield ("act", lambda e, dd=dd, tmask=tmask, bias=bias, sc=sc:
                   e.activation(dd[:], tmask[:, 0:128], AF.Exp, bias=bias,
                                scale=sc),
                   [K("tm"), mkey], [K(dkey)])
        yield ("pe", lambda e, khs=khs, xbk=xbk: e.matmul(
            PB[xbk][:, 0:128], khs, khs, start=True, stop=True),
            [("qkh", "k", h)], [("pb", xbk)])
        yield ("pe", lambda e, khs=khs, qhs=qhs, zbk=zbk: e.matmul(
            PB[zbk][:, 0:128], khs, qhs, start=True, stop=True),
            [("qkh", "k", h), ("qkh", "q", h)], [("pb", zbk)])
        yield ("dve", lambda e, Ntl=Ntl, xbk=xbk, Dn=Dn: e.scalar_tensor_tensor(
            Ntl[:], PB[xbk][:, 0:128], t["neg1"], Dn[:], ALU.mult, ALU.mult),
            [("pb", xbk), K("Dn"), ("const", "c4")], [K("Ntl")])
        yield ("dve", lambda e, NTl=NTl, xbk=xbk, Dnt=Dnt: e.scalar_tensor_tensor(
            NTl[:], PB[xbk][:, 0:128], t["neg1"], Dnt[:], ALU.mult, ALU.mult),
            [("pb", xbk), K("Dnt"), ("const", "c4")], [K("NTl")])
        yield ("dve", lambda e, Xt=Xt, zbk=zbk, Dqk=Dqk: e.tensor_mul(
            Xt[:], PB[zbk][:, 0:128], Dqk[:]),
            [("pb", zbk), K("Dqk")], [K("Xt")])
        yield ("pe", lambda e, khs=khs, Ss=Ss, zbk=zbk: e.matmul(
            PB[zbk][:, 0:DV], khs, Ss, start=True, stop=True),
            [("qkh", "k", h), ("Sb", h)], [("pb", zbk)])
        yield ("dve", lambda e, vslice=vslice, tmpv=tmpv, zbk=zbk, negEGlk=negEGlk:
               e.scalar_tensor_tensor(
                   tmpv[:], PB[zbk][:, 0:DV], negEGlk, vslice, ALU.mult, ALU.add),
               [("pb", zbk), ("cc", c, 8)] + [("vtm", c, fc) for fc in range(12)],
               [K("tmpv")])
        yield ("dve", lambda e, bcol=bcol, Ut=Ut, tmpv=tmpv: e.tensor_scalar(
            Ut[:], tmpv[:], bcol, None, ALU.mult),
            [K("tmpv"), ("beta",)], [K("Ut")])
        for k in range(7):
            yield ("pe", lambda e, NTl=NTl, Ut=Ut, zbk=zbk: e.matmul(
                PB[zbk][:, 0:DV], NTl[:], Ut[:], start=True, stop=True),
                [K("NTl"), K("Ut")], [("pb", zbk)])
            if k < 6:
                yield ("pe", lambda e, Ntl=Ntl, NTl=NTl, xbk=xbk: e.matmul(
                    PB[xbk][:, 0:128], Ntl[:], NTl[:], start=True, stop=True),
                    [K("Ntl"), K("NTl")], [("pb", xbk)])
                yield ("pe", lambda e, Ntl=Ntl, NTl=NTl, ybk=ybk: e.matmul(
                    PB[ybk][:, 0:128], NTl[:], Ntl[:], start=True, stop=True),
                    [K("Ntl"), K("NTl")], [("pb", ybk)])
            yield ("dve", lambda e, Ut=Ut, zbk=zbk: e.tensor_add(
                Ut[:], PB[zbk][:, 0:DV], Ut[:]),
                [("pb", zbk)], [K("Ut")])
            if k < 6:
                yield ("act", lambda e, NTl=NTl, xbk=xbk: e.activation(
                    NTl[:], PB[xbk][:, 0:128], AF.Copy),
                    [("pb", xbk)], [K("NTl")])
                yield ("act", lambda e, Ntl=Ntl, ybk=ybk: e.activation(
                    Ntl[:], PB[ybk][:, 0:128], AF.Copy),
                    [("pb", ybk)], [K("Ntl")])
        yield ("pe", lambda e, qhs=qhs, Ss=Ss, xbk=xbk: e.matmul(
            PB[xbk][:, 0:DV], qhs, Ss, start=True, stop=True),
            [("qkh", "q", h), ("Sb", h)], [("pb", xbk)])
        yield ("pe", lambda e, Xt=Xt, Ut=Ut, ybk=ybk: e.matmul(
            PB[ybk][:, 0:DV], Xt[:], Ut[:], start=True, stop=True),
            [K("Xt"), K("Ut")], [("pb", ybk)])
        yield ("dve", lambda e, o2t=o2t, xbk=xbk, expQ=expQ: e.tensor_scalar(
            o2t[:], PB[xbk][:, 0:DV], expQ, None, ALU.mult),
            [("pb", xbk), ("cc", c, 9)], [K("o2t")])
        yield ("dve", lambda e, h=h, o2t=o2t, ybk=ybk: e.tensor_add(
            a["ost"][:, h * DV:(h + 1) * DV], PB[ybk][:, 0:DV], o2t[:]),
            [("pb", ybk), K("o2t")], [("ost", h)])
        yield ("pe", lambda e, khs=khs, ptb=ptb: e.transpose(
            ptb[:, 0:96], khs, t["ident_bf"][0:96, 0:96]),
            [("qkh", "k", h), ("const", "c_ident_bf")], [("pt", pr)])
        yield ("dve", lambda e, ktm=ktm, ptb=ptb: e.tensor_copy(
            ktm[:, 0:96], ptb[:, 0:96]),
            [("pt", pr)], [K("ktm")])
        yield ("dve", lambda e, Usc=Usc, Ut=Ut, e6c=e6c: e.tensor_scalar(
            Usc[:], Ut[:], e6c, None, ALU.mult),
            [K("Ut"), ("cc", c, 7)], [K("Usc")])
        yield ("pe", lambda e, ktm=ktm, Usc=Usc, zbk=zbk: e.matmul(
            PB[zbk][0:96, 0:DV], ktm[:, 0:96], Usc[:], start=True, stop=True),
            [K("ktm"), K("Usc")], [("pb", zbk)])
        yield ("dve", lambda e, Ss=Ss, zbk=zbk, eGc=eGc: e.scalar_tensor_tensor(
            Ss, Ss, eGc[0:96, :], PB[zbk][0:96, 0:DV], ALU.mult, ALU.add),
            [("pb", zbk), ("cc", c, 10), ("Sb", h)], [("Sb", h)])

    def run(gen):
        for eng, fn, rd, wr in gen:
            p.op(eng, fn, reads=rd, writes=wr)

    def interleave(g1, g2):
        it1, it2 = iter(g1), iter(g2)
        while True:
            done = 0
            for it in (it1, it2):
                try:
                    eng, fn, rd, wr = next(it)
                    p.op(eng, fn, reads=rd, writes=wr)
                except StopIteration:
                    done += 1
            if done == 2:
                break

    for c in range(8):
        run(chunk_pre(c))
        for hp in range(4):
            interleave(instance(c, 2 * hp), instance(c, 2 * hp + 1))
        p.op("sp", lambda e, b=b, c=c: e.dma_start(
            out=io["o_st"][b, w0 + c * 128:w0 + c * 128 + 128, :],
            in_=a["ost"][:]),
            reads=[("ost", h) for h in range(8)],
            writes=[("d_ost", b, (w0 + c * 128) // 128)], dma=True)


def emit_A2(ctx, t, io):
    p = ctx.p
    PB, PT = ctx.psum, ctx.psb
    wg = ctx.sb("a2_wg", (128, 8 * VD), BF16)
    wo = ctx.sb("a2_wo", (128, 12 * D), BF16)
    p.op("pool", lambda e: e.dma_start(
        out=wg[:], in_=io["L0_attn_Wg"].rearrange("(c p) n -> p c n", p=128)),
        writes=[("w", "wg")], dma=True)
    p.op("pool", lambda e: e.dma_start(
        out=wo[:], in_=io["L0_attn_Wo"].rearrange("(c p) n -> p c n", p=128)),
        writes=[("w", "wo")], dma=True)
    onr = ctx.sb("a2_onr", (1, VD), F32)
    onw1536 = ctx.sb("a2_onw", (128, VD), F32)
    w2row = ctx.sb("a2_w2row", (1, D), F32)
    w2rep = ctx.sb("a2_w2rep", (128, D), F32)
    p.op("sp", lambda e: e.dma_start(out=onr[0:1, 0:DV],
                                     in_=io["L0_attn_onorm_w"][None, :]),
         writes=[("onr", 0)], dma=True)
    for h in range(1, 8):
        p.op("dve", lambda e, h=h: e.tensor_copy(onr[0:1, h * DV:(h + 1) * DV],
                                                 onr[0:1, 0:DV]),
             reads=[("onr", 0)], writes=[("onr", h)])
    for seg in range(3):
        p.op("pe", lambda e, seg=seg: e.matmul(
            PB[0][:, 0:512], t["ones_r_f"][0:1, 0:128],
            onr[0:1, seg * 512:(seg + 1) * 512], start=True, stop=True),
            reads=[("onr", h) for h in range(8)] + [("const", "o2")],
            writes=[("pb", 0)])
        p.op("dve", lambda e, seg=seg: e.tensor_copy(
            onw1536[:, seg * 512:(seg + 1) * 512], PB[0][:, 0:512]),
            reads=[("pb", 0)], writes=[("onw",)])
    p.op("sp", lambda e: e.dma_start(out=w2row[:], in_=io["L0_attn_norm2_w"][None, :]),
         writes=[("w2row",)], dma=True)
    for seg in range(2):
        p.op("pe", lambda e, seg=seg: e.matmul(
            PB[0][:, 0:512], t["ones_r_f"][0:1, 0:128],
            w2row[0:1, seg * 512:(seg + 1) * 512], start=True, stop=True),
            reads=[("w2row",), ("const", "o2")], writes=[("pb", 0)])
        p.op("dve", lambda e, seg=seg: e.tensor_copy(
            w2rep[:, seg * 512:(seg + 1) * 512], PB[0][:, 0:512]),
            reads=[("pb", 0)], writes=[("w2rep",)])

    ot = ctx.sb("a2_ot", (128, VD), BF16)
    sqt = ctx.sb("a2_sq", (128, VD), BF16)
    hfmt = ctx.sb("a2_hfmt", (128, 8 * 128), BF16)
    gsl = ctx.sb("a2_gsl", (128, VD), BF16)
    og = ctx.sb("a2_og", (128, VD), BF16)
    ogfm = ctx.sb("a2_ogfm", (128, 12 * 128), BF16)
    xnt = ctx.sb("a2_xnt", (128, D), F32)
    xm = ctx.sb("a2_xm", (128, D), F32)
    h2b = ctx.sb("a2_h2b", (128, D), BF16)
    st = ctx.sb("a2_st", (128, 12), F32)
    stage = ctx.sb("a2_stage", (128, 8 * 512), BF16)

    for b in range(B_PER_CORE):
        for grp in range(4):
            for t4 in range(4):
                tt = grp * 4 + t4
                r0 = tt * 128
                p.op("sp", lambda e, b=b, r0=r0: e.dma_start(
                    out=ot[:], in_=io["o_st"][b, r0:r0 + 128, :]),
                    reads=[("d_ost", b, r0 // 128)], writes=[("a2ot",)], dma=True)
                for dc in range(8):
                    p.op("sp", lambda e, b=b, dc=dc, r0=r0: e.dma_start(
                        out=hfmt[:, dc * 128:(dc + 1) * 128],
                        in_=io["h_fm"][b, dc * 128:(dc + 1) * 128, r0:r0 + 128]),
                        reads=[("d_h_fm", b, dc, r0 // 512)],
                        writes=[("a2hf", dc)], dma=True)
                p.op("sp", lambda e, b=b, r0=r0: e.dma_start(
                    out=xnt[:], in_=io["xn0_tm"][b, r0:r0 + 128, :]),
                    reads=[("d_xn0_tm", b, tt)], writes=[("a2xn",)], dma=True)
                p.op("act", lambda e: e.activation(sqt[:], ot[:], AF.Square),
                     reads=[("a2ot",)], writes=[("a2sq",)])
                p.op("dve", lambda e: e.reduce_sum(
                    st[:, 0:8], sqt.rearrange("p (h v) -> p h v", v=DV), AX.X),
                    reads=[("a2sq",)], writes=[("a2st", 0)])
                p.op("act", lambda e: e.activation(st[:, 0:8], st[:, 0:8], AF.Ln,
                                                   bias=t["eps6"], scale=1.0 / DV),
                     reads=[("a2st", 0), ("const", "c0")], writes=[("a2st", 1)])
                p.op("act", lambda e: e.activation(st[:, 0:8], st[:, 0:8], AF.Exp,
                                                   scale=-0.5),
                     reads=[("a2st", 1)], writes=[("a2st", 2)])
                for fb in range(3):
                    for dc in range(8):
                        p.op("pe", lambda e, fb=fb, dc=dc: e.matmul(
                            PB[fb][:, 0:512], hfmt[:, dc * 128:(dc + 1) * 128],
                            wg[:, dc * VD + fb * 512: dc * VD + (fb + 1) * 512],
                            start=(dc == 0), stop=(dc == 7)),
                            reads=[("a2hf", dc), ("w", "wg")], writes=[("pb", fb)])
                    p.op("act", lambda e, fb=fb: e.activation(
                        gsl[:, fb * 512:(fb + 1) * 512], PB[fb][:, 0:512], AF.Silu),
                        reads=[("pb", fb)], writes=[("a2gsl", fb)])
                p.op("dve", lambda e: e.tensor_mul(gsl[:], gsl[:], onw1536[:]),
                     reads=[("a2gsl", 0), ("a2gsl", 1), ("a2gsl", 2), ("onw",)],
                     writes=[("a2gsl2",)])
                for h in range(8):
                    p.op("dve", lambda e, h=h: e.tensor_scalar(
                        og[:, h * DV:(h + 1) * DV], ot[:, h * DV:(h + 1) * DV],
                        st[:, h:h + 1], None, ALU.mult),
                        reads=[("a2ot",), ("a2st", 2)], writes=[("a2og", h)])
                p.op("dve", lambda e: e.tensor_mul(og[:], og[:], gsl[:]),
                     reads=[("a2og", h) for h in range(8)] + [("a2gsl2",)],
                     writes=[("a2ogf",)])
                for vc in range(12):
                    p.op("pe", lambda e, vc=vc: e.transpose(
                        PT[0][:, 0:128], og[:, vc * 128:(vc + 1) * 128],
                        t["ident_bf"]),
                        reads=[("a2ogf",), ("const", "c_ident_bf")],
                        writes=[("pt", 0)])
                    eng = "dve" if vc % 2 == 0 else "act"
                    if eng == "dve":
                        p.op("dve", lambda e, vc=vc: e.tensor_copy(
                            ogfm[:, vc * 128:(vc + 1) * 128], PT[0][:, 0:128]),
                            reads=[("pt", 0)], writes=[("a2ogfm", vc)])
                    else:
                        p.op("act", lambda e, vc=vc: e.activation(
                            ogfm[:, vc * 128:(vc + 1) * 128], PT[0][:, 0:128],
                            AF.Copy),
                            reads=[("pt", 0)], writes=[("a2ogfm", vc)])
                for dh in range(2):
                    for vc in range(12):
                        p.op("pe", lambda e, dh=dh, vc=vc: e.matmul(
                            PB[3 + dh][:, 0:512], ogfm[:, vc * 128:(vc + 1) * 128],
                            wo[:, vc * D + dh * 512: vc * D + (dh + 1) * 512],
                            start=(vc == 0), stop=(vc == 11)),
                            reads=[("a2ogfm", vc), ("w", "wo")],
                            writes=[("pb", 3 + dh)])
                    p.op("dve", lambda e, dh=dh: e.tensor_add(
                        xm[:, dh * 512:(dh + 1) * 512], PB[3 + dh][:, 0:512],
                        xnt[:, dh * 512:(dh + 1) * 512]),
                        reads=[("pb", 3 + dh), ("a2xn",)], writes=[("a2xm", dh)])
                p.op("sp", lambda e, b=b, r0=r0: e.dma_start(
                    out=io["xmid"][b, r0:r0 + 128, :], in_=xm[:]),
                    reads=[("a2xm", 0), ("a2xm", 1)],
                    writes=[("d_xmid", b, r0 // 128)], dma=True)
                p.op("act", lambda e: e.activation(sqt[:, 0:D], xm[:], AF.Square,
                                                   accum_out=st[:, 8:9]),
                     reads=[("a2xm", 0), ("a2xm", 1)], writes=[("a2st", 3), ("a2sq",)])
                p.op("act", lambda e: e.activation(st[:, 9:10], st[:, 8:9], AF.Ln,
                                                   bias=t["eps6"], scale=1.0 / D),
                     reads=[("a2st", 3), ("const", "c0")], writes=[("a2st", 4)])
                p.op("act", lambda e: e.activation(st[:, 10:11], st[:, 9:10], AF.Exp,
                                                   scale=-0.5),
                     reads=[("a2st", 4)], writes=[("a2st", 5)])
                p.op("dve", lambda e: e.scalar_tensor_tensor(
                    h2b[:], xm[:], st[:, 10:11], w2rep[:], ALU.mult, ALU.mult),
                    reads=[("a2xm", 0), ("a2xm", 1), ("a2st", 5), ("w2rep",)],
                    writes=[("a2h2",)])
                for j in range(8):
                    c0 = j * 512 + t4 * 128
                    p.op("pe", lambda e, j=j: e.transpose(
                        PT[1][:, 0:128], h2b[:, j * 128:(j + 1) * 128],
                        t["ident_bf"]),
                        reads=[("a2h2",), ("const", "c_ident_bf")],
                        writes=[("pt", 1)])
                    p.op("act", lambda e, c0=c0: e.activation(
                        stage[:, c0:c0 + 128], PT[1][:, 0:128], AF.Copy),
                        reads=[("pt", 1)], writes=[("a2stg", j, t4)])
            for j in range(8):
                cc = grp * 512
                p.op("sp", lambda e, b=b, j=j, cc=cc: e.dma_start(
                    out=io["h2fm"][b, j * 128:(j + 1) * 128, cc:cc + 512],
                    in_=stage[:, j * 512:(j + 1) * 512]),
                    reads=[("a2stg", j, 0), ("a2stg", j, 1), ("a2stg", j, 2),
                           ("a2stg", j, 3)],
                    writes=[("d_h2fm", b, j, grp)], dma=True)
    p.barrier()


def emit_A3(ctx, t, io):
    p = ctx.p
    PB, PT = ctx.psum, ctx.psb
    wd = ctx.sb("a3_wd", (128, 32 * D), BF16)
    p.op("pool", lambda e: e.dma_start(
        out=wd[:], in_=io["L0_attn_Wdown"].rearrange("(c p) n -> p c n", p=128)),
        writes=[("w", "wd")], dma=True)
    wg_t = [ctx.sb("a3_wg", (128, 8 * 128), BF16) for _ in range(2)]
    wu_t = [ctx.sb("a3_wu", (128, 8 * 128), BF16) for _ in range(2)]
    h2w = ctx.sb("a3_h2w", (128, 8 * 1024), BF16)
    intfm = ctx.sb("a3_int", (128, 32 * 1024), BF16)
    gs = [ctx.sb("a3_gs", (128, 512), BF16) for _ in range(2)]
    xmt = ctx.sb("a3_xm", (128, D), F32)
    s0t = ctx.sb("a3_s0", (128, D), BF16)

    for b in range(B_PER_CORE):
        for sb2 in range(2):
            w0 = sb2 * 1024
            for dc in range(8):
                p.op("sp", lambda e, b=b, dc=dc, w0=w0: e.dma_start(
                    out=h2w[:, dc * 1024:(dc + 1) * 1024],
                    in_=io["h2fm"][b, dc * 128:(dc + 1) * 128, w0:w0 + 1024]),
                    reads=[("d_h2fm", b, dc, w0 // 512), ("d_h2fm", b, dc, w0 // 512 + 1)],
                    writes=[("h2w", dc)], dma=True)
            for nc32 in range(32):
                par = nc32 % 2
                src_g = io["L0_attn_Wgate"].rearrange("(c p) n -> p c n", p=128)
                src_u = io["L0_attn_Wup"].rearrange("(c p) n -> p c n", p=128)
                p.op("pool", lambda e, par=par, nc32=nc32, src_g=src_g: e.dma_start(
                    out=wg_t[par][:], in_=src_g[:, :, nc32 * 128:(nc32 + 1) * 128]),
                    writes=[("wgt", par)], dma=True)
                p.op("pool", lambda e, par=par, nc32=nc32, src_u=src_u: e.dma_start(
                    out=wu_t[par][:], in_=src_u[:, :, nc32 * 128:(nc32 + 1) * 128]),
                    writes=[("wut", par)], dma=True)
                for tb in range(2):
                    bg0 = ((nc32 * 2 + tb) % 2) * 2
                    for dc in range(8):
                        p.op("pe", lambda e, par=par, dc=dc, tb=tb, bg0=bg0: e.matmul(
                            PB[bg0][:, 0:512], wg_t[par][:, dc * 128:(dc + 1) * 128],
                            h2w[:, dc * 1024 + tb * 512: dc * 1024 + (tb + 1) * 512],
                            start=(dc == 0), stop=(dc == 7)),
                            reads=[("wgt", par), ("h2w", dc)], writes=[("pb", bg0)])
                        p.op("pe", lambda e, par=par, dc=dc, tb=tb, bg0=bg0: e.matmul(
                            PB[bg0 + 1][:, 0:512],
                            wu_t[par][:, dc * 128:(dc + 1) * 128],
                            h2w[:, dc * 1024 + tb * 512: dc * 1024 + (tb + 1) * 512],
                            start=(dc == 0), stop=(dc == 7)),
                            reads=[("wut", par), ("h2w", dc)],
                            writes=[("pb", bg0 + 1)])
                    p.op("act", lambda e, tb=tb, bg0=bg0: e.activation(
                        gs[tb][:], PB[bg0][:, 0:512], AF.Silu),
                        reads=[("pb", bg0)], writes=[("a3gs", tb)])
                    p.op("dve", lambda e, nc32=nc32, tb=tb, bg0=bg0: e.tensor_mul(
                        intfm[:, nc32 * 1024 + tb * 512: nc32 * 1024 + (tb + 1) * 512],
                        gs[tb][:], PB[bg0 + 1][:, 0:512]),
                        reads=[("a3gs", tb), ("pb", bg0 + 1)],
                        writes=[("int", nc32, tb)])
            for t8 in range(8):
                r0 = w0 + t8 * 128
                p.op("sp", lambda e, b=b, r0=r0: e.dma_start(
                    out=xmt[:], in_=io["xmid"][b, r0:r0 + 128, :]),
                    reads=[("d_xmid", b, r0 // 128)], writes=[("a3xm",)], dma=True)
                for dh in range(2):
                    bw = (4 if t8 % 2 == 0 else 2) + dh
                    for nc32 in range(32):
                        p.op("pe", lambda e, nc32=nc32, t8=t8, dh=dh, bw=bw: e.matmul(
                            PB[bw][:, 0:512],
                            intfm[:, nc32 * 1024 + t8 * 128: nc32 * 1024 + t8 * 128 + 128],
                            wd[:, nc32 * D + dh * 512: nc32 * D + (dh + 1) * 512],
                            start=(nc32 == 0), stop=(nc32 == 31)),
                            reads=[("int", nc32, t8 // 4), ("w", "wd")],
                            writes=[("pb", bw)])
                    p.op("dve", lambda e, dh=dh, bw=bw: e.tensor_add(
                        s0t[:, dh * 512:(dh + 1) * 512], PB[bw][:, 0:512],
                        xmt[:, dh * 512:(dh + 1) * 512]),
                        reads=[("pb", bw), ("a3xm",)], writes=[("a3s0", dh)])
                p.op("sp", lambda e, b=b, r0=r0: e.dma_start(
                    out=io["s0"][b, r0:r0 + 128, :], in_=s0t[:]),
                    reads=[("a3s0", 0), ("a3s0", 1)],
                    writes=[("d_s0", b, r0 // 128)], dma=True)
    p.barrier()



W_SHAPES_A = {
    "L0_attn_Wq": (D, KD), "L0_attn_Wk": (D, KD), "L0_attn_Wv": (D, VD),
    "L0_attn_Wb": (D, 8), "L0_attn_Wa": (D, 8),
    "L0_attn_A_log": (1, 8), "L0_attn_dt_bias": (1, 8),
    "L0_attn_conv_q_w": (KD, 4), "L0_attn_conv_k_w": (KD, 4),
    "L0_attn_conv_v_w": (VD, 4),
    "L0_attn_Wg": (D, VD), "L0_attn_onorm_w": (1, DV),
    "L0_attn_Wo": (VD, D), "L0_attn_norm1_w": (1, D), "L0_attn_norm2_w": (1, D),
    "L0_attn_Wgate": (D, IFF), "L0_attn_Wup": (D, IFF), "L0_attn_Wdown": (IFF, D),
}


def _mk_ctx(nc, es):
    p = Prog(nc)
    ctx = Ctx(nc, p, es)
    ctx.init_psum()
    ctx.init_sems()
    return ctx


def _finish(ctx, t, final_keys):
    ctx.p.op("dve", lambda e: e.memset(t["consts"][:, 7:8], 7.0),
             reads=final_keys, writes=[("fin",)])


def _emit_all(ctx, blk):
    ctx.p.emit(blk, ctx.sems)


def build_launch_A(phases=(0, 1, 2, 3), debug_out=()):
    from contextlib import ExitStack
    es = ExitStack()
    nc = bass.Bass("TRN2", target_bir_lowering=False)
    io = {}
    io["x"] = nc.dram_tensor("x", [B_PER_CORE, T, D], F32, kind="ExternalInput").ap()
    for nm, shp in W_SHAPES_A.items():
        io[nm] = nc.dram_tensor(nm, list(shp), F32, kind="ExternalInput").ap()
    io["s0"] = nc.dram_tensor("s0", [B_PER_CORE, T, D], BF16, kind="ExternalOutput").ap()
    io["xn0_tm"] = nc.dram_tensor("xn0_tm", [B_PER_CORE, T, D], F32,
                                  kind="ExternalOutput").ap()
    io["xn0_fm"] = nc.dram_tensor("xn0_fm", [B_PER_CORE, D, T], BF16,
                                  kind="ExternalOutput").ap()
    internals = {"h_fm": ([B_PER_CORE, D, T], BF16),
                 "o_st": ([B_PER_CORE, T, VD], BF16),
                 "xmid": ([B_PER_CORE, T, D], F32),
                 "h2fm": ([B_PER_CORE, D, T], BF16)}
    for nm, (shp, dt) in internals.items():
        kind = "ExternalOutput" if nm in debug_out else "Internal"
        io[nm] = nc.dram_tensor(nm, shp, dt, kind=kind).ap()
    if "dbg" in debug_out:
        io["dbg"] = nc.dram_tensor("dbg", [256, 8192], BF16, kind="ExternalOutput").ap()
        io["dbg2"] = nc.dram_tensor("dbg2", [128, 256], F32, kind="ExternalOutput").ap()
        io["dbg3"] = nc.dram_tensor("dbg3", [128, 8 * VD], BF16, kind="ExternalOutput").ap()
    ctx = _mk_ctx(nc, es)
    blk = es.enter_context(nc.Block())
    t = setup_common(ctx)
    for ph, fn in ((0, emit_A0), (1, emit_A1), (2, emit_A2), (3, emit_A3)):
        if ph in phases:
            ctx.push()
            fn(ctx, t, io)
            ctx.pop()
    fk = []
    for b in range(B_PER_CORE):
        if 3 in phases:
            fk += [("d_s0", b, i) for i in range(NT)]
        fk += [("d_xn0_tm", b, i) for i in range(NT)]
        fk += [("d_xn0_fm", b, j, g) for j in range(8) for g in range(4)]
        if 1 in phases:
            fk += [("d_ost", b, i) for i in range(NT)]
        if 2 in phases:
            fk += [("d_xmid", b, i) for i in range(NT)]
    _finish(ctx, t, fk)
    _emit_all(ctx, blk)
    consts = ctx.const_inputs_used()
    return nc, es, consts, ctx


def prep_inputs_A(inputs, xs):
    m = {"x": np.ascontiguousarray(xs)}
    for nm, shp in W_SHAPES_A.items():
        m[nm] = np.ascontiguousarray(inputs[nm]).reshape(shp)
    return m



W_SHAPES_B = {
    "L0_ffn_Wq": (D, D), "L0_ffn_bq": (1, D), "L0_ffn_Wk": (D, D),
    "L0_ffn_bk": (1, D), "L0_ffn_Wv": (D, D), "L0_ffn_bv": (1, D),
    "L0_ffn_Wo": (D, D), "L0_ffn_bo": (1, D),
    "L0_ffn_g1": (1, D), "L0_ffn_be1": (1, D), "L0_ffn_g2": (1, D),
    "L0_ffn_be2": (1, D),
    "L0_ffn_Wg": (D, 8), "L0_ffn_bg": (1, 8),
    "L0_ffn_We": (8, D, D), "L0_ffn_bexp": (1, 8 * D),
}


def _repl_row(ctx, t, rowap, dst, ncols, rkey, wkey, bank=0):
    p = ctx.p
    PB = ctx.psum
    nseg = (ncols + 511) // 512
    for seg in range(nseg):
        c0 = seg * 512
        cw = min(512, ncols - c0)
        p.op("pe", lambda e, c0=c0, cw=cw: e.matmul(
            PB[bank][:, 0:cw], t["ones_r_f"][0:1, 0:128], rowap[0:1, c0:c0 + cw],
            start=True, stop=True),
            reads=list(rkey) + [("const", "o2")], writes=[("pb", bank)])
        p.op("dve", lambda e, c0=c0, cw=cw: e.tensor_copy(
            dst[:, c0:c0 + cw], PB[bank][:, 0:cw]),
            reads=[("pb", bank)], writes=[wkey])


def emit_B(ctx, t, io):
    p = ctx.p
    PB, PT = ctx.psum, ctx.psb

    wgm = ctx.sb("b_wgm", (128, 8 * 8), F32)
    p.op("sp", lambda e: e.dma_start(
        out=wgm[:], in_=io["L0_ffn_Wg"].rearrange("(c p) n -> p c n", p=128)),
        writes=[("w", "wgm")], dma=True)
    berow = ctx.sb("b_berow", (1, 8 * D), BF16)
    p.op("pool", lambda e: e.dma_start(out=berow[:], in_=io["L0_ffn_bexp"]),
         writes=[("w", "berow")], dma=True)
    reps = {}
    for nm in ("bv", "bo", "g1", "be1", "g2", "be2"):
        reps[nm] = ctx.sb("b_rep" + nm, (128, D), F32)
    bgrep = ctx.sb("b_repbg", (128, 8), F32)
    bqc = ctx.sb("b_bqc", (64, 16), F32)
    bkc = ctx.sb("b_bkc", (64, 16), F32)
    ctx.push()
    rows = {}
    for nm in ("bq", "bk", "bv", "bo", "g1", "be1", "g2", "be2"):
        rows[nm] = ctx.sb("b_r" + nm, (1, D), F32)
        p.op("sp", lambda e, nm=nm: e.dma_start(out=rows[nm][:],
                                                in_=io["L0_ffn_" + nm]),
             writes=[("row", nm)], dma=True)
    bgrow = ctx.sb("b_bgrow", (1, 8), F32)
    p.op("sp", lambda e: e.dma_start(out=bgrow[:], in_=io["L0_ffn_bg"]),
         writes=[("row", "bg")], dma=True)
    for nm in ("bv", "bo", "g1", "be1", "g2", "be2"):
        _repl_row(ctx, t, rows[nm][:], reps[nm], D, [("row", nm)], ("rep", nm))
    _repl_row(ctx, t, bgrow[:], bgrep, 8, [("row", "bg")], ("rep", "bg"))
    for nm, dst in (("bq", bqc), ("bk", bkc)):
        for h in range(16):
            p.op("pe", lambda e, nm=nm, h=h: e.transpose(
                PB[0][0:64, 0:1], rows[nm][0:1, h * 64:(h + 1) * 64],
                t["ident_f"][0:1, 0:1]),
                reads=[("row", nm), ("const", "c_ident_f")], writes=[("pb", 0)])
            p.op("dve", lambda e, dst=dst, h=h: e.tensor_copy(
                dst[:, h:h + 1], PB[0][0:64, 0:1]),
                reads=[("pb", 0)], writes=[("bc", nm, h)])

    p.barrier()
    ctx.pop()
    vtm = ctx.sb("b_vtm", (128, NT * 16 * 65), BF16)
    ctxb = ctx.sb("b_ctxb", (128, NT * D), BF16)
    cmbb = ctx.sb("b_cmb", (128, NT * 8), F32)
    st = ctx.sb("b_st", (128, 24), F32)
    gl = ctx.sb("b_gl", (128, 8), F32)
    glx = ctx.sb("b_glx", (128, 8), F32)
    e1t = ctx.sb("b_e1", (128, 8), F32)
    e2t = ctx.sb("b_e2", (128, 8), F32)
    sq = ctx.sb("b_sq", (128, D), F32)
    at = ctx.sb("b_at", (128, D), F32)
    xm0 = ctx.sb("b_xm0", (128, D), F32)
    xmb = ctx.sb("b_xmb", (128, D), BF16)
    xnt = ctx.sb("b_xnt", (128, D), F32)

    for b in range(B_PER_CORE):
        ctx.push()
        wvf = ctx.sb("b_wvf", (128, 8 * D), BF16)
        xnw = ctx.sb("b_xnw", (128, 8 * 512), BF16)
        p.op("pool", lambda e: e.dma_start(
            out=wvf[:], in_=io["L0_ffn_Wv"].rearrange("(c p) n -> p c n", p=128)),
            writes=[("w", "wv", b)], dma=True)
        p.op("dve", lambda e: e.memset(
            vtm.rearrange("p (n h c) -> p n h c", h=16, c=65)[:, :, :, 64:65], 1.0),
            writes=[("vone",)])
        for grp in range(4):
            for dc in range(8):
                p.op("sp", lambda e, b=b, dc=dc, grp=grp: e.dma_start(
                    out=xnw[:, dc * 512:(dc + 1) * 512],
                    in_=io["xn0_fm"][b, dc * 128:(dc + 1) * 128,
                                     grp * 512:(grp + 1) * 512]),
                    writes=[("xnw", dc)], dma=True)
            for t4 in range(4):
                tt = grp * 4 + t4
                for half in range(2):
                    bk = half
                    for dc in range(8):
                        p.op("pe", lambda e, dc=dc, t4=t4, half=half, bk=bk: e.matmul(
                            PB[bk][:, 0:512],
                            xnw[:, dc * 512 + t4 * 128: dc * 512 + t4 * 128 + 128],
                            wvf[:, dc * D + half * 512: dc * D + (half + 1) * 512],
                            start=(dc == 0), stop=(dc == 7)),
                            reads=[("xnw", dc), ("w", "wv", b)],
                            writes=[("pb", bk)])
                    dst = vtm.rearrange("p (n h c) -> p n h c", h=16, c=65)[
                        :, tt, half * 8:(half + 1) * 8, 0:64]
                    src = PB[bk].rearrange("p (h c) -> p h c", c=64)
                    brep = reps["bv"].rearrange("p (h c) -> p h c", c=64)[
                        :, half * 8:(half + 1) * 8, :]
                    p.op("dve", lambda e, dst=dst, src=src, brep=brep: e.tensor_add(
                        dst, src, brep),
                        reads=[("pb", bk), ("rep", "bv"), ("vone",)],
                        writes=[("vtm", b, tt, half)])
        p.barrier()
        ctx.pop()
        ctx.push()
        qfh = [ctx.sb("b_qfh", (64, T), BF16) for _ in range(2)]
        kfh = [ctx.sb("b_kfh", (64, T), BF16) for _ in range(2)]
        ebuf = ctx.sb("b_ebuf", (128, 16 * 512), BF16)
        xnr = ctx.sb("b_xnr", (128, 8 * T), BF16)
        wqk = [ctx.sb("b_wqk", (128, 8 * 128), BF16) for _ in range(2)]
        for dc in range(8):
            p.op("sp", lambda e, b=b, dc=dc: e.dma_start(
                out=xnr[:, dc * T:(dc + 1) * T],
                in_=io["xn0_fm"][b, dc * 128:(dc + 1) * 128, :]),
                writes=[("xnr", dc)], dma=True)
        for h in range(16):
            hp = h % 2
            wsb = wqk[hp]
            for which, off, wnm in (("q", 0, "L0_ffn_Wq"),
                                    ("k", 64, "L0_ffn_Wk")):
                p.op("pool", lambda e, wsb=wsb, off=off, wnm=wnm, h=h:
                     e.dma_start(
                         out=wsb.rearrange("p (c k) -> p c k", k=128)[
                             :, :, off:off + 64],
                         in_=io[wnm].rearrange("(c p) n -> p c n", p=128)[
                             :, :, h * 64:(h + 1) * 64]),
                     writes=[("ws", which, hp)], dma=True)
            for grp in range(4):
                bk = 2 + (grp % 2)
                for dc in range(8):
                    p.op("pe", lambda e, wsb=wsb, dc=dc, grp=grp, bk=bk: e.matmul(
                        PB[bk][0:128, 0:512], wsb[:, dc * 128:(dc + 1) * 128],
                        xnr[:, dc * T + grp * 512: dc * T + (grp + 1) * 512],
                        start=(dc == 0), stop=(dc == 7)),
                        reads=[("xnr", dc), ("ws", "q", hp), ("ws", "k", hp)],
                        writes=[("pb", bk)])
                p.op("dve", lambda e, grp=grp, h=h, bk=bk, hp=hp:
                     e.tensor_scalar(
                         qfh[hp][:, grp * 512:(grp + 1) * 512],
                         PB[bk][0:64, 0:512],
                         bqc[:, h:h + 1], None, ALU.add),
                     reads=[("pb", bk), ("bc", "bq", h)],
                     writes=[("qk", "q", grp, hp)])
                p.op("dve", lambda e, grp=grp, h=h, bk=bk, hp=hp:
                     e.tensor_scalar(
                         kfh[hp][:, grp * 512:(grp + 1) * 512],
                         PB[bk][64:128, 0:512],
                         bkc[:, h:h + 1], None, ALU.add),
                     reads=[("pb", bk), ("bc", "bk", h)],
                     writes=[("qk", "k", grp, hp)])
            for qb in range(4):
                for kt in range(16):
                    bk = 4 + (kt % 2)
                    p.op("pe", lambda e, kt=kt, qb=qb, bk=bk, hp=hp: e.matmul(
                        PB[bk][:, 0:512], kfh[hp][0:64, kt * 128:(kt + 1) * 128],
                        qfh[hp][0:64, qb * 512:(qb + 1) * 512], start=True,
                        stop=True),
                        reads=[("qk", "q", qb, hp), ("qk", "k", kt // 4, hp)],
                        writes=[("pb", bk)])
                    p.op("act", lambda e, kt=kt, bk=bk: e.activation(
                        ebuf[:, kt * 512:(kt + 1) * 512], PB[bk][:, 0:512],
                        AF.Exp, scale=0.125),
                        reads=[("pb", bk)], writes=[("eb", kt)])
                for qc in range(4):
                    q0 = qb * 512 + qc * 128
                    bk = qc % 2
                    for kt in range(16):
                        rhs = vtm[:, kt * (16 * 65) + h * 65:
                                  kt * (16 * 65) + h * 65 + 65]
                        p.op("pe", lambda e, kt=kt, qc=qc, rhs=rhs, bk=bk: e.matmul(
                            PB[bk][:, 0:65], ebuf[:, kt * 512 + qc * 128:
                                                 kt * 512 + qc * 128 + 128],
                            rhs, start=(kt == 0), stop=(kt == 15)),
                            reads=[("eb", kt)] +
                            [("vtm", b, kt, hf) for hf in range(2)] + [("vone",)],
                            writes=[("pb", bk)])
                    p.op("dve", lambda e, qc=qc, bk=bk: e.reciprocal(
                        st[:, 16 + qc:17 + qc], PB[bk][:, 64:65]),
                        reads=[("pb", bk)], writes=[("brc", qc)])
                    p.op("dve", lambda e, q0=q0, h=h, qc=qc, bk=bk: e.tensor_scalar(
                        ctxb[:, (q0 // 128) * D + h * 64:
                             (q0 // 128) * D + h * 64 + 64],
                        PB[bk][:, 0:64], st[:, 16 + qc:17 + qc], None, ALU.mult),
                        reads=[("pb", bk), ("brc", qc)],
                        writes=[("ctx", q0 // 128, h)])
        p.barrier()
        ctx.pop()
        ctx.push()
        wof = ctx.sb("b_wof", (128, 8 * D), BF16)
        cfm = ctx.sb("b_cfm", (128, 8 * 128), BF16)
        xmffm = ctx.sb("b_xmffm", (128, 8 * 128), F32)
        p.op("pool", lambda e: e.dma_start(
            out=wof[:], in_=io["L0_ffn_Wo"].rearrange("(c p) n -> p c n", p=128)),
            writes=[("w", "wo", b)], dma=True)
        for tt in range(NT):
            r0 = tt * 128
            for j in range(8):
                p.op("pe", lambda e, tt=tt, j=j: e.transpose(
                    PT[0][:, 0:128], ctxb[:, tt * D + j * 128: tt * D + (j + 1) * 128],
                    t["ident_bf"]),
                    reads=[("ctx", tt, 2 * j), ("ctx", tt, 2 * j + 1),
                           ("const", "c_ident_bf")],
                    writes=[("pt", 0)])
                p.op("dve", lambda e, j=j: e.tensor_copy(
                    cfm[:, j * 128:(j + 1) * 128], PT[0][:, 0:128]),
                    reads=[("pt", 0)], writes=[("cfm", j)])
            for half in range(2):
                for dc in range(8):
                    p.op("pe", lambda e, dc=dc, half=half: e.matmul(
                        PB[half][:, 0:512], cfm[:, dc * 128:(dc + 1) * 128],
                        wof[:, dc * D + half * 512: dc * D + (half + 1) * 512],
                        start=(dc == 0), stop=(dc == 7)),
                        reads=[("cfm", dc), ("w", "wo", b)],
                        writes=[("pb", half)])
                p.op("dve", lambda e, half=half: e.tensor_add(
                    at[:, half * 512:(half + 1) * 512], PB[half][:, 0:512],
                    reps["bo"][:, half * 512:(half + 1) * 512]),
                    reads=[("pb", half), ("rep", "bo")], writes=[("at", half)])
            p.op("dve", lambda e: e.reduce_sum(st[:, 0:1], at[:], AX.X),
                 reads=[("at", 0), ("at", 1)], writes=[("st", 0)])
            p.op("act", lambda e: e.activation(sq[:], at[:], AF.Square,
                                               accum_out=st[:, 1:2]),
                 reads=[("at", 0), ("at", 1)], writes=[("st", 1), ("bsq",)])
            p.op("dve", lambda e: e.tensor_scalar(
                st[:, 2:3], st[:, 0:1], -1.0 / D, None, ALU.mult),
                reads=[("st", 0)], writes=[("st", 2)])
            p.op("dve", lambda e: e.tensor_scalar(
                st[:, 3:4], st[:, 1:2], 1.0 / D, None, ALU.mult),
                reads=[("st", 1)], writes=[("st", 3)])
            p.op("dve", lambda e: e.scalar_tensor_tensor(
                st[:, 4:5], st[:, 2:3], 1.0, st[:, 2:3], ALU.mult, ALU.mult),
                reads=[("st", 2)], writes=[("st", 4)])
            p.op("dve", lambda e: e.tensor_sub(st[:, 5:6], st[:, 3:4], st[:, 4:5]),
                 reads=[("st", 3), ("st", 4)], writes=[("st", 5)])
            p.op("act", lambda e: e.activation(st[:, 6:7], st[:, 5:6], AF.Ln,
                                               bias=t["eps5"], scale=1.0),
                 reads=[("st", 5), ("const", "c1")], writes=[("st", 6)])
            p.op("act", lambda e: e.activation(st[:, 7:8], st[:, 6:7], AF.Exp,
                                               scale=-0.5),
                 reads=[("st", 6)], writes=[("st", 7)])
            p.op("dve", lambda e: e.tensor_scalar(
                at[:], at[:], st[:, 2:3], st[:, 7:8], ALU.add, ALU.mult),
                reads=[("at", 0), ("at", 1), ("st", 2), ("st", 7)],
                writes=[("atn",)])
            p.op("dve", lambda e: e.tensor_mul(at[:], at[:], reps["g1"][:]),
                 reads=[("atn",), ("rep", "g1")], writes=[("atn2",)])
            p.op("dve", lambda e: e.tensor_add(at[:], at[:], reps["be1"][:]),
                 reads=[("atn2",), ("rep", "be1")], writes=[("atn3",)])
            p.op("sp", lambda e, b=b, r0=r0: e.dma_start(
                out=xnt[:], in_=io["xn0_tm"][b, r0:r0 + 128, :]),
                writes=[("bxn",)], dma=True)
            p.op("dve", lambda e: e.tensor_add(xm0[:], at[:], xnt[:]),
                 reads=[("atn3",), ("bxn",)], writes=[("xm0",)])
            p.op("dve", lambda e: e.tensor_copy(xmb[:], xm0[:]),
                 reads=[("xm0",)], writes=[("xmb",)])
            for j in range(8):
                p.op("pe", lambda e, j=j: e.transpose(
                    PT[1][:, 0:128], xmb[:, j * 128:(j + 1) * 128], t["ident_bf"]),
                    reads=[("xmb",), ("const", "c_ident_bf")], writes=[("pt", 1)])
                p.op("dve", lambda e, j=j: e.tensor_copy(
                    cfm[:, j * 128:(j + 1) * 128], PT[1][:, 0:128]),
                    reads=[("pt", 1)], writes=[("cfm", j)])
                p.op("sp", lambda e, b=b, j=j, tt=tt: e.dma_start(
                    out=io["xm0fm"][b, j * 128:(j + 1) * 128,
                                    tt * 128:(tt + 1) * 128],
                    in_=cfm[:, j * 128:(j + 1) * 128]),
                    reads=[("cfm", j)], writes=[("d_xm0fm", b, j, tt)], dma=True)
                p.op("pe", lambda e, j=j: e.transpose(
                    PB[2][:, 0:128], xm0[:, j * 128:(j + 1) * 128], t["ident_f"]),
                    reads=[("xm0",), ("const", "c_ident_f")], writes=[("pb", 2)])
                p.op("act", lambda e, j=j: e.activation(
                    xmffm[:, j * 128:(j + 1) * 128], PB[2][:, 0:128], AF.Copy),
                    reads=[("pb", 2)], writes=[("xmffm", j)])
            for dc in range(8):
                p.op("pe", lambda e, dc=dc: e.matmul(
                    PB[3][:, 0:8], xmffm[:, dc * 128:(dc + 1) * 128],
                    wgm[:, dc * 8:(dc + 1) * 8], start=(dc == 0), stop=(dc == 7)),
                    reads=[("xmffm", dc), ("w", "wgm")], writes=[("pb", 3)])
            p.op("dve", lambda e: e.tensor_add(gl[:], PB[3][:, 0:8], bgrep[:]),
                 reads=[("pb", 3), ("rep", "bg")], writes=[("gl",)])
            p.op("dve", lambda e: e.reduce_max(st[:, 8:9], gl[:], AX.X),
                 reads=[("gl",)], writes=[("st", 8)])
            p.op("dve", lambda e: e.tensor_scalar(
                st[:, 9:10], st[:, 8:9], -1.0, None, ALU.mult),
                reads=[("st", 8)], writes=[("st", 9)])
            p.op("act", lambda e: e.activation(glx[:], gl[:], AF.Exp,
                                               bias=st[:, 9:10],
                                               accum_out=st[:, 10:11]),
                 reads=[("gl",), ("st", 9)], writes=[("glx",), ("st", 10)])
            p.op("dve", lambda e: e.reciprocal(st[:, 11:12], st[:, 10:11]),
                 reads=[("st", 10)], writes=[("st", 11)])
            p.op("dve", lambda e: e.tensor_scalar(
                glx[:], glx[:], st[:, 11:12], None, ALU.mult),
                reads=[("glx",), ("st", 11)], writes=[("probs",)])
            p.op("dve", lambda e: e.reduce_max(st[:, 12:13], glx[:], AX.X),
                 reads=[("probs",)], writes=[("st", 12)])
            p.op("dve", lambda e: e.tensor_scalar(
                e1t[:], glx[:], st[:, 12:13], None, ALU.is_ge),
                reads=[("probs",), ("st", 12)], writes=[("e1",)])
            p.op("dve", lambda e: e.scalar_tensor_tensor(
                e2t[:], e1t[:], -1e9, glx[:], ALU.mult, ALU.add),
                reads=[("e1",), ("probs",)], writes=[("glm",)])
            p.op("dve", lambda e: e.reduce_max(st[:, 13:14], e2t[:], AX.X),
                 reads=[("glm",)], writes=[("st", 13)])
            p.op("dve", lambda e: e.tensor_scalar(
                e2t[:], e2t[:], st[:, 13:14], None, ALU.is_ge),
                reads=[("glm",), ("st", 13)], writes=[("e2",)])
            p.op("dve", lambda e: e.tensor_scalar(
                e1t[:], e1t[:], st[:, 12:13], None, ALU.mult),
                reads=[("e1",), ("st", 12)], writes=[("e1v",)])
            p.op("dve", lambda e, tt=tt: e.scalar_tensor_tensor(
                cmbb[:, tt * 8:(tt + 1) * 8], e2t[:], st[:, 13:14], e1t[:],
                ALU.mult, ALU.add),
                reads=[("e2",), ("st", 13), ("e1v",)], writes=[("cmb", tt)])
            p.op("sp", lambda e, b=b, r0=r0: e.dma_start(
                out=io["xmid0"][b, r0:r0 + 128, :], in_=xm0[:]),
                reads=[("xm0",)], writes=[("d_xmid0", b, tt)], dma=True)
        p.barrier()
        ctx.pop()
        ctx.push()
        xmbfm = ctx.sb("b_xmbfm", (128, 8 * 1024), BF16)
        accb = ctx.sb("b_acc", (128, 8 * D), BF16)
        wet = [ctx.sb("b_wet", (128, 8 * D), BF16) for _ in range(2)]
        for hp in range(2):
            t0h = hp * 8
            for dc in range(8):
                p.op("sp", lambda e, b=b, dc=dc, hp=hp: e.dma_start(
                    out=xmbfm[:, dc * 1024:(dc + 1) * 1024],
                    in_=io["xm0fm"][b, dc * 128:(dc + 1) * 128,
                                    hp * 1024:(hp + 1) * 1024]),
                    reads=[("d_xm0fm", b, dc, t0h + i) for i in range(8)],
                    writes=[("xmw", dc)], dma=True)
            p.op("dve", lambda e: e.memset(accb[:], 0.0),
                 writes=[("acc", i, hf) for i in range(8) for hf in range(2)])
            if hp == 0:
                p.op("pool", lambda e: e.dma_start(
                    out=wet[0][:],
                    in_=io["L0_ffn_We"][0].rearrange("(c p) n -> p c n", p=128)),
                    writes=[("wet", 0)], dma=True)
            for ee in range(8):
                par = ee % 2
                if ee + 1 < 8:
                    p.op("pool", lambda e, ee=ee: e.dma_start(
                        out=wet[(ee + 1) % 2][:],
                        in_=io["L0_ffn_We"][ee + 1].rearrange(
                            "(c p) n -> p c n", p=128)),
                        writes=[("wet", (ee + 1) % 2)], dma=True)
                elif hp == 0:
                    p.op("pool", lambda e: e.dma_start(
                        out=wet[1 if False else (8 % 2)][:],
                        in_=io["L0_ffn_We"][0].rearrange("(c p) n -> p c n", p=128)),
                        writes=[("wet", 0)], dma=True)
                for ti in range(8):
                    tt = t0h + ti
                    for half in range(2):
                        bk = (ti % 2) * 2 + half
                        for dc in range(8):
                            p.op("pe", lambda e, ti=ti, dc=dc, half=half, bk=bk,
                                 par=par: e.matmul(
                                     PB[bk][:, 0:512],
                                     xmbfm[:, dc * 1024 + ti * 128:
                                           dc * 1024 + ti * 128 + 128],
                                     wet[par][:, dc * D + half * 512:
                                              dc * D + (half + 1) * 512],
                                     start=(dc == 0), stop=False),
                                 reads=[("xmw", dc), ("wet", par)],
                                 writes=[("pb", bk)])
                        p.op("pe", lambda e, ee=ee, half=half, bk=bk: e.matmul(
                            PB[bk][:, 0:512], t["ones_r_bf"][0:1, 0:128],
                            berow[0:1, ee * D + half * 512:
                                  ee * D + (half + 1) * 512],
                            start=False, stop=True),
                            reads=[("w", "berow"), ("const", "o1")],
                            writes=[("pb", bk)])
                        p.op("dve", lambda e, ti=ti, tt=tt, half=half, bk=bk, ee=ee:
                             e.scalar_tensor_tensor(
                                 accb[:, ti * D + half * 512:
                                      ti * D + (half + 1) * 512],
                                 PB[bk][:, 0:512],
                                 cmbb[:, tt * 8 + ee: tt * 8 + ee + 1],
                                 accb[:, ti * D + half * 512:
                                      ti * D + (half + 1) * 512],
                                 ALU.mult, ALU.add),
                             reads=[("pb", bk), ("cmb", tt), ("acc", ti, half)],
                             writes=[("acc", ti, half)])
            for ti in range(8):
                tt = t0h + ti
                r0 = tt * 128
                accs = accb[:, ti * D:(ti + 1) * D]
                p.op("dve", lambda e, accs=accs: e.reduce_sum(st[:, 0:1], accs, AX.X),
                     reads=[("acc", ti, 0), ("acc", ti, 1)], writes=[("st", 0)])
                p.op("act", lambda e, accs=accs: e.activation(
                    sq[:], accs, AF.Square, accum_out=st[:, 1:2]),
                    reads=[("acc", ti, 0), ("acc", ti, 1)],
                    writes=[("st", 1), ("bsq",)])
                p.op("dve", lambda e: e.tensor_scalar(
                    st[:, 2:3], st[:, 0:1], -1.0 / D, None, ALU.mult),
                    reads=[("st", 0)], writes=[("st", 2)])
                p.op("dve", lambda e: e.tensor_scalar(
                    st[:, 3:4], st[:, 1:2], 1.0 / D, None, ALU.mult),
                    reads=[("st", 1)], writes=[("st", 3)])
                p.op("dve", lambda e: e.scalar_tensor_tensor(
                    st[:, 4:5], st[:, 2:3], 1.0, st[:, 2:3], ALU.mult, ALU.mult),
                    reads=[("st", 2)], writes=[("st", 4)])
                p.op("dve", lambda e: e.tensor_sub(st[:, 5:6], st[:, 3:4], st[:, 4:5]),
                     reads=[("st", 3), ("st", 4)], writes=[("st", 5)])
                p.op("act", lambda e: e.activation(st[:, 6:7], st[:, 5:6], AF.Ln,
                                                   bias=t["eps5"], scale=1.0),
                     reads=[("st", 5), ("const", "c1")], writes=[("st", 6)])
                p.op("act", lambda e: e.activation(st[:, 7:8], st[:, 6:7], AF.Exp,
                                                   scale=-0.5),
                     reads=[("st", 6)], writes=[("st", 7)])
                p.op("dve", lambda e, accs=accs: e.tensor_scalar(
                    at[:], accs, st[:, 2:3], st[:, 7:8], ALU.add, ALU.mult),
                    reads=[("acc", ti, 0), ("acc", ti, 1), ("st", 2), ("st", 7)],
                    writes=[("atn",)])
                p.op("dve", lambda e: e.tensor_mul(at[:], at[:], reps["g2"][:]),
                     reads=[("atn",), ("rep", "g2")], writes=[("atn2",)])
                p.op("dve", lambda e: e.tensor_add(at[:], at[:], reps["be2"][:]),
                     reads=[("atn2",), ("rep", "be2")], writes=[("atn3",)])
                p.op("sp", lambda e, b=b, r0=r0: e.dma_start(
                    out=xm0[:], in_=io["xmid0"][b, r0:r0 + 128, :]),
                    reads=[("d_xmid0", b, tt)], writes=[("xm0",)], dma=True)
                p.op("dve", lambda e: e.tensor_add(at[:], at[:], xm0[:]),
                     reads=[("atn3",), ("xm0",)], writes=[("f0",)])
                p.op("act", lambda e: e.activation(at[:], at[:], AF.Silu),
                     reads=[("f0",)], writes=[("g0t",)])
                p.op("dve", lambda e: e.tensor_copy(xmb[:], at[:]),
                     reads=[("g0t",)], writes=[("g0b",)])
                p.op("sp", lambda e, b=b, r0=r0: e.dma_start(
                    out=io["g0"][b, r0:r0 + 128, :], in_=xmb[:]),
                    reads=[("g0b",)], writes=[("d_g0", b, tt)], dma=True)
        p.barrier()
        ctx.pop()
    p.barrier()


def build_launch_B():
    from contextlib import ExitStack
    es = ExitStack()
    nc = bass.Bass("TRN2", target_bir_lowering=False)
    io = {}
    io["xn0_tm"] = nc.dram_tensor("xn0_tm", [B_PER_CORE, T, D], F32,
                                  kind="ExternalInput").ap()
    io["xn0_fm"] = nc.dram_tensor("xn0_fm", [B_PER_CORE, D, T], BF16,
                                  kind="ExternalInput").ap()
    for nm, shp in W_SHAPES_B.items():
        io[nm] = nc.dram_tensor(nm, list(shp), F32, kind="ExternalInput").ap()
    io["g0"] = nc.dram_tensor("g0", [B_PER_CORE, T, D], BF16,
                              kind="ExternalOutput").ap()
    io["xmid0"] = nc.dram_tensor("xmid0", [B_PER_CORE, T, D], F32,
                                 kind="Internal").ap()
    io["xm0fm"] = nc.dram_tensor("xm0fm", [B_PER_CORE, D, T], BF16,
                                 kind="Internal").ap()
    ctx = _mk_ctx(nc, es)
    blk = es.enter_context(nc.Block())
    t = setup_common(ctx)
    ctx.push()
    emit_B(ctx, t, io)
    ctx.pop()
    fk = []
    for b in range(B_PER_CORE):
        fk += [("d_g0", b, i) for i in range(NT)]
    _finish(ctx, t, fk)
    _emit_all(ctx, blk)
    return nc, es, ctx.const_inputs_used(), ctx


def prep_inputs_B(inputs, xn0_tm, xn0_fm):
    m = {"xn0_tm": xn0_tm, "xn0_fm": xn0_fm}
    for nm, shp in W_SHAPES_B.items():
        m[nm] = np.ascontiguousarray(inputs[nm]).reshape(shp)
    return m



def emit_C(ctx, t, io):
    p = ctx.p
    PB, PT = ctx.psum, ctx.psb
    wg1 = ctx.sb("c_wg1", (128, 8 * 8), F32)
    p.op("sp", lambda e: e.dma_start(
        out=wg1[:], in_=io["L1_ffn_gate_w"].rearrange("(c p) n -> p c n", p=128)),
        writes=[("w", "wg1")], dma=True)
    xt = ctx.sb("c_x", (128, D), F32)
    s0t = ctx.sb("c_s0", (128, D), BF16)
    g0t = ctx.sb("c_g0", (128, D), BF16)
    x1t = ctx.sb("c_x1", (128, D), F32)
    xnt = ctx.sb("c_xn", (128, D), F32)
    xnb = ctx.sb("c_xnb", (128, D), BF16)
    xnffm = ctx.sb("c_xnffm", (128, 8 * 128), F32)
    sgt = ctx.sb("c_sg", (128, D), F32)
    st = ctx.sb("c_st", (128, 16), F32)
    glt = ctx.sb("c_gl", (128, 8), F32)
    stage = ctx.sb("c_stage", (128, 8 * 512), BF16)

    for b in range(B_PER_CORE):
        for grp in range(4):
            for t4 in range(4):
                tt = grp * 4 + t4
                r0 = tt * 128
                p.op("sp", lambda e, b=b, r0=r0: e.dma_start(
                    out=xt[:], in_=io["x"][b, r0:r0 + 128, :]),
                    writes=[("cx",)], dma=True)
                p.op("sp", lambda e, b=b, r0=r0: e.dma_start(
                    out=s0t[:], in_=io["s0"][b, r0:r0 + 128, :]),
                    writes=[("cs0",)], dma=True)
                p.op("sp", lambda e, b=b, r0=r0: e.dma_start(
                    out=g0t[:], in_=io["g0"][b, r0:r0 + 128, :]),
                    writes=[("cg0",)], dma=True)
                p.op("dve", lambda e: e.tensor_mul(sgt[:], s0t[:], g0t[:]),
                     reads=[("cs0",), ("cg0",)], writes=[("csg",)])
                p.op("dve", lambda e: e.tensor_add(x1t[:], xt[:], sgt[:]),
                     reads=[("cx",), ("csg",)], writes=[("cx1",)])
                p.op("sp", lambda e, b=b, r0=r0: e.dma_start(
                    out=io["x1_tm"][b, r0:r0 + 128, :], in_=x1t[:]),
                    reads=[("cx1",)], writes=[("d_x1", b, tt)], dma=True)
                p.op("dve", lambda e: e.reduce_sum(st[:, 0:1], x1t[:], AX.X),
                     reads=[("cx1",)], writes=[("st", 0)])
                p.op("act", lambda e: e.activation(sgt[:], x1t[:], AF.Square,
                                                   accum_out=st[:, 1:2]),
                     reads=[("cx1",)], writes=[("st", 1), ("csq",)])
                p.op("dve", lambda e: e.tensor_scalar(
                    st[:, 2:3], st[:, 0:1], -1.0 / D, None, ALU.mult),
                    reads=[("st", 0)], writes=[("st", 2)])
                p.op("dve", lambda e: e.tensor_scalar(
                    st[:, 3:4], st[:, 1:2], 1.0 / D, None, ALU.mult),
                    reads=[("st", 1)], writes=[("st", 3)])
                p.op("dve", lambda e: e.scalar_tensor_tensor(
                    st[:, 4:5], st[:, 2:3], 1.0, st[:, 2:3], ALU.mult, ALU.mult),
                    reads=[("st", 2)], writes=[("st", 4)])
                p.op("dve", lambda e: e.tensor_sub(st[:, 5:6], st[:, 3:4],
                                                   st[:, 4:5]),
                     reads=[("st", 3), ("st", 4)], writes=[("st", 5)])
                p.op("act", lambda e: e.activation(st[:, 6:7], st[:, 5:6], AF.Ln,
                                                   bias=t["eps5"], scale=1.0),
                     reads=[("st", 5), ("const", "c1")], writes=[("st", 6)])
                p.op("act", lambda e: e.activation(st[:, 7:8], st[:, 6:7], AF.Exp,
                                                   scale=-0.5),
                     reads=[("st", 6)], writes=[("st", 7)])
                p.op("dve", lambda e: e.tensor_scalar(
                    xnt[:], x1t[:], st[:, 2:3], st[:, 7:8], ALU.add, ALU.mult),
                    reads=[("cx1",), ("st", 2), ("st", 7)], writes=[("cxn",)])
                p.op("dve", lambda e: e.tensor_copy(xnb[:], xnt[:]),
                     reads=[("cxn",)], writes=[("cxnb",)])
                for j in range(8):
                    c0 = j * 512 + t4 * 128
                    p.op("pe", lambda e, j=j: e.transpose(
                        PT[0][:, 0:128], xnb[:, j * 128:(j + 1) * 128],
                        t["ident_bf"]),
                        reads=[("cxnb",), ("const", "c_ident_bf")],
                        writes=[("pt", 0)])
                    p.op("dve", lambda e, c0=c0: e.tensor_copy(
                        stage[:, c0:c0 + 128], PT[0][:, 0:128]),
                        reads=[("pt", 0)], writes=[("cstg", j, t4)])
                    p.op("pe", lambda e, j=j: e.transpose(
                        PB[2][:, 0:128], xnt[:, j * 128:(j + 1) * 128],
                        t["ident_f"]),
                        reads=[("cxn",), ("const", "c_ident_f")],
                        writes=[("pb", 2)])
                    p.op("act", lambda e, j=j: e.activation(
                        xnffm[:, j * 128:(j + 1) * 128], PB[2][:, 0:128], AF.Copy),
                        reads=[("pb", 2)], writes=[("cxnffm", j)])
                for dc in range(8):
                    p.op("pe", lambda e, dc=dc: e.matmul(
                        PB[3][:, 0:8], xnffm[:, dc * 128:(dc + 1) * 128],
                        wg1[:, dc * 8:(dc + 1) * 8], start=(dc == 0), stop=(dc == 7)),
                        reads=[("cxnffm", dc), ("w", "wg1")], writes=[("pb", 3)])
                p.op("dve", lambda e: e.tensor_copy(glt[:], PB[3][:, 0:8]),
                     reads=[("pb", 3)], writes=[("cgl",)])
                p.op("sp", lambda e, b=b, r0=r0: e.dma_start(
                    out=io["glog"][b, r0:r0 + 128, :], in_=glt[:]),
                    reads=[("cgl",)], writes=[("d_glog", b, tt)], dma=True)
            for j in range(8):
                cc = grp * 512
                p.op("sp", lambda e, b=b, j=j, cc=cc: e.dma_start(
                    out=io["xn1_fm"][b, j * 128:(j + 1) * 128, cc:cc + 512],
                    in_=stage[:, j * 512:(j + 1) * 512]),
                    reads=[("cstg", j, 0), ("cstg", j, 1), ("cstg", j, 2),
                           ("cstg", j, 3)],
                    writes=[("d_xn1fm", b, j, grp)], dma=True)
    p.barrier()
    ctx.push()
    wq1 = ctx.sb("c_wq1", (128, 8 * D), BF16)
    wk1 = ctx.sb("c_wk1", (128, 8 * D), BF16)
    wv1 = ctx.sb("c_wv1", (128, 8 * D), BF16)
    for nm, sb in (("L1_attn_Wq", wq1), ("L1_attn_Wk", wk1), ("L1_attn_Wv", wv1)):
        p.op("pool", lambda e, sb=sb, nm=nm: e.dma_start(
            out=sb[:], in_=io[nm].rearrange("(c p) n -> p c n", p=128)),
            writes=[("w", nm)], dma=True)
    xnw = ctx.sb("c_xnw", (128, 8 * 512), BF16)
    for b in range(B_PER_CORE):
        for grp in range(4):
            for dc in range(8):
                p.op("sp", lambda e, b=b, dc=dc, grp=grp: e.dma_start(
                    out=xnw[:, dc * 512:(dc + 1) * 512],
                    in_=io["xn1_fm"][b, dc * 128:(dc + 1) * 128,
                                     grp * 512:(grp + 1) * 512]),
                    reads=[("d_xn1fm", b, dc, grp)], writes=[("cxnw", dc)], dma=True)
            for nm, wsb, onm in (("L1_attn_Wq", wq1, "q1fm"), ("L1_attn_Wk", wk1, "k1fm")):
                for fc in range(8):
                    for dc in range(8):
                        p.op("pe", lambda e, wsb=wsb, fc=fc, dc=dc: e.matmul(
                            PB[fc % 2][:, 0:512],
                            wsb[:, dc * D + fc * 128: dc * D + fc * 128 + 128],
                            xnw[:, dc * 512:(dc + 1) * 512],
                            start=(dc == 0), stop=(dc == 7)),
                            reads=[("cxnw", dc), ("w", nm)],
                            writes=[("pb", fc % 2)])
                    p.op("dve" if fc % 2 == 0 else "act",
                         (lambda e, fc=fc: e.tensor_copy(
                             stage[:, 0:512], PB[fc % 2][:, 0:512]))
                         if fc % 2 == 0 else
                         (lambda e, fc=fc: e.activation(
                             stage[:, 0:512], PB[fc % 2][:, 0:512], AF.Copy)),
                         reads=[("pb", fc % 2)], writes=[("cqkst",)])
                    p.op("sp", lambda e, b=b, fc=fc, grp=grp, onm=onm: e.dma_start(
                        out=io[onm][b, fc * 128:(fc + 1) * 128,
                                    grp * 512:(grp + 1) * 512],
                        in_=stage[:, 0:512]),
                        reads=[("cqkst",)], writes=[("d_" + onm, b, fc, grp)],
                        dma=True)
            for t4 in range(4):
                tt = grp * 4 + t4
                for half in range(2):
                    bk = 2 + half
                    for dc in range(8):
                        p.op("pe", lambda e, dc=dc, t4=t4, half=half, bk=bk: e.matmul(
                            PB[bk][:, 0:512],
                            xnw[:, dc * 512 + t4 * 128: dc * 512 + t4 * 128 + 128],
                            wv1[:, dc * D + half * 512: dc * D + (half + 1) * 512],
                            start=(dc == 0), stop=(dc == 7)),
                            reads=[("cxnw", dc), ("w", "L1_attn_Wv")],
                            writes=[("pb", bk)])
                    p.op("dve", lambda e, half=half, bk=bk: e.tensor_copy(
                        stage[:, 1024 + half * 512:1024 + (half + 1) * 512],
                        PB[bk][:, 0:512]),
                        reads=[("pb", bk)], writes=[("cvst", half)])
                p.op("sp", lambda e, b=b, tt=tt: e.dma_start(
                    out=io["v1tm"][b, tt * 128:(tt + 1) * 128, :],
                    in_=stage[:, 1024:2048]),
                    reads=[("cvst", 0), ("cvst", 1)],
                    writes=[("d_v1tm", b, tt)], dma=True)
    p.barrier()
    ctx.pop()



def emit_D(ctx, t, io):
    p = ctx.p
    PB, PT = ctx.psum, ctx.psb
    gt_em = ctx.sb("d_gtem", (8, T), F32)
    grepb = ctx.sb("d_grepb", (128, T), BF16)
    grow = ctx.sb("d_grow", (1, T), F32)
    grep = ctx.sb("d_grep", (128, T), F32)
    gates_tm = ctx.sb("d_gtm", (128, NT * 8), F32)
    ranks = ctx.sb("d_ranks", (128, NT * 8), F32)
    selb = ctx.sb("d_sel", (128, NT * 8), F32)
    rdump = ctx.sb("d_rdump", (128, T), BF16)
    rdump2 = ctx.sb("d_rdump2", (128, T), BF16)
    glt = ctx.sb("d_gl", (128, 8), F32)
    cst = ctx.sb("d_cst", (8, 4), F32)
    rl = ctx.sb("d_rl", (1, T), F32)
    st = ctx.sb("d_st", (128, 16), F32)
    erep = ctx.sb("d_erep", (128, 8), F32)
    pick = ctx.sb("d_pick", (128, 8), F32)
    keyt = ctx.sb("d_key", (128, 8), F32)
    keym = ctx.sb("d_keym", (128, 8), F32)
    isw = ctx.sb("d_isw", (128, 8), F32)
    ov5 = ctx.sb("d_ov5", (128, 8), F32)
    for ee in range(8):
        p.op("dve", lambda e, ee=ee: e.memset(erep[:, ee:ee + 1], float(ee)),
             writes=[("erep",)])
    for b in range(B_PER_CORE):
        for tt in range(NT):
            r0 = tt * 128
            p.op("sp", lambda e, b=b, r0=r0: e.dma_start(
                out=glt[:], in_=io["glog"][b, r0:r0 + 128, :]),
                writes=[("dgl",)], dma=True)
            p.op("dve", lambda e: e.tensor_scalar(glt[:], glt[:], 1e-6, None,
                                                  ALU.max),
                 reads=[("dgl",)], writes=[("dglc",)])
            p.op("act", lambda e: e.activation(glt[:], glt[:], AF.Ln),
                 reads=[("dglc",)], writes=[("dgln",)])
            p.op("pe", lambda e: e.transpose(PB[2][0:8, 0:128], glt[:],
                                             t["ident_f"]),
                 reads=[("dgln",), ("const", "c_ident_f")], writes=[("pb", 2)])
            p.op("dve", lambda e, tt=tt: e.tensor_copy(
                gt_em[:, tt * 128:(tt + 1) * 128], PB[2][0:8, 0:128]),
                reads=[("pb", 2)], writes=[("gtem", tt)])
        rall = [("gtem", i) for i in range(NT)]
        for it in range(8):
            p.op("act", lambda e: e.activation(gt_em[:], gt_em[:], AF.Exp),
                 reads=rall, writes=rall)
            p.op("dve", lambda e: e.reduce_sum(cst[:, 0:1], gt_em[0:8, :], AX.X),
                 reads=rall, writes=[("cs0",)])
            p.op("dve", lambda e: e.reciprocal(cst[:, 2:3], cst[:, 0:1]),
                 reads=[("cs0",)], writes=[("cs2",)])
            p.op("dve", lambda e: e.tensor_scalar(
                gt_em[:], gt_em[:], cst[:, 2:3], None, ALU.mult),
                reads=rall + [("cs2",)], writes=rall)
            for blkq in range(4):
                p.op("pe", lambda e, blkq=blkq: e.matmul(
                    PB[2][0:1, 0:512], t["ones_c_f"][0:8, :],
                    gt_em[0:8, blkq * 512:(blkq + 1) * 512],
                    start=True, stop=True),
                    reads=rall + [("const", "o4")], writes=[("pb", 2)])
                p.op("act", lambda e, blkq=blkq: e.activation(
                    rl[0:1, blkq * 512:(blkq + 1) * 512], PB[2][0:1, 0:512],
                    AF.Ln),
                    reads=[("pb", 2)], writes=[("rl", blkq)])
            p.op("act", lambda e: e.activation(gt_em[:], gt_em[:], AF.Ln),
                 reads=rall, writes=rall)
            for blkq in range(4):
                p.op("pe", lambda e, blkq=blkq: e.matmul(
                    PB[2][0:8, 0:512], t["ones_r_f"][0:1, 0:8],
                    rl[0:1, blkq * 512:(blkq + 1) * 512], start=True, stop=True),
                    reads=[("rl", blkq), ("const", "o2")], writes=[("pb", 2)])
                p.op("dve", lambda e, blkq=blkq: e.tensor_sub(
                    gt_em[0:8, blkq * 512:(blkq + 1) * 512],
                    gt_em[0:8, blkq * 512:(blkq + 1) * 512], PB[2][0:8, 0:512]),
                    reads=rall + [("pb", 2)], writes=rall)
        p.op("act", lambda e: e.activation(gt_em[:], gt_em[:], AF.Exp),
             reads=rall, writes=rall)
        for tt in range(NT):
            p.op("pe", lambda e, tt=tt: e.transpose(
                PB[2][:, 0:8], gt_em[0:8, tt * 128:(tt + 1) * 128],
                t["ident_f"][0:8, 0:8]),
                reads=rall + [("const", "c_ident_f")], writes=[("pb", 2)])
            p.op("dve", lambda e, tt=tt: e.tensor_copy(
                gates_tm[:, tt * 8:(tt + 1) * 8], PB[2][:, 0:8]),
                reads=[("pb", 2)], writes=[("gtm", tt)])
        for ee in range(8):
            p.op("sp", lambda e, ee=ee: e.dma_start(
                out=grow[:], in_=gt_em[ee:ee + 1, :]),
                reads=rall, writes=[("grow",)], dma=True)
            for blkq in range(4):
                p.op("pe", lambda e, blkq=blkq: e.matmul(
                    PB[3][:, 0:512], t["ones_r_f"][0:1, 0:128],
                    grow[0:1, blkq * 512:(blkq + 1) * 512], start=True, stop=True),
                    reads=[("grow",), ("const", "o2")], writes=[("pb", 3)])
                p.op("dve", lambda e, blkq=blkq: e.tensor_copy(
                    grepb[:, blkq * 512:(blkq + 1) * 512], PB[3][:, 0:512]),
                    reads=[("pb", 3)], writes=[("grep", blkq)])
            for tt in range(NT):
                eng = "dve"
                rdmp = rdump
                p.op(eng, lambda e, tt=tt, ee=ee, rdmp=rdmp: e.tensor_scalar(
                    rdmp[:], grepb[:], gates_tm[:, tt * 8 + ee: tt * 8 + ee + 1],
                    None, ALU.is_gt, ALU.add,
                    accum_out=ranks[:, tt * 8 + ee: tt * 8 + ee + 1]),
                    reads=[("grep", 0), ("grep", 1), ("grep", 2), ("grep", 3),
                           ("gtm", tt)],
                    writes=[("rk", tt, ee), ("rdump", eng)])
        for tt in range(NT):
            rks = ranks[:, tt * 8:(tt + 1) * 8]
            gts = gates_tm[:, tt * 8:(tt + 1) * 8]
            p.op("dve", lambda e, rks=rks: e.tensor_scalar(
                pick[:], rks, 256.0, None, ALU.is_lt),
                reads=[("rk", tt, ee) for ee in range(8)], writes=[("pick",)])
            p.op("dve", lambda e, rks=rks: e.tensor_scalar(
                keyt[:], rks, 8.0, None, ALU.mult),
                reads=[("rk", tt, ee) for ee in range(8)], writes=[("key",)])
            p.op("dve", lambda e: e.tensor_add(keyt[:], keyt[:], erep[:]),
                 reads=[("key",), ("erep",)], writes=[("key2",)])
            p.op("dve", lambda e: e.tensor_mul(keym[:], keyt[:], pick[:]),
                 reads=[("key2",), ("pick",)], writes=[("km1",)])
            p.op("dve", lambda e: e.tensor_scalar(st[:, 0:8], pick[:], -1.0,
                                                  None, ALU.add),
                 reads=[("pick",)], writes=[("km2",)])
            p.op("dve", lambda e: e.tensor_add(keym[:], keym[:], st[:, 0:8]),
                 reads=[("km1",), ("km2",)], writes=[("km",)])
            p.op("dve", lambda e: e.reduce_max(st[:, 8:9], keym[:], AX.X),
                 reads=[("km",)], writes=[("wk",)])
            p.op("dve", lambda e: e.tensor_scalar(
                isw[:], keym[:], st[:, 8:9], None, ALU.is_ge),
                reads=[("km",), ("wk",)], writes=[("isw",)])
            p.op("dve", lambda e, gts=gts: e.tensor_scalar(
                ov5[:], gts, 0.5, None, ALU.is_gt),
                reads=[("gtm", tt)], writes=[("ov5",)])
            p.op("dve", lambda e: e.tensor_mul(isw[:], isw[:], ov5[:]),
                 reads=[("isw",), ("ov5",)], writes=[("isw2",)])
            p.op("dve", lambda e, tt=tt: e.tensor_mul(
                selb[:, tt * 8:(tt + 1) * 8], isw[:], pick[:]),
                reads=[("isw2",), ("pick",)], writes=[("sel", tt)])
        if "dbgD" in io:
            p.op("sp", lambda e, b=b: e.dma_start(
                out=io["dbgD"][b, :, 0:NT * 8], in_=gates_tm[:]),
                reads=[("gtm", tt2) for tt2 in range(NT)],
                writes=[("d_dbgD", b, 0)], dma=True)
            p.op("sp", lambda e, b=b: e.dma_start(
                out=io["dbgD"][b, :, NT * 8:2 * NT * 8], in_=ranks[:]),
                reads=[("rk", tt2, ee2) for tt2 in range(NT) for ee2 in range(8)],
                writes=[("d_dbgD", b, 1)], dma=True)
            p.op("sp", lambda e, b=b: e.dma_start(
                out=io["dbgD"][b, :, 2 * NT * 8:3 * NT * 8], in_=selb[:]),
                reads=[("sel", tt2) for tt2 in range(NT)],
                writes=[("d_dbgD", b, 2)], dma=True)
        ctx.push()
        xmw = ctx.sb("d_xmw", (128, 8 * 1024), BF16)
        accb = ctx.sb("d_acc", (128, 8 * D), BF16)
        wet = [ctx.sb("d_wet", (128, 8 * D), BF16) for _ in range(2)]
        s0tile = ctx.sb("d_ytile", (128, D), BF16)
        for hp in range(2):
            t0h = hp * 8
            for dc in range(8):
                p.op("sp", lambda e, b=b, dc=dc, hp=hp: e.dma_start(
                    out=xmw[:, dc * 1024:(dc + 1) * 1024],
                    in_=io["xn1_fm"][b, dc * 128:(dc + 1) * 128,
                                     hp * 1024:(hp + 1) * 1024]),
                    writes=[("dxm", dc)], dma=True)
            p.op("dve", lambda e: e.memset(accb[:], 0.0),
                 writes=[("acc", i, hf) for i in range(8) for hf in range(2)])
            if hp == 0:
                p.op("pool", lambda e: e.dma_start(
                    out=wet[0][:],
                    in_=io["L1_ffn_experts"][0].rearrange("(c p) n -> p c n", p=128)),
                    writes=[("wet", 0)], dma=True)
            for ee in range(8):
                par = ee % 2
                if ee + 1 < 8:
                    p.op("pool", lambda e, ee=ee: e.dma_start(
                        out=wet[(ee + 1) % 2][:],
                        in_=io["L1_ffn_experts"][ee + 1].rearrange(
                            "(c p) n -> p c n", p=128)),
                        writes=[("wet", (ee + 1) % 2)], dma=True)
                elif hp == 0:
                    p.op("pool", lambda e: e.dma_start(
                        out=wet[0][:],
                        in_=io["L1_ffn_experts"][0].rearrange(
                            "(c p) n -> p c n", p=128)),
                        writes=[("wet", 0)], dma=True)
                for ti in range(8):
                    tt = t0h + ti
                    for half in range(2):
                        bk = (ti % 2) * 2 + half
                        for dc in range(8):
                            p.op("pe", lambda e, ti=ti, dc=dc, half=half, bk=bk,
                                 par=par: e.matmul(
                                     PB[bk][:, 0:512],
                                     xmw[:, dc * 1024 + ti * 128:
                                         dc * 1024 + ti * 128 + 128],
                                     wet[par][:, dc * D + half * 512:
                                              dc * D + (half + 1) * 512],
                                     start=(dc == 0), stop=(dc == 7)),
                                 reads=[("dxm", dc), ("wet", par)],
                                 writes=[("pb", bk)])
                        p.op("dve", lambda e, ti=ti, tt=tt, half=half, bk=bk, ee=ee:
                             e.scalar_tensor_tensor(
                                 accb[:, ti * D + half * 512:
                                      ti * D + (half + 1) * 512],
                                 PB[bk][:, 0:512],
                                 selb[:, tt * 8 + ee: tt * 8 + ee + 1],
                                 accb[:, ti * D + half * 512:
                                      ti * D + (half + 1) * 512],
                                 ALU.mult, ALU.add),
                             reads=[("pb", bk), ("sel", tt), ("acc", ti, half)],
                             writes=[("acc", ti, half)])
            for ti in range(8):
                tt = t0h + ti
                r0 = tt * 128
                p.op("dve", lambda e, ti=ti: e.tensor_copy(
                    s0tile[:], accb[:, ti * D:(ti + 1) * D]),
                    reads=[("acc", ti, 0), ("acc", ti, 1)], writes=[("yt",)])
                p.op("sp", lambda e, b=b, r0=r0: e.dma_start(
                    out=io["y1"][b, r0:r0 + 128, :], in_=s0tile[:]),
                    reads=[("yt",)], writes=[("d_y1", b, r0 // 128)], dma=True)
        p.barrier()
        ctx.pop()
    p.barrier()



def emit_E(ctx, t, io):
    p = ctx.p
    PB, PT = ctx.psum, ctx.psb
    cms = {}
    for r in range(4):
        cms[r] = ctx.sb(f"e_cm{r}", (128, 512), F32)
        hh = ctx.const_in(f"c_causal{r}")
        p.op("sp", lambda e, r=r, hh=hh: e.dma_start(out=cms[r][:], in_=hh),
             writes=[("cm", r)], dma=True)
    q1f = ctx.sb("e_q1", (128, 8 * T), BF16)
    k1f = ctx.sb("e_k1", (128, 8 * T), BF16)
    vtm = ctx.sb("e_vtm", (128, NT * D), BF16)
    ebuf = ctx.sb("e_ebuf", (128, 16 * 512), BF16)
    tmsk = ctx.sb("e_tmsk", (128, 512), F32)
    s1t = ctx.sb("e_s1", (128, D), BF16)
    st = ctx.sb("e_st", (128, 4), F32)
    for b in range(B_PER_CORE):
        for dc in range(8):
            p.op("sp", lambda e, b=b, dc=dc: e.dma_start(
                out=q1f[:, dc * T:(dc + 1) * T],
                in_=io["q1fm"][b, dc * 128:(dc + 1) * 128, :]),
                writes=[("eq", dc)], dma=True)
            p.op("sp", lambda e, b=b, dc=dc: e.dma_start(
                out=k1f[:, dc * T:(dc + 1) * T],
                in_=io["k1fm"][b, dc * 128:(dc + 1) * 128, :]),
                writes=[("ek", dc)], dma=True)
        for tt in range(NT):
            p.op("sp", lambda e, b=b, tt=tt: e.dma_start(
                out=vtm[:, tt * D:(tt + 1) * D],
                in_=io["v1tm"][b, tt * 128:(tt + 1) * 128, :]),
                writes=[("ev", tt)], dma=True)
        for qb in range(4):
            ktmax = 4 * qb + 4
            for kt in range(ktmax):
                for dc in range(8):
                    p.op("pe", lambda e, kt=kt, qb=qb, dc=dc: e.matmul(
                        PB[0][:, 0:512],
                        k1f[:, dc * T + kt * 128: dc * T + kt * 128 + 128],
                        q1f[:, dc * T + qb * 512: dc * T + (qb + 1) * 512],
                        start=(dc == 0), stop=(dc == 7)),
                        reads=[("eq", dc), ("ek", dc)], writes=[("pb", 0)])
                r = kt - 4 * qb
                if 0 <= r:
                    p.op("dve", lambda e, r=r: e.tensor_add(
                        tmsk[:], PB[0][:, 0:512], cms[r][:]),
                        reads=[("pb", 0), ("cm", r)], writes=[("etm",)])
                    p.op("act", lambda e, kt=kt: e.activation(
                        ebuf[:, kt * 512:(kt + 1) * 512], tmsk[:], AF.Exp,
                        scale=1.0 / 32.0),
                        reads=[("etm",)], writes=[("eb", kt)])
                else:
                    p.op("act", lambda e, kt=kt: e.activation(
                        ebuf[:, kt * 512:(kt + 1) * 512], PB[0][:, 0:512], AF.Exp,
                        scale=1.0 / 32.0),
                        reads=[("pb", 0)], writes=[("eb", kt)])
            for qc in range(4):
                qg = qb * 4 + qc
                nkt = qg + 1
                for kt in range(nkt):
                    for half in range(2):
                        p.op("pe", lambda e, kt=kt, qc=qc, half=half, nkt=nkt:
                             e.matmul(
                                 PB[2 + half][:, 0:512],
                                 ebuf[:, kt * 512 + qc * 128:
                                      kt * 512 + qc * 128 + 128],
                                 vtm[:, kt * D + half * 512:
                                     kt * D + (half + 1) * 512],
                                 start=(kt == 0), stop=(kt == nkt - 1)),
                             reads=[("eb", kt), ("ev", kt)],
                             writes=[("pb", 2 + half)])
                    p.op("pe", lambda e, kt=kt, qc=qc, nkt=nkt: e.matmul(
                        PB[4][:, 0:1],
                        ebuf[:, kt * 512 + qc * 128: kt * 512 + qc * 128 + 128],
                        t["ones_c_bf"], start=(kt == 0), stop=(kt == nkt - 1)),
                        reads=[("eb", kt), ("const", "o3")], writes=[("pb", 4)])
                p.op("dve", lambda e: e.reciprocal(st[:, 0:1], PB[4][:, 0:1]),
                     reads=[("pb", 4)], writes=[("erc",)])
                for half in range(2):
                    p.op("dve", lambda e, half=half: e.tensor_scalar(
                        s1t[:, half * 512:(half + 1) * 512],
                        PB[2 + half][:, 0:512], st[:, 0:1], None, ALU.mult),
                        reads=[("pb", 2 + half), ("erc",)],
                        writes=[("es1", half)])
                p.op("sp", lambda e, b=b, qg=qg: e.dma_start(
                    out=io["s1"][b, qg * 128:(qg + 1) * 128, :], in_=s1t[:]),
                    reads=[("es1", 0), ("es1", 1)],
                    writes=[("d_s1", b, qg)], dma=True)
    p.barrier()



def emit_F(ctx, t, io):
    p = ctx.p
    x1t = ctx.sb("f_x1", (128, D), F32)
    s1t = ctx.sb("f_s1", (128, D), BF16)
    y1t = ctx.sb("f_y1", (128, D), BF16)
    sil = ctx.sb("f_sil", (128, D), F32)
    ot = ctx.sb("f_o", (128, D), F32)
    for b in range(B_PER_CORE):
        for tt in range(NT):
            r0 = tt * 128
            p.op("sp", lambda e, b=b, r0=r0: e.dma_start(
                out=x1t[:], in_=io["x1_tm"][b, r0:r0 + 128, :]),
                writes=[("fx1",)], dma=True)
            p.op("sp", lambda e, b=b, r0=r0: e.dma_start(
                out=s1t[:], in_=io["s1"][b, r0:r0 + 128, :]),
                writes=[("fs1",)], dma=True)
            p.op("sp", lambda e, b=b, r0=r0: e.dma_start(
                out=y1t[:], in_=io["y1"][b, r0:r0 + 128, :]),
                writes=[("fy1",)], dma=True)
            p.op("act", lambda e: e.activation(sil[:], y1t[:], AF.Silu),
                 reads=[("fy1",)], writes=[("fsil",)])
            p.op("dve", lambda e: e.tensor_mul(sil[:], sil[:], s1t[:]),
                 reads=[("fsil",), ("fs1",)], writes=[("fsg",)])
            p.op("dve", lambda e: e.tensor_add(ot[:], x1t[:], sil[:]),
                 reads=[("fx1",), ("fsg",)], writes=[("fo",)])
            p.op("sp", lambda e, b=b, r0=r0: e.dma_start(
                out=io["out"][b, r0:r0 + 128, :], in_=ot[:]),
                reads=[("fo",)], writes=[("d_out", b, tt)], dma=True)
    p.barrier()


def build_launch_C():
    from contextlib import ExitStack
    es = ExitStack()
    nc = bass.Bass("TRN2", target_bir_lowering=False)
    io = {}
    io["x"] = nc.dram_tensor("x", [B_PER_CORE, T, D], F32, kind="ExternalInput").ap()
    io["s0"] = nc.dram_tensor("s0", [B_PER_CORE, T, D], BF16, kind="ExternalInput").ap()
    io["g0"] = nc.dram_tensor("g0", [B_PER_CORE, T, D], BF16, kind="ExternalInput").ap()
    for nm, shp in (("L1_attn_Wq", (D, D)), ("L1_attn_Wk", (D, D)),
                    ("L1_attn_Wv", (D, D)), ("L1_ffn_gate_w", (D, 8))):
        io[nm] = nc.dram_tensor(nm, list(shp), F32, kind="ExternalInput").ap()
    io["x1_tm"] = nc.dram_tensor("x1_tm", [B_PER_CORE, T, D], F32,
                                 kind="ExternalOutput").ap()
    io["xn1_fm"] = nc.dram_tensor("xn1_fm", [B_PER_CORE, D, T], BF16,
                                  kind="ExternalOutput").ap()
    io["glog"] = nc.dram_tensor("glog", [B_PER_CORE, T, 8], F32,
                                kind="ExternalOutput").ap()
    io["q1fm"] = nc.dram_tensor("q1fm", [B_PER_CORE, D, T], BF16,
                                kind="ExternalOutput").ap()
    io["k1fm"] = nc.dram_tensor("k1fm", [B_PER_CORE, D, T], BF16,
                                kind="ExternalOutput").ap()
    io["v1tm"] = nc.dram_tensor("v1tm", [B_PER_CORE, T, D], BF16,
                                kind="ExternalOutput").ap()
    ctx = _mk_ctx(nc, es)
    blk = es.enter_context(nc.Block())
    t = setup_common(ctx)
    ctx.push()
    emit_C(ctx, t, io)
    ctx.pop()
    fk = []
    for b in range(B_PER_CORE):
        fk += [("d_x1", b, i) for i in range(NT)]
        fk += [("d_glog", b, i) for i in range(NT)]
        fk += [("d_xn1fm", b, j, g) for j in range(8) for g in range(4)]
        fk += [("d_q1fm", b, j, g) for j in range(8) for g in range(4)]
        fk += [("d_k1fm", b, j, g) for j in range(8) for g in range(4)]
        fk += [("d_v1tm", b, i) for i in range(NT)]
    _finish(ctx, t, fk)
    _emit_all(ctx, blk)
    return nc, es, ctx.const_inputs_used(), ctx


def build_launch_D():
    from contextlib import ExitStack
    es = ExitStack()
    nc = bass.Bass("TRN2", target_bir_lowering=False)
    io = {}
    io["xn1_fm"] = nc.dram_tensor("xn1_fm", [B_PER_CORE, D, T], BF16,
                                  kind="ExternalInput").ap()
    io["glog"] = nc.dram_tensor("glog", [B_PER_CORE, T, 8], F32,
                                kind="ExternalInput").ap()
    io["L1_ffn_experts"] = nc.dram_tensor("L1_ffn_experts", [8, D, D], F32,
                                          kind="ExternalInput").ap()
    io["y1"] = nc.dram_tensor("y1", [B_PER_CORE, T, D], BF16,
                              kind="ExternalOutput").ap()
    import os as _os
    if _os.environ.get("DBG_D"):
        io["dbgD"] = nc.dram_tensor("dbgD", [B_PER_CORE, 128, 3 * NT * 8], F32,
                                    kind="ExternalOutput").ap()
    ctx = _mk_ctx(nc, es)
    blk = es.enter_context(nc.Block())
    t = setup_common(ctx)
    ctx.push()
    emit_D(ctx, t, io)
    ctx.pop()
    fk = []
    for b in range(B_PER_CORE):
        fk += [("d_y1", b, i) for i in range(NT)]
    _finish(ctx, t, fk)
    _emit_all(ctx, blk)
    return nc, es, ctx.const_inputs_used(), ctx


def build_launch_E():
    from contextlib import ExitStack
    es = ExitStack()
    nc = bass.Bass("TRN2", target_bir_lowering=False)
    io = {}
    io["q1fm"] = nc.dram_tensor("q1fm", [B_PER_CORE, D, T], BF16,
                                kind="ExternalInput").ap()
    io["k1fm"] = nc.dram_tensor("k1fm", [B_PER_CORE, D, T], BF16,
                                kind="ExternalInput").ap()
    io["v1tm"] = nc.dram_tensor("v1tm", [B_PER_CORE, T, D], BF16,
                                kind="ExternalInput").ap()
    io["s1"] = nc.dram_tensor("s1", [B_PER_CORE, T, D], BF16,
                              kind="ExternalOutput").ap()
    ctx = _mk_ctx(nc, es)
    blk = es.enter_context(nc.Block())
    t = setup_common(ctx)
    ctx.push()
    emit_E(ctx, t, io)
    ctx.pop()
    fk = []
    for b in range(B_PER_CORE):
        fk += [("d_s1", b, i) for i in range(NT)]
    _finish(ctx, t, fk)
    _emit_all(ctx, blk)
    return nc, es, ctx.const_inputs_used(), ctx


def build_launch_F():
    from contextlib import ExitStack
    es = ExitStack()
    nc = bass.Bass("TRN2", target_bir_lowering=False)
    io = {}
    io["x1_tm"] = nc.dram_tensor("x1_tm", [B_PER_CORE, T, D], F32,
                                 kind="ExternalInput").ap()
    io["s1"] = nc.dram_tensor("s1", [B_PER_CORE, T, D], BF16,
                              kind="ExternalInput").ap()
    io["y1"] = nc.dram_tensor("y1", [B_PER_CORE, T, D], BF16,
                              kind="ExternalInput").ap()
    io["out"] = nc.dram_tensor("out", [B_PER_CORE, T, D], F32,
                               kind="ExternalOutput").ap()
    ctx = _mk_ctx(nc, es)
    blk = es.enter_context(nc.Block())
    t = setup_common(ctx)
    ctx.push()
    emit_F(ctx, t, io)
    ctx.pop()
    fk = []
    for b in range(B_PER_CORE):
        fk += [("d_out", b, i) for i in range(NT)]
    _finish(ctx, t, fk)
    _emit_all(ctx, blk)
    return nc, es, ctx.const_inputs_used(), ctx



_BUILD_CACHE = {}


def _get_build(name):
    if name not in _BUILD_CACHE:
        _BUILD_CACHE[name] = {"A": build_launch_A, "B": build_launch_B,
                              "C": build_launch_C, "D": build_launch_D,
                              "E": build_launch_E, "F": build_launch_F,
                              "FUSED": build_fused}[name]()
    return _BUILD_CACHE[name]


def kernel_unfused(**inputs):
    x = np.ascontiguousarray(np.asarray(inputs["L0_attn_x"], dtype=np.float32))
    cores = list(range(N_CORES))
    xs = [x[c * B_PER_CORE:(c + 1) * B_PER_CORE] for c in cores]

    nc, es, consts, ctx = _get_build("A")
    maps = []
    for c in cores:
        m = prep_inputs_A(inputs, xs[c])
        m.update(consts)
        maps.append(m)
    resA = run_bass_kernel_spmd(nc, maps, core_ids=cores).results
    nc, es, consts, ctx = _get_build("B")
    maps = []
    for c in cores:
        m = prep_inputs_B(inputs, resA[c]["xn0_tm"], resA[c]["xn0_fm"])
        m.update(consts)
        maps.append(m)
    resB = run_bass_kernel_spmd(nc, maps, core_ids=cores).results
    nc, es, consts, ctx = _get_build("C")
    maps = []
    for c in cores:
        m = {"x": xs[c], "s0": resA[c]["s0"], "g0": resB[c]["g0"],
             "L1_attn_Wq": np.ascontiguousarray(inputs["L1_attn_Wq"]),
             "L1_attn_Wk": np.ascontiguousarray(inputs["L1_attn_Wk"]),
             "L1_attn_Wv": np.ascontiguousarray(inputs["L1_attn_Wv"]),
             "L1_ffn_gate_w": np.ascontiguousarray(
                 np.asarray(inputs["L1_ffn_gate_w"]).reshape(D, 8))}
        m.update(consts)
        maps.append(m)
    resC = run_bass_kernel_spmd(nc, maps, core_ids=cores).results
    nc, es, consts, ctx = _get_build("D")
    maps = []
    for c in cores:
        m = {"xn1_fm": resC[c]["xn1_fm"], "glog": resC[c]["glog"],
             "L1_ffn_experts": np.ascontiguousarray(inputs["L1_ffn_experts"])}
        m.update(consts)
        maps.append(m)
    resD = run_bass_kernel_spmd(nc, maps, core_ids=cores).results
    nc, es, consts, ctx = _get_build("E")
    maps = []
    for c in cores:
        m = {"q1fm": resC[c]["q1fm"], "k1fm": resC[c]["k1fm"],
             "v1tm": resC[c]["v1tm"]}
        m.update(consts)
        maps.append(m)
    resE = run_bass_kernel_spmd(nc, maps, core_ids=cores).results
    nc, es, consts, ctx = _get_build("F")
    maps = []
    for c in cores:
        m = {"x1_tm": resC[c]["x1_tm"], "s1": resE[c]["s1"], "y1": resD[c]["y1"]}
        m.update(consts)
        maps.append(m)
    resF = run_bass_kernel_spmd(nc, maps, core_ids=cores).results
    out = np.concatenate([resF[c]["out"] for c in cores], axis=0)
    return out.astype(np.float32)



def build_fused():
    from contextlib import ExitStack
    es = ExitStack()
    nc = bass.Bass("TRN2", target_bir_lowering=False)
    io = {}
    io["x"] = nc.dram_tensor("x", [B_PER_CORE, T, D], F32, kind="ExternalInput").ap()
    for nm, shp in W_SHAPES_A.items():
        io[nm] = nc.dram_tensor(nm, list(shp), F32, kind="ExternalInput").ap()
    for nm, shp in W_SHAPES_B.items():
        io[nm] = nc.dram_tensor(nm, list(shp), F32, kind="ExternalInput").ap()
    for nm, shp in (("L1_attn_Wq", (D, D)), ("L1_attn_Wk", (D, D)),
                    ("L1_attn_Wv", (D, D)), ("L1_ffn_gate_w", (D, 8)),
                    ("L1_ffn_experts", (8, D, D))):
        io[nm] = nc.dram_tensor(nm, list(shp), F32, kind="ExternalInput").ap()
    io["out"] = nc.dram_tensor("out", [B_PER_CORE, T, D], F32,
                               kind="ExternalOutput").ap()
    scratch = {
        "xn0_tm": ([B_PER_CORE, T, D], F32), "xn0_fm": ([B_PER_CORE, D, T], BF16),
        "h_fm": ([B_PER_CORE, D, T], BF16), "o_st": ([B_PER_CORE, T, VD], BF16),
        "xmid": ([B_PER_CORE, T, D], F32), "h2fm": ([B_PER_CORE, D, T], BF16),
        "s0": ([B_PER_CORE, T, D], BF16), "g0": ([B_PER_CORE, T, D], BF16),
        "xmid0": ([B_PER_CORE, T, D], F32), "xm0fm": ([B_PER_CORE, D, T], BF16),
        "x1_tm": ([B_PER_CORE, T, D], F32), "xn1_fm": ([B_PER_CORE, D, T], BF16),
        "glog": ([B_PER_CORE, T, 8], F32), "q1fm": ([B_PER_CORE, D, T], BF16),
        "k1fm": ([B_PER_CORE, D, T], BF16), "v1tm": ([B_PER_CORE, T, D], BF16),
        "y1": ([B_PER_CORE, T, D], BF16), "s1": ([B_PER_CORE, T, D], BF16),
    }
    for nm, (shp, dt) in scratch.items():
        io[nm] = nc.dram_tensor(nm, shp, dt, kind="Internal").ap()
    ctx = _mk_ctx(nc, es)
    blk = es.enter_context(nc.Block())
    t = setup_common(ctx)
    for fn in (emit_A0, emit_A1, emit_A2, emit_A3, emit_B, emit_C, emit_D,
               emit_E, emit_F):
        ctx.push()
        fn(ctx, t, io)
        ctx.pop()
    fk = []
    for b in range(B_PER_CORE):
        fk += [("d_out", b, i) for i in range(NT)]
    _finish(ctx, t, fk)
    _emit_all(ctx, blk)
    return nc, es, ctx.const_inputs_used(), ctx


def prep_inputs_fused(inputs, xs):
    m = {"x": np.ascontiguousarray(xs)}
    for nm, shp in list(W_SHAPES_A.items()) + list(W_SHAPES_B.items()):
        m[nm] = np.ascontiguousarray(np.asarray(inputs[nm],
                                                dtype=np.float32)).reshape(shp)
    for nm, shp in (("L1_attn_Wq", (D, D)), ("L1_attn_Wk", (D, D)),
                    ("L1_attn_Wv", (D, D)), ("L1_ffn_gate_w", (D, 8)),
                    ("L1_ffn_experts", (8, D, D))):
        m[nm] = np.ascontiguousarray(np.asarray(inputs[nm],
                                                dtype=np.float32)).reshape(shp)
    return m


def kernel_fused(**inputs):
    x = np.ascontiguousarray(np.asarray(inputs["L0_attn_x"], dtype=np.float32))
    cores = list(range(N_CORES))
    nc, es, consts, ctx = _get_build("FUSED")
    maps = []
    for c in cores:
        m = prep_inputs_fused(inputs, x[c * B_PER_CORE:(c + 1) * B_PER_CORE])
        m.update(consts)
        maps.append(m)
    res = run_bass_kernel_spmd(nc, maps, core_ids=cores).results
    return np.concatenate([res[c]["out"] for c in cores], axis=0).astype(np.float32)


def kernel(**inputs):
    return kernel_fused(**inputs)
